# version 13
# baseline (speedup 1.0000x reference)
import numpy as np
import ml_dtypes
from contextlib import ExitStack
import concourse.bass as bass
import concourse.mybir as mybir
from concourse.bass_utils import run_bass_kernel_spmd

F32 = mybir.dt.float32
BF16 = mybir.dt.bfloat16
AX = mybir.AxisListType.X
ALU = mybir.AluOpType
AF = mybir.ActivationFunctionType

NCORES = 8
NSEQ = 2
S = 2048
D = 1024
NT = S // 128
TT = NSEQ * NT
DFF = 4096

ENG2BLOCK = {"pe": "tensor", "act": "scalar", "dve": "vector",
             "pool": "gpsimd", "sp": "sync"}


class Sched:

    def __init__(self, nc):
        self.nc = nc
        self.ops = []
        semnames = list(ENG2BLOCK) + ["poolc"]
        self.cnt = {e: 0 for e in semnames}
        self.es = ExitStack()
        self.sems = {e: self.es.enter_context(nc.semaphore(f"sem_{e}"))
                     for e in semnames}
        self.dstreams = {}
        self.scopes = [self.es]
        self.prefix = ""
        self.base_after = ()

    def push_scope(self, prefix):
        self.scopes.append(ExitStack())
        self.prefix = prefix
        self.base_after = tuple((k, v) for k, v in self.cnt.items() if v > 0)

    def pop_scope(self):
        self.scopes.pop().close()
        self.prefix = ""
        self.base_after = ()

    def sb(self, name, shape, dtype=F32):
        return self.scopes[-1].enter_context(
            self.nc.sbuf_tensor(self.prefix + name, shape, dtype))

    def ps(self, name, shape, dtype=F32):
        return self.scopes[-1].enter_context(
            self.nc.psum_tensor(self.prefix + name, shape, dtype))

    def op(self, eng, fn, after=(), inc=True):
        step = 16 if inc == "dma" else (1 if inc else 0)
        sem = eng
        if eng == "pool" and step == 1:
            sem = "poolc"
        aft = tuple(a for a in after if a is not None) + self.base_after
        self.ops.append((eng, sem, aft, fn, step))
        if step:
            self.cnt[sem] += step
            return (sem, self.cnt[sem])
        return None

    def dma(self, queue, out, in_, after=(), stream=None, **kw):
        stream = "dma_" + (stream if stream is not None else queue)
        if stream not in self.dstreams:
            self.sems[stream] = self.es.enter_context(self.nc.semaphore(stream))
            self.cnt[stream] = 0
            self.dstreams[stream] = True
        prev = (stream, self.cnt[stream]) if self.cnt[stream] else None
        aft = tuple(after) + ((prev,) if prev else ()) + self.base_after
        self.cnt[stream] += 16
        tok = (stream, self.cnt[stream])
        self.ops.append((queue, stream,
                         tuple(a for a in aft if a is not None),
                         lambda e: e.dma_start(out=out, in_=in_, **kw), 16))
        return tok

    def barrier(self, eng, after):
        self.op(eng, lambda e: None, after=after, inc=False)

    def emit(self):
        per_eng = {e: [] for e in ENG2BLOCK}
        for (eng, sem, aft, fn, step) in self.ops:
            per_eng[eng].append((sem, aft, fn, step))
        sems = self.sems

        def make_body(eng):
            def body(proxy):
                waited = {}
                for (sem, aft, fn, step) in per_eng[eng]:
                    for (weng, wval) in aft:
                        if waited.get(weng, 0) >= wval:
                            continue
                        proxy.wait_ge(sems[weng], wval)
                        waited[weng] = wval
                    ins = fn(proxy)
                    if step:
                        assert ins is not None
                        ins.then_inc(sems[sem], step)
            return body

        with self.nc.Block() as block:
            for eng, bname in ENG2BLOCK.items():
                if per_eng[eng]:
                    getattr(block, bname)(make_body(eng))
        self.es.close()



def emit_transpose(s, dst_ap, src_ap, idb, pst, after=(), pst_free=None,
                   copier="act"):
    p, f = src_ap.shape[0], src_ap.shape[1]
    t1 = s.op("pe", lambda e: e.transpose(pst[0:f, 0:p], src_ap, idb[0:p, 0:p]),
              after=tuple(after) + ((pst_free,) if pst_free else ()))
    t2 = s.op(copier, lambda e: (e.activation(dst_ap, pst[0:f, 0:p], AF.Copy)
                                 if copier == "act"
                                 else e.tensor_copy(dst_ap, pst[0:f, 0:p])),
              after=(t1,))
    return t1, t2


class LNHelper:

    def __init__(self, s, tag=""):
        self.s = s
        self.sets = []
        for i in range(2):
            dummy = s.sb(f"ln_dummy{tag}{i}", [128, D], BF16)
            t = {k: s.sb(f"ln_{k}{tag}{i}", [128, 1], F32)
                 for k in ["ssq", "sm", "mean", "rstd", "mb", "eps"]}
            t_eps = s.op("pool", lambda e, t=t: e.memset(t["eps"][:], 1e-5))
            self.sets.append({"dummy": dummy, "t": t, "eps": t_eps, "prev": None})
        self.i = 0

    def emit(self, res_ap, out_ap, after):
        s = self.s
        st = self.sets[self.i % 2]
        self.i += 1
        t = st["t"]
        dummy = st["dummy"]
        aft = tuple(after) + ((st["prev"],) if st["prev"] else ()) + (st["eps"],)
        t_sq = s.op("act", lambda e: e.activation(dummy[:], res_ap, AF.Square,
                                                  accum_out=t["ssq"][:]), after=aft)
        t_sm = s.op("dve", lambda e: e.reduce_sum(t["sm"][:], res_ap, axis=AX),
                    after=aft)
        t_mean = s.op("dve", lambda e: e.tensor_scalar(t["mean"][:], t["sm"][:],
                                                       1.0 / D, None, op0=ALU.mult),
                      after=(t_sm,))
        t_v1 = s.op("dve", lambda e: e.tensor_scalar(t["rstd"][:], t["ssq"][:],
                                                     1.0 / D, None, op0=ALU.mult),
                    after=(t_sq,))
        t_v2 = s.op("dve", lambda e: e.scalar_tensor_tensor(t["mb"][:], t["mean"][:],
                                                            -1.0, t["mean"][:],
                                                            op0=ALU.mult, op1=ALU.mult),
                    after=(t_mean,))
        t_v3 = s.op("dve", lambda e: e.tensor_tensor(t["rstd"][:], t["rstd"][:],
                                                     t["mb"][:], op=ALU.add),
                    after=(t_v1, t_v2))
        t_l = s.op("act", lambda e: e.activation(t["rstd"][:], t["rstd"][:], AF.Ln,
                                                 bias=t["eps"][:]),
                   after=(t_v3,))
        t_e = s.op("act", lambda e: e.activation(t["rstd"][:], t["rstd"][:], AF.Exp,
                                                 scale=-0.5), after=(t_l,))
        t_mb = s.op("dve", lambda e: e.scalar_tensor_tensor(t["mb"][:], t["mean"][:],
                                                            -1.0, t["rstd"][:],
                                                            op0=ALU.mult, op1=ALU.mult),
                    after=(t_e,))
        t_out = s.op("act", lambda e: e.activation(out_ap, res_ap, AF.Identity,
                                                   bias=t["mb"][:], scale=t["rstd"][:]),
                     after=(t_mb,))
        st["prev"] = t_out
        return t_out



def emit_L1f(nc, s, Y2, W1T, B1C, W2T, B2C, IDENT, ONESC, ONESR, OUT,
             WQH=None):
    w1q = s.sb("w1q", [128, 8 * DFF], BF16)
    w2q = s.sb("w2q", [128, 32 * D], BF16)
    wld = s.sb("wld", [128, 1024], F32)
    m2b = s.sb("m2b", [128, 1024], BF16)
    y2bf = [s.sb(f"y2bf{i}", [128, D], BF16) for i in range(2)]
    y2T = s.sb("y2T", [128, 8 * 512], BF16)
    hT = s.sb("hT", [128, 32 * 512], BF16)
    outT = s.sb("outT", [128, 8 * 512], BF16)
    res = [s.sb(f"res{i}", [128, D], F32) for i in range(2)]
    idb = s.sb("idb", [128, 128], BF16)
    idf = s.sb("idf", [128, 128], F32)
    onescol = s.sb("onescol", [128, 1], F32)
    onesrow = s.sb("onesrow", [1, 128], F32)
    b1r = s.sb("b1r", [32, 128], F32)
    b2r = s.sb("b2r", [8, 128], F32)
    b1c = s.sb("b1c", [128, 32], F32)
    b2c = s.sb("b2c", [128, 8], F32)
    sums = s.sb("sums", [128, 32], F32)
    thr = s.sb("thr", [128, 1], F32)
    nthr = s.sb("nthr", [128, 1], F32)
    tot = s.sb("tot", [128, 1], F32)
    t11 = s.sb("t11", [1, 1], F32)
    ln = LNHelper(s)

    pA = [s.ps(f"pA{i}", [128, 512], F32) for i in range(4)]
    pT = s.ps("pT", [128, 128], BF16)
    pTb = s.ps("pTb", [128, 128], BF16)
    pQ = s.ps("pQ", [128, 128], F32)
    p11 = s.ps("p11", [1, 512], F32)

    t_id = s.dma("pool", idb[:], IDENT)
    t_idf = s.dma("sp", idf[:], IDENT)
    t_ones = s.dma("sp", onescol[:], ONESC)
    t_onesr = s.dma("sp", onesrow[:], ONESR)
    t_b1r = s.dma("sp", b1r[:], B1C.rearrange("(c p) -> c p", p=128))
    t_b2r = s.dma("sp", b2r[:], B2C.rearrange("(c p) -> c p", p=128))

    tb1 = s.op("pe", lambda e: e.transpose(pQ[:, 0:32], b1r[:], idf[0:32, 0:32]),
               after=(t_b1r, t_idf))
    tb1c = s.op("dve", lambda e: e.tensor_copy(b1c[:], pQ[:, 0:32]), after=(tb1,))
    tb2 = s.op("pe", lambda e: e.transpose(pQ[:, 32:40], b2r[:], idf[0:8, 0:8]),
               after=(t_b2r, t_idf))
    tb2c = s.op("dve", lambda e: e.tensor_copy(b2c[:], pQ[:, 32:40]), after=(tb2,))

    def quantize(WT, nchunk, qdst, qdst_cols, nelem, entry):
        npc = WT.shape[1] // 1024
        half = 1024
        toks = []
        t_prev = entry
        for c in range(nchunk):
            for h in range(npc):
                tl = s.dma("sp", wld[:, 0:half],
                           WT[c * 128:(c + 1) * 128, h * half:(h + 1) * half],
                           after=(t_prev,), stream="w")
                col = (c * npc + h) % 32
                prevw = toks[col - 32] if len(toks) >= 32 else None
                tr = s.op("dve", lambda e, col=col: e.tensor_scalar(
                    sums[:, col:col + 1], wld[:, 0:half], 1.0, None, op0=ALU.mult)
                    if False else e.reduce_sum(
                    sums[:, col:col + 1], wld[:, 0:half], axis=AX,
                    apply_absolute_value=True), after=(tl,))
                toks.append(tr)
                t_prev = tr
        assert nchunk * npc <= 32, "sums col reuse would need accumulation"
        ncols = nchunk * npc
        t_f = s.op("dve", lambda e: e.reduce_sum(tot[:], sums[:, 0:ncols], axis=AX),
                   after=tuple(toks))
        t_t = s.op("pe", lambda e: e.matmul(p11[0:1, 0:1], tot[:], onescol[:],
                                            start=True, stop=True),
                   after=(t_f, t_ones))
        t_g = s.op("dve", lambda e: e.tensor_scalar(t11[:], p11[0:1, 0:1],
                                                    0.5 / nelem, 0.5e-5,
                                                    op0=ALU.mult, op1=ALU.add),
                   after=(t_t,))
        t_bc = s.op("pe", lambda e: e.matmul(pQ[:, 0:1], onesrow[:], t11[:],
                                             start=True, stop=True),
                    after=(t_g, t_onesr, tb1c, tb2c))
        t_thr = s.op("dve", lambda e: e.tensor_copy(thr[:], pQ[:, 0:1]),
                     after=(t_bc,))
        t_nthr = s.op("dve", lambda e: e.tensor_scalar(nthr[:], thr[:], -1.0, None,
                                                       op0=ALU.mult), after=(t_thr,))
        t_prev = t_nthr
        out_toks = []
        for c in range(nchunk):
            for h in range(npc):
                tl = s.dma("sp", wld[:, 0:half],
                           WT[c * 128:(c + 1) * 128, h * half:(h + 1) * half],
                           after=(t_prev,), stream="w")
                dst = qdst[:, c * qdst_cols + h * half: c * qdst_cols + (h + 1) * half]
                t1 = s.op("dve", lambda e, dst=dst: e.tensor_scalar(
                    dst, wld[:, 0:half], thr[:], None, op0=ALU.is_gt), after=(tl,))
                t2 = s.op("dve", lambda e: e.tensor_scalar(
                    m2b[:, 0:half], wld[:, 0:half], nthr[:], None, op0=ALU.is_lt),
                    after=(tl, t1))
                t3 = s.op("dve", lambda e, dst=dst: e.tensor_tensor(
                    dst, dst, m2b[:, 0:half], op=ALU.subtract), after=(t2,))
                out_toks.append(t3)
                t_prev = t3
        return out_toks[-1]

    if WQH is not None:
        W1QH, W2QH = WQH
        t_q1 = s.dma("pool", w1q[:].rearrange("p (c f) -> p c f", f=DFF),
                     W1QH.rearrange("(c p) f -> p c f", p=128), stream="w")
        t_q2 = s.dma("pool", w2q[:].rearrange("p (c f) -> p c f", f=D),
                     W2QH.rearrange("(c p) f -> p c f", p=128), stream="wb")
    else:
        t_q1 = quantize(W1T, 8, w1q, DFF, D * DFF, None)
        t_q2 = quantize(W2T, 32, w2q, D, D * DFF, t_q1)

    buf_tok = {}
    pA_tok = [None] * 4
    pT_toks = [None, None]
    t_out_dma = None
    for g in range(TT // 4):
        trtoks = []
        for ti in range(4):
            tile = g * 4 + ti
            bname = f"y2bf{ti % 2}"
            tl = s.dma("pool", y2bf[ti % 2][:],
                       Y2[tile * 128:(tile + 1) * 128, :],
                       after=(buf_tok.get(bname), t_id), stream="xa")
            last_pe = None
            for c in range(8):
                src = y2bf[ti % 2][:, c * 128:(c + 1) * 128]
                dst = y2T[:, c * 512 + ti * 128: c * 512 + ti * 128 + 128]
                sl = c % 2
                pe_t, cp_t = emit_transpose(s, dst, src, idb,
                                            pT if sl == 0 else pTb,
                                            after=(tl,), pst_free=pT_toks[sl])
                pT_toks[sl] = cp_t
                last_pe = pe_t
                trtoks.append(cp_t)
            buf_tok[bname] = last_pe
        h_toks = []
        for fc in range(32):
            pb = pA[fc % 4]
            mm_last = None
            for dc in range(8):
                lhs = w1q[:, dc * DFF + fc * 128: dc * DFF + (fc + 1) * 128]
                rhs = y2T[:, dc * 512:(dc + 1) * 512]
                aft = ((pA_tok[fc % 4], t_q1) + tuple(trtoks)) if dc == 0 else ()
                mm_last = s.op("pe", lambda e, pb=pb, lhs=lhs, rhs=rhs, dc=dc:
                               e.matmul(pb[:], lhs, rhs, start=(dc == 0),
                                        stop=(dc == 7)),
                               after=aft, inc=(dc == 7))
            t_ev = s.op("act", lambda e, pb=pb, fc=fc: e.activation(
                hT[:, fc * 512:(fc + 1) * 512], pb[:], AF.Relu,
                bias=b1c[:, fc:fc + 1]), after=(mm_last, tb1c))
            pA_tok[fc % 4] = t_ev
            h_toks.append(t_ev)
        o_toks = []
        for oc in range(8):
            pb = pA[oc % 4]
            mm_last = None
            for fc in range(32):
                lhs = w2q[:, fc * D + oc * 128: fc * D + (oc + 1) * 128]
                rhs = hT[:, fc * 512:(fc + 1) * 512]
                aft = ((pA_tok[oc % 4], t_q2) + tuple(h_toks)) if fc == 0 else ()
                mm_last = s.op("pe", lambda e, pb=pb, lhs=lhs, rhs=rhs, fc=fc:
                               e.matmul(pb[:], lhs, rhs, start=(fc == 0),
                                        stop=(fc == 31)),
                               after=aft, inc=(fc == 31))
            t_ev = s.op("act", lambda e, pb=pb, oc=oc: e.activation(
                outT[:, oc * 512:(oc + 1) * 512], pb[:], AF.Identity,
                bias=b2c[:, oc:oc + 1]), after=(mm_last, tb2c))
            pA_tok[oc % 4] = t_ev
            o_toks.append(t_ev)
        for ti in range(4):
            tile = g * 4 + ti
            rname = f"res{ti % 2}"
            tlf = s.dma("sp", res[ti % 2][:], Y2[tile * 128:(tile + 1) * 128, :],
                        after=(buf_tok.get(rname),), stream="xb")
            add_toks = []
            for oc in range(8):
                src = outT[:, oc * 512 + ti * 128: oc * 512 + ti * 128 + 128]
                sl = oc % 2
                psl = pT if sl == 0 else pTb
                t1 = s.op("pe", lambda e, src=src, psl=psl: e.transpose(
                    psl[:], src, idb[:]),
                          after=tuple(o_toks) + (pT_toks[sl], tlf))
                t2 = s.op("dve", lambda e, oc=oc, b=res[ti % 2], psl=psl:
                          e.tensor_tensor(
                    b[:, oc * 128:(oc + 1) * 128], psl[:],
                    b[:, oc * 128:(oc + 1) * 128], op=ALU.add),
                    after=(t1, tlf))
                pT_toks[sl] = t2
                add_toks.append(t2)
            t_ln = ln.emit(res[ti % 2][:], res[ti % 2][:], tuple(add_toks))
            t_out_dma = s.dma("sp", OUT[tile * 128:(tile + 1) * 128, :],
                              res[ti % 2][:], after=(t_ln,), stream="out")
            buf_tok[rname] = t_out_dma
    s.barrier("sp", (t_out_dma,))


def build_L1f():
    nc = bass.Bass("TRN2", target_bir_lowering=False)
    NTOK = TT * 128
    Y2 = nc.dram_tensor("Y2", [NTOK, D], F32, kind="ExternalInput").ap()
    W1T = nc.dram_tensor("W1T", [D, DFF], F32, kind="ExternalInput").ap()
    B1C = nc.dram_tensor("B1C", [DFF], F32, kind="ExternalInput").ap()
    W2T = nc.dram_tensor("W2T", [DFF, D], F32, kind="ExternalInput").ap()
    B2C = nc.dram_tensor("B2C", [D], F32, kind="ExternalInput").ap()
    IDENT = nc.dram_tensor("IDENT", [128, 128], F32, kind="ExternalInput").ap()
    ONESC = nc.dram_tensor("ONESC", [128, 1], F32, kind="ExternalInput").ap()
    ONESR = nc.dram_tensor("ONESR", [1, 128], F32, kind="ExternalInput").ap()
    OUT = nc.dram_tensor("OUT", [NTOK, D], F32, kind="ExternalOutput").ap()

    s = Sched(nc)
    emit_L1f(nc, s, Y2, W1T, B1C, W2T, B2C, IDENT, ONESC, ONESR, OUT)
    s.emit()
    return nc


def prep_L1f(core_y2, W1, b1, W2, b2):
    return dict(
        Y2=np.ascontiguousarray(core_y2.reshape(-1, D).astype(np.float32)),
        W1T=np.ascontiguousarray(W1.T.astype(np.float32)),
        B1C=np.ascontiguousarray(b1.astype(np.float32)),
        W2T=np.ascontiguousarray(W2.T.astype(np.float32)),
        B2C=np.ascontiguousarray(b2.astype(np.float32)),
        IDENT=np.eye(128, dtype=np.float32),
        ONESC=np.ones((128, 1), dtype=np.float32),
        ONESR=np.ones((1, 128), dtype=np.float32),
    )


def _gather(res, key):
    return [res.results[c][key] for c in range(NCORES)]


def kernel(**inputs):
    return kernel_fused(**inputs)


def kernel_multilaunch(**inputs):
    x = np.asarray(inputs["L0_attn_x"], dtype=np.float32)
    cores = list(range(NCORES))
    xs = [x[2 * c:2 * c + 2] for c in cores]

    nc = build_L0a()
    ims = [prep_L0a(xs[c], inputs["L0_attn_W_ih"], inputs["L0_attn_W_hh"],
                    inputs["L0_attn_b_ih"], inputs["L0_attn_b_hh"],
                    inputs["L0_attn_W_mu"], inputs["L0_attn_b_mu"],
                    inputs["L0_attn_W_sig"], inputs["L0_attn_b_sig"])
           for c in cores]
    r = run_bass_kernel_spmd(nc, ims, core_ids=cores)
    y0 = _gather(r, "Y0")

    nc = build_L0f()
    ims = [prep_L0f(y0[c], inputs["L0_ffn_gate_W"], inputs["L0_ffn_gate_b"],
                    inputs["L0_ffn_expert_W"], inputs["L0_ffn_expert_b"])
           for c in cores]
    r = run_bass_kernel_spmd(nc, ims, core_ids=cores)
    y1 = _gather(r, "OUT")

    nc = build_L1a()
    ims = [prep_L1a(y1[c], inputs["L1_attn_Wqkv"], inputs["L1_attn_Wout"],
                    inputs["L1_attn_a"], inputs["L1_attn_b"],
                    inputs["L1_attn_dt"]) for c in cores]
    r = run_bass_kernel_spmd(nc, ims, core_ids=cores)
    y2 = _gather(r, "Y2")

    nc = build_L1f()
    ims = [prep_L1f(y2[c], inputs["L1_ffn_W1"], inputs["L1_ffn_b1"],
                    inputs["L1_ffn_W2"], inputs["L1_ffn_b2"]) for c in cores]
    r = run_bass_kernel_spmd(nc, ims, core_ids=cores)
    out = _gather(r, "OUT")

    return np.concatenate([o.reshape(NSEQ, S, D) for o in out], axis=0)



def emit_L0f(nc, s, Y0, GATEW, GATEB, EXPW, EXPB, IDENT, IDX8, IDXB8, OUT, debug=False):
    NTOK = TT * 128
    NE = 8
    if debug:
        DBG_P = nc.dram_tensor("DBG_P", [NTOK, 8], F32, kind="ExternalOutput").ap()
        DBG_C = nc.dram_tensor("DBG_C", [NTOK, 8], F32, kind="ExternalOutput").ap()
        DBG_A = nc.dram_tensor("DBG_A", [NTOK, D], F32, kind="ExternalOutput").ap()
        DBG_I = nc.dram_tensor("DBG_I", [NTOK, D], F32, kind="ExternalOutput").ap()
    y0T = s.sb("y0T", [128, TT * 8 * 128], BF16)
    y0f = s.sb("y0f", [128, 8 * D], F32)
    acc = s.sb("acc", [128, 8 * D], F32)
    expw = [s.sb(f"expw{i}", [128, 8 * D], BF16) for i in range(2)]
    y0bf = [s.sb(f"y0bf{i}", [128, D], BF16) for i in range(2)]
    gatew = s.sb("gatew", [128, 8 * NE], BF16)
    gateb = s.sb("gateb", [128, NE], F32)
    expb = s.sb("expb", [NE, D], BF16)
    idb = s.sb("idb", [128, 128], BF16)
    idx8 = s.sb("idx8", [128, NE], F32)
    idxb8 = s.sb("idxb8", [128, NE], F32)
    comb = s.sb("comb", [128, TT * NE], F32)
    combbf = s.sb("combbf", [128, NE], BF16)
    combT = s.sb("combT", [NE, TT * 128], BF16)
    gw = {k: s.sb(f"g_{k}", [128, NE], F32)
          for k in ["p", "oh1", "imk", "pw", "eq2", "oh2", "wcl"]}
    gc = {k: s.sb(f"gc_{k}", [128, 1], F32)
          for k in ["m1", "i1", "m2", "i2", "nmx", "rs", "tot", "rt"]}
    ln = LNHelper(s)

    pG = s.ps("pG", [128, NE], F32)
    pT = s.ps("pT", [128, 128], BF16)
    pTc = s.ps("pTc", [128, 128], BF16)
    pE = [s.ps(f"pE{i}", [128, 512], F32) for i in range(4)]

    t_id = s.dma("pool", idb[:], IDENT)
    t_gw = s.dma("pool", gatew[:].rearrange("p (c e) -> p c e", e=NE), GATEW.rearrange("(c p) e -> p c e", p=128))
    t_gb = s.dma("sp", gateb[:], GATEB)
    t_eb = s.dma("pool", expb[:], EXPB)
    t_ix = s.dma("sp", idx8[:], IDX8)
    t_ib = s.dma("sp", idxb8[:], IDXB8)

    buf_tok = {}
    pE_tok = [None] * 4
    pT_tok = None
    pT_toks2 = [None, None]
    pG_tok = None
    t_out = None
    NG = TT // 8
    t_wdma = {}
    t_wdma[0] = s.dma("pool", expw[0][:].rearrange("p (c f) -> p c f", f=D),
                      EXPW[0].rearrange("(c p) f -> p c f", p=128), stream="w")
    gall = {}
    grp_cp = {}
    for tile in range(TT):
        bname = f"y0bf{tile % 2}"
        tl = s.dma("pool", y0bf[tile % 2][:],
                   Y0[tile * 128:(tile + 1) * 128, :],
                   after=(buf_tok.get(bname), t_id), stream="xa")
        last_pe = None
        trt8 = []
        for c in range(8):
            src_ = y0bf[tile % 2][:, c * 128:(c + 1) * 128]
            dst = y0T[:, (tile * 8 + c) * 128:(tile * 8 + c + 1) * 128]
            pe_t, cp_t = emit_transpose(s, dst, src_, idb,
                                        pT if c % 2 == 0 else pTc,
                                        after=(tl,),
                                        pst_free=pT_toks2[c % 2])
            pT_toks2[c % 2] = cp_t
            last_pe = pe_t
            trt8.append(cp_t)
        buf_tok[bname] = last_pe
        grp_cp[tile // 8] = trt8[-1]
        mm_last = None
        for c in range(8):
            aft = (pG_tok, t_gw) + tuple(trt8) if c == 0 else ()
            mm_last = s.op("pe", lambda e, tile=tile, c=c: e.matmul(
                pG[:], y0T[:, (tile * 8 + c) * 128:(tile * 8 + c + 1) * 128],
                gatew[:, c * NE:(c + 1) * NE],
                start=(c == 0), stop=(c == 7)), after=aft, inc=(c == 7))
        p = gw["p"]

        def dv(fn, after):
            return s.op("dve", fn, after=after)
        t1 = dv(lambda e: e.tensor_tensor(p[:], pG[:], gateb[:], op=ALU.add),
                (mm_last, t_gb))
        t2 = dv(lambda e: e.tensor_scalar(p[:], p[:], 1e-4, None, op0=ALU.max),
                (t1,))
        pG_tok = t1
        t3 = dv(lambda e: e.reduce_max(gc["nmx"][:], p[:], axis=AX, negate=True),
                (t2,))
        t4 = s.op("act", lambda e: e.activation(p[:], p[:], AF.Exp,
                                                bias=gc["nmx"][:],
                                                accum_out=gc["rs"][:]),
                  after=(t3,))
        t5 = dv(lambda e: e.reciprocal(gc["rt"][:], gc["rs"][:]), (t4,))
        t6 = dv(lambda e: e.tensor_scalar(p[:], p[:], gc["rt"][:], None,
                                          op0=ALU.mult), (t5,))
        t7 = dv(lambda e: e.reduce_max(gc["m1"][:], p[:], axis=AX), (t6,))
        t8 = dv(lambda e: e.tensor_scalar(gw["oh1"][:], p[:], gc["m1"][:],
                                          None, op0=ALU.is_equal), (t7,))
        t9 = dv(lambda e: e.scalar_tensor_tensor(gw["imk"][:], gw["oh1"][:],
                                                 -1e3, idxb8[:],
                                                 op0=ALU.mult, op1=ALU.add),
                (t8, t_ib))
        t10 = dv(lambda e: e.tensor_reduce(gc["i1"][:], gw["imk"][:], axis=AX,
                                           op=ALU.min), (t9,))
        t11 = dv(lambda e: e.tensor_scalar(gw["oh1"][:], idx8[:], gc["i1"][:],
                                           None, op0=ALU.is_equal), (t10, t_ix))
        t12 = dv(lambda e: e.scalar_tensor_tensor(gw["pw"][:], gw["oh1"][:],
                                                  -1e3, p[:],
                                                  op0=ALU.mult, op1=ALU.add),
                 (t11,))
        t13 = dv(lambda e: e.reduce_max(gc["m2"][:], gw["pw"][:], axis=AX),
                 (t12,))
        t14 = dv(lambda e: e.tensor_scalar(gw["eq2"][:], gw["pw"][:],
                                           gc["m2"][:], None,
                                           op0=ALU.is_equal), (t13,))
        t15 = dv(lambda e: e.scalar_tensor_tensor(gw["imk"][:], gw["eq2"][:],
                                                  -1e3, idxb8[:],
                                                  op0=ALU.mult, op1=ALU.add),
                 (t14,))
        t16 = dv(lambda e: e.tensor_reduce(gc["i2"][:], gw["imk"][:], axis=AX,
                                           op=ALU.min), (t15,))
        t17 = dv(lambda e: e.tensor_scalar(gw["oh2"][:], idx8[:], gc["i2"][:],
                                           None, op0=ALU.is_equal), (t16,))
        t18 = dv(lambda e: e.tensor_tensor(gw["oh1"][:], gw["oh1"][:],
                                           gw["oh2"][:], op=ALU.add), (t17,))
        t19 = dv(lambda e: e.tensor_scalar(gw["wcl"][:], p[:], 0.1, None,
                                           op0=ALU.max), (t6,))
        cslice = comb[:, tile * NE:(tile + 1) * NE]
        t20 = dv(lambda e, cs=cslice: e.tensor_tensor(cs, gw["oh1"][:],
                                                      gw["wcl"][:],
                                                      op=ALU.mult),
                 (t18, t19))
        t21 = dv(lambda e, cs=cslice: e.reduce_sum(gc["tot"][:], cs, axis=AX),
                 (t20,))
        t22 = dv(lambda e: e.reciprocal(gc["rt"][:], gc["tot"][:]), (t21,))
        t23 = dv(lambda e: e.tensor_scalar(gc["rt"][:], gc["rt"][:], 0.5,
                                           None, op0=ALU.mult), (t22,))
        t24 = dv(lambda e, cs=cslice: e.tensor_scalar(cs, cs, gc["rt"][:],
                                                      None, op0=ALU.mult),
                 (t23,))
        t25 = dv(lambda e, cs=cslice: e.tensor_copy(combbf[:], cs),
                 (t24, buf_tok.get("combbf")))
        tr = s.op("pe", lambda e: e.transpose(pT[0:NE, 0:128], combbf[:],
                                              idb[:]),
                  after=(t25, pT_tok, pT_toks2[0]))
        buf_tok["combbf"] = tr
        t26 = s.op("act", lambda e, tile=tile: e.activation(
            combT[:, tile * 128:(tile + 1) * 128], pT[0:NE, 0:128], AF.Copy),
            after=(tr,))
        pT_tok = t26
        pT_toks2[0] = t26
        gall[tile] = (t24, t26)
    for g in range(NG):
        y0f_toks = []
        for ti in range(8):
            tile = g * 8 + ti
            tlf = s.dma("sp", y0f[:, ti * D:(ti + 1) * D],
                        Y0[tile * 128:(tile + 1) * 128, :],
                        after=(buf_tok.get("y0f_%d" % ti),), stream="xb")
            y0f_toks.append(tlf)
        init_toks = []
        for ti in range(8):
            (t24, t26) = gall[g * 8 + ti]
            tlf = y0f_toks[ti]
            for hhalf in range(2):
                pb = pE[hhalf]
                mm = s.op("pe", lambda e, g=g, ti=ti, hh=hhalf, pb=pb: e.matmul(
                    pb[:], combT[:, (g * 8 + ti) * 128:(g * 8 + ti + 1) * 128],
                    expb[:, hh * 512:(hh + 1) * 512],
                    start=True, stop=True),
                    after=(t26, t_eb, pE_tok[hhalf]), inc=True)
                t_in = s.op("dve", lambda e, ti=ti, hh=hhalf, pb=pb:
                            e.tensor_tensor(acc[:, ti * D + hh * 512:
                                                ti * D + (hh + 1) * 512],
                                            pb[:],
                                            y0f[:, ti * D + hh * 512:
                                                ti * D + (hh + 1) * 512],
                                            op=ALU.add),
                            after=(mm, tlf, buf_tok.get("acc_%d" % ti)))
                pE_tok[hhalf] = t_in
                init_toks.append(t_in)
        for e_i in range(NE):
            t_w = t_wdma[g * NE + e_i]
            mm_final = None
            for ti in range(8):
                evs = []
                for hh in range(2):
                    pb = pE[2 + hh]
                    mm_last = None
                    for c in range(8):
                        aft = ((pE_tok[2 + hh], t_w, grp_cp[g])
                               if c == 0 else ())
                        mm_last = s.op("pe", lambda e, g=g, ti=ti, c=c, hh=hh,
                                       pb=pb, w=expw[e_i % 2]:
                                       e.matmul(pb[:],
                                                y0T[:, ((g * 8 + ti) * 8 + c) * 128:
                                                    ((g * 8 + ti) * 8 + c + 1) * 128],
                                                w[:, c * D + hh * 512:c * D + (hh + 1) * 512],
                                                start=(c == 0), stop=(c == 7)),
                                       after=aft, inc=(c == 7))
                    t_ev = s.op("dve", lambda e, g=g, ti=ti, hh=hh, pb=pb, e_i=e_i:
                                e.scalar_tensor_tensor(
                                    acc[:, ti * D + hh * 512:ti * D + (hh + 1) * 512],
                                    pb[:],
                                    comb[:, (g * 8 + ti) * NE + e_i:
                                         (g * 8 + ti) * NE + e_i + 1],
                                    acc[:, ti * D + hh * 512:ti * D + (hh + 1) * 512],
                                    op0=ALU.mult, op1=ALU.add),
                                after=(mm_last, gall[g * 8 + ti][0]) +
                                      tuple(init_toks[ti * 2:ti * 2 + 2]))
                    pE_tok[2 + hh] = t_ev
                    evs.append(t_ev)
                    mm_final = mm_last
                if e_i == NE - 1:
                    buf_tok[f"done_{ti}"] = tuple(evs)
            buf_tok[f"expw{e_i % 2}"] = mm_final
            nxt = g * NE + e_i + 1
            if nxt < NG * NE:
                t_wdma[nxt] = s.dma("pool",
                                    expw[nxt % 2][:].rearrange("p (c f) -> p c f", f=D),
                                    EXPW[nxt % NE].rearrange("(c p) f -> p c f", p=128),
                                    after=(buf_tok.get(f"expw{nxt % 2}"),),
                                    stream="w")
        for ti in range(8):
            tile = g * 8 + ti
            evs = buf_tok[f"done_{ti}"]
            t_ln = ln.emit(acc[:, ti * D:(ti + 1) * D], acc[:, ti * D:(ti + 1) * D],
                           tuple(evs))
            t_out = s.dma("sp", OUT[tile * 128:(tile + 1) * 128, :],
                          acc[:, ti * D:(ti + 1) * D], after=(t_ln,), stream="out")
            buf_tok["acc_%d" % ti] = t_out
            buf_tok["y0f_%d" % ti] = init_toks[ti * 2 + 1]
    s.barrier("sp", (t_out,))


def build_L0f(debug=False):
    nc = bass.Bass("TRN2", target_bir_lowering=False)
    NTOK = TT * 128
    NE = 8
    Y0 = nc.dram_tensor("Y0", [NTOK, D], F32, kind="ExternalInput").ap()
    GATEW = nc.dram_tensor("GATEW", [D, NE], F32, kind="ExternalInput").ap()
    GATEB = nc.dram_tensor("GATEB", [128, NE], F32, kind="ExternalInput").ap()
    EXPW = nc.dram_tensor("EXPW", [NE, D, D], F32, kind="ExternalInput").ap()
    EXPB = nc.dram_tensor("EXPB", [NE, D], F32, kind="ExternalInput").ap()
    IDENT = nc.dram_tensor("IDENT", [128, 128], F32, kind="ExternalInput").ap()
    IDX8 = nc.dram_tensor("IDX8", [128, NE], F32, kind="ExternalInput").ap()
    IDXB8 = nc.dram_tensor("IDXB8", [128, NE], F32, kind="ExternalInput").ap()
    OUT = nc.dram_tensor("OUT", [NTOK, D], F32, kind="ExternalOutput").ap()

    s = Sched(nc)
    emit_L0f(nc, s, Y0, GATEW, GATEB, EXPW, EXPB, IDENT, IDX8, IDXB8, OUT,
             debug=debug)
    s.emit()
    return nc


def prep_L0f(core_y0, gate_W, gate_b, expert_W, expert_b):
    return dict(
        Y0=np.ascontiguousarray(core_y0.reshape(-1, D).astype(np.float32)),
        GATEW=np.ascontiguousarray(gate_W.astype(np.float32)),
        GATEB=np.ascontiguousarray(np.broadcast_to(gate_b.astype(np.float32), (128, 8))),
        EXPW=np.ascontiguousarray(expert_W.astype(np.float32)),
        EXPB=np.ascontiguousarray(expert_b.astype(np.float32)),
        IDENT=np.eye(128, dtype=np.float32),
        IDX8=np.broadcast_to(np.arange(8, dtype=np.float32), (128, 8)).copy(),
        IDXB8=(np.broadcast_to(np.arange(8, dtype=np.float32), (128, 8)) + 1e3).copy(),
    )



def emit_L1a(nc, s, Y1, WQKVT, WOUTT, ABD, NEGTRI, NEGTRIL, IDENT, ONESR,
             SEL16, RALLH, Y2):
    NH = 16
    HD = 64
    y1bf = [s.sb(f"y1bf{i}", [128, D], BF16) for i in range(2)]
    y1T = s.sb("y1T", [128, 8 * S], BF16)
    wqkvT = s.sb("wqkvT", [128, 8 * 3 * D], BF16)
    woutT = s.sb("woutT", [128, 8 * D], BF16)
    qT2 = s.sb("qT2", [128, S], BF16)
    kT2 = s.sb("kT2", [128, S], BF16)
    p_bf = s.sb("p_bf", [128, S], BF16)
    v_sb = s.sb("v_sb", [128, NT * 130], BF16)
    ptb = s.sb("ptb", [128, 16 * 256], BF16)
    msml = s.sb("msml", [1, 4], F32)
    b11 = s.sb("b11", [1, 1], F32)
    negBt = [s.sb(f"negBt{i}", [128, 1], F32) for i in range(2)]
    negtril = s.sb("negtril", [128, 128], F32)
    negfull = s.sb("negfull", [128, 128], F32)
    onescolb = s.sb("onescolb", [128, 1], BF16)
    ones11c = s.sb("ones11c", [1, 1], F32)
    attnT = s.sb("attnT", [128, 8 * S], BF16)
    foutT = s.sb("foutT", [128, 8 * 512], BF16)
    qsq = foutT
    res = [s.sb(f"res{i}", [128, D], F32) for i in range(2)]
    Rsb2 = s.sb("Rsb2", [65, S], F32)
    Rall = s.sb("Rall", [128, 256], F32)
    rs_row0 = s.sb("rsrow0", [1, S], F32)
    rs_row_h = [rs_row0, rs_row0]
    nrow = rs_row0
    strow = s.sb("strow", [1, 128], F32)
    st16 = s.sb("st16", [1, 16], F32)
    st16c = s.sb("st16c", [16, 1], F32)
    gmax = s.sb("gmax", [128, 1], F32)
    sel16 = s.sb("sel16", [16, 128], F32)
    idb = s.sb("idb", [128, 128], BF16)
    idf = s.sb("idf", [128, 128], F32)
    onesr = s.sb("onesr", [1, 128], F32)
    negtri = s.sb("negtri", [128, 128], F32)
    abd = s.sb("abd", [1, 3], F32)
    gcols = {k: s.sb(f"ga_{k}", [128, 1], F32)
             for k in ["a", "b", "dt", "dtau", "den", "stim", "five"]}
    ga = {k: s.sb(f"g16_{k}", [128, 256], F32)
          for k in ["rcpR", "E", "t1", "t2", "I", "v1", "w1", "rs"]}
    t11 = s.sb("t11a", [1, 1], F32)
    ln = LNHelper(s)

    pS = s.ps("pS", [128, 1024], F32)
    pST = [s.ps(f"pST{i}", [128, 256], F32) for i in range(2)]
    pT = s.ps("pT", [128, 128], BF16)
    pT2 = s.ps("pT2", [128, 128], BF16)
    pV = s.ps("pV", [65, 256], F32)
    pB = s.ps("pB", [128, 512], F32)

    t_id = s.dma("pool", idb[:], IDENT)
    t_idf = s.dma("sp", idf[:], IDENT)
    t_onesr = s.dma("sp", onesr[:], ONESR)
    t_tri = s.dma("sp", negtri[:], NEGTRI)
    t_tril = s.dma("sp", negtril[:], NEGTRIL)
    t_sel = s.dma("sp", sel16[:], SEL16)
    t_mfull = s.op("pool", lambda e: e.memset(negfull[:], -1e30))
    t_ocb = s.op("pool", lambda e: e.memset(onescolb[:], 1.0))
    t_o11c = s.op("pool", lambda e: e.memset(ones11c[:], 1.0))
    t_abd = s.dma("sp", abd[:], ABD)
    t_five = s.op("pool", lambda e: e.memset(gcols["five"][:], 5.0))
    vones = []
    for sc in range(NT):
        vones.append(s.op("pool", lambda e, sc=sc: e.memset(
            v_sb[:, sc * 130 + 64:sc * 130 + 65], 1.0)))
        vones.append(s.op("pool", lambda e, sc=sc: e.memset(
            v_sb[:, sc * 130 + 129:sc * 130 + 130], 1.0)))
    t_vones = tuple(vones)
    t_wq = s.dma("pool", wqkvT[:].rearrange("p (c f) -> p c f", f=3 * D),
                 WQKVT.rearrange("(c p) f -> p c f", p=128))
    t_wo = s.dma("pool", woutT[:].rearrange("p (c f) -> p c f", f=D),
                 WOUTT.rearrange("(c p) f -> p c f", p=128))

    col_toks = {}
    for i, k in enumerate(["a", "b", "dt"]):
        tmm = s.op("pe", lambda e, i=i: e.matmul(pB[0:128, 0:1],
                                                 onesr[:], abd[0:1, i:i + 1],
                                                 start=True, stop=True),
                   after=(t_onesr, t_abd, col_toks.get("prev")))
        tcp = s.op("dve", lambda e, k=k: e.tensor_copy(gcols[k][:], pB[0:128, 0:1]),
                   after=(tmm,))
        col_toks["prev"] = tcp
        col_toks[k] = tcp
    t_dtau = s.op("dve", lambda e: e.tensor_scalar(gcols["dtau"][:], gcols["dt"][:],
                                                   1.0 / 12.5, None, op0=ALU.mult),
                  after=(col_toks["dt"],))
    t_d1 = s.op("dve", lambda e: e.tensor_tensor(gcols["den"][:], gcols["dtau"][:],
                                                 gcols["b"][:], op=ALU.mult),
                after=(t_dtau, col_toks["b"]))
    t_d2 = s.op("dve", lambda e: e.tensor_scalar(gcols["den"][:], gcols["den"][:],
                                                 1.0, None, op0=ALU.add),
                after=(t_d1,))
    t_den = s.op("dve", lambda e: e.reciprocal(gcols["den"][:], gcols["den"][:]),
                 after=(t_d2,))

    buf = {}
    pT_tok = None
    pV_tok = None
    pB_tok = col_toks["prev"]
    t_out = None
    for sq in range(NSEQ):
        trtoks = []
        for ti in range(NT):
            tile = sq * NT + ti
            bname = f"y1bf{ti % 2}"
            tl = s.dma("pool", y1bf[ti % 2][:],
                       Y1[tile * 128:(tile + 1) * 128, :],
                       after=(buf.get(bname), t_id), stream="xa")
            last_pe = None
            for c in range(8):
                src = y1bf[ti % 2][:, c * 128:(c + 1) * 128]
                dst = y1T[:, c * S + ti * 128: c * S + ti * 128 + 128]
                sl = c % 2
                pe_t, cp_t = emit_transpose(s, dst, src, idb,
                                            pT if sl == 0 else pT2,
                                            after=(tl, buf.get("y1T_cons")),
                                            pst_free=buf.get(f"vTw{sl}"))
                buf[f"vTw{sl}"] = cp_t
                last_pe = pe_t
                trtoks.append(cp_t)
            buf[bname] = last_pe
        attn_w = {}
        for hp in range(8):
            qkv_last_mm = None
            for (name, off) in [("q", 128 * hp), ("k", D + 128 * hp),
                                ("v", 2 * D + 128 * hp)]:
                dstT = {"q": qT2, "k": kT2, "v": p_bf}[name]
                warkey = {"q": "q2_cons", "k": "k2_cons",
                          "v": "p_bf_cons"}[name]
                qscale = 0.125 if name == "q" else 1.0
                for n4 in range(4):
                    half = n4 % 2
                    mm_last = None
                    for c in range(8):
                        aft = ()
                        if c == 0:
                            aft = (buf.get(f"pSh{half}"), t_wq,
                                   buf.get(warkey)) + tuple(trtoks)
                        mm_last = s.op("pe", lambda e, c=c, half=half, off=off,
                                       n4=n4: e.matmul(
                                       pS[0:128, half * 512:(half + 1) * 512],
                                       wqkvT[:, c * 3 * D + off:
                                             c * 3 * D + off + 128],
                                       y1T[:, c * S + n4 * 512:
                                           c * S + (n4 + 1) * 512],
                                       start=(c == 0), stop=(c == 7)),
                                       after=aft, inc=(c == 7))
                    qkv_last_mm = mm_last
                    cp = s.op("act", lambda e, d=dstT, n4=n4, half=half,
                              qs=qscale: e.activation(
                                  d[:, n4 * 512:(n4 + 1) * 512],
                                  pS[0:128, half * 512:(half + 1) * 512],
                                  AF.Copy, scale=qs),
                              after=(mm_last, buf.get(f"{name}2_w")))
                    buf[f"pSh{half}"] = cp
                    buf.setdefault(f"{name}2_done", [])
                    if n4 == 0:
                        buf[f"{name}2_done"] = []
                    buf[f"{name}2_done"].append(cp)
                buf[f"{name}2_w"] = buf[f"{name}2_done"][-1]
            buf["y1T_cons"] = qkv_last_mm
            q_done = tuple(buf["q2_done"])
            k_done = tuple(buf["k2_done"])
            v_done_cp = tuple(buf["v2_done"])
            vtoks = []
            for sc in range(NT):
                slot = sc % 2
                psl = pT if slot == 0 else pT2
                pe_t = s.op("pe", lambda e, sc=sc, psl=psl: e.transpose(
                    psl[:], p_bf[:, sc * 128:(sc + 1) * 128], idb[:]),
                    after=(v_done_cp[sc // 4], buf.get(f"vTw{slot}"),
                           buf.get("v_sb_cons")))
                c1 = s.op("act", lambda e, sc=sc, psl=psl: e.activation(
                    v_sb[:, sc * 130:sc * 130 + 64], psl[:, 0:64], AF.Copy),
                    after=(pe_t,) + (t_vones if sc == 0 else ()))
                c2 = s.op("act", lambda e, sc=sc, psl=psl: e.activation(
                    v_sb[:, sc * 130 + 65:sc * 130 + 129], psl[:, 64:128],
                    AF.Copy), after=(c1,))
                buf[f"vTw{slot}"] = c2
                vtoks.append(c2)
            v_ready = tuple(vtoks)
            buf["p_bf_cons"] = vtoks[-1]
            mx_toks = []
            for (mat, base) in [(qT2, 0), (kT2, 2)]:
                t_sq = s.op("dve", lambda e, mat=mat: e.tensor_tensor(
                    qsq[:, 0:S], mat[:], mat[:], op=ALU.mult),
                    after=(q_done if base == 0 else k_done) +
                          (buf.get("qsq_cons"),))
                for idx in range(2):
                    hh = 64 * idx
                    cps = []
                    for j in range(4):
                        t_mm = s.op("pe", lambda e, hh=hh, j=j: e.matmul(
                            pB[0:1, 0:512], onescolb[hh:hh + 64, 0:1],
                            qsq[hh:hh + 64, j * 512:(j + 1) * 512],
                            start=True, stop=True),
                            after=(t_sq, t_ocb, pB_tok))
                        t_cp = s.op("act", lambda e, j=j: e.activation(
                            nrow[0:1, j * 512:(j + 1) * 512], pB[0:1, 0:512],
                            AF.Copy), after=(t_mm, buf.get("nrow")))
                        pB_tok = t_cp
                        cps.append(t_cp)
                    t_mx = s.op("dve", lambda e, col=base + idx: e.reduce_max(
                        msml[0:1, col:col + 1], nrow[0:1, 0:S], axis=AX),
                        after=tuple(cps) + (buf.get("msml"),))
                    buf["nrow"] = t_mx
                    mx_toks.append(t_mx)
                    if base == 0 and idx == 1:
                        buf["qsq_cons"] = t_mx
            buf["msml"] = mx_toks[-1]
            buf["qsq_cons"] = mx_toks[-1]
            ng_toks = []
            for idx in range(2):
                t_b2 = s.op("dve", lambda e, idx=idx: e.tensor_tensor(
                    b11[:], msml[0:1, idx:idx + 1], msml[0:1, 2 + idx:3 + idx],
                    op=ALU.mult), after=(mx_toks[idx], mx_toks[2 + idx],
                                         buf.get("b11")))
                buf["msml"] = t_b2
                t_ln = s.op("act", lambda e: e.activation(b11[:], b11[:], AF.Ln),
                            after=(t_b2,))
                t_ex = s.op("act", lambda e: e.activation(b11[:], b11[:], AF.Exp,
                                                          scale=0.5),
                            after=(t_ln,))
                t_bc = s.op("pe", lambda e: e.matmul(pB[0:128, 0:1], onesr[:],
                                                     b11[:], start=True,
                                                     stop=True),
                            after=(t_ex, pB_tok))
                t_ng = s.op("dve", lambda e, idx=idx: e.tensor_scalar(
                    negBt[idx][:], pB[0:128, 0:1], -1.05, None, op0=ALU.mult),
                    after=(t_bc, buf.get(f"negB{idx}")))
                pB_tok = t_ng
                buf["b11"] = t_ng
                ng_toks.append(t_ng)
            last_att_mm = None
            for idx in range(2):
                hx = 2 * hp + idx
                hh = 64 * idx
                att_toks = []
                for qp in range(8):
                    nch = 2 * (qp + 1)
                    exp_toks = []
                    for c in range(nch):
                        slot = c % 2
                        aft = (buf.get(f"pSTw{slot}"),)
                        if c == 0 and qp == 0:
                            aft = aft + q_done + k_done
                        mmst = s.op("pe", lambda e, c=c, qp=qp, slot=slot,
                                    hh=hh: e.matmul(
                            pST[slot][:], kT2[hh:hh + 64, c * 128:(c + 1) * 128],
                            qT2[hh:hh + 64, qp * 256:(qp + 1) * 256],
                            start=True, stop=True), after=aft)
                        last_att_mm = mmst
                        t_f = mmst
                        for h2 in range(2):
                            qt = 2 * qp + h2
                            if c == qt:
                                t_f = s.op("dve", lambda e, slot=slot, h2=h2,
                                           t_f=t_f: e.tensor_tensor(
                                    pST[slot][:, h2 * 128:(h2 + 1) * 128],
                                    pST[slot][:, h2 * 128:(h2 + 1) * 128],
                                    negtril[:], op=ALU.add),
                                    after=(t_f, t_tril))
                            elif c > qt:
                                t_f = s.op("dve", lambda e, slot=slot, h2=h2,
                                           t_f=t_f: e.tensor_tensor(
                                    pST[slot][:, h2 * 128:(h2 + 1) * 128],
                                    pST[slot][:, h2 * 128:(h2 + 1) * 128],
                                    negfull[:], op=ALU.add),
                                    after=(t_f, t_mfull))
                        t_e = s.op("act", lambda e, c=c, slot=slot, idx=idx:
                                   e.activation(ptb[:, c * 256:(c + 1) * 256],
                                                pST[slot][:], AF.Exp,
                                                bias=negBt[idx][:]),
                                   after=(t_f, ng_toks[idx], buf.get("ptb")))
                        buf[f"pSTw{slot}"] = t_e
                        exp_toks.append(t_e)
                    mm_pv = None
                    for c in range(nch):
                        aft = (exp_toks[c],)
                        if c == 0:
                            aft = aft + (pV_tok,) + v_ready
                        mm_pv = s.op("pe", lambda e, c=c, nch=nch, idx=idx:
                                     e.matmul(
                            pV[:], v_sb[:, c * 130 + 65 * idx:
                                        c * 130 + 65 * idx + 65],
                            ptb[:, c * 256:(c + 1) * 256],
                            start=(c == 0), stop=(c == nch - 1)),
                            after=aft, inc=(c == nch - 1))
                        last_att_mm = mm_pv
                    buf["ptb"] = mm_pv
                    cp1 = s.op("dve", lambda e, idx=idx, hp=hp, qp=qp:
                               e.tensor_copy(
                        attnT[64 * idx:64 * idx + 64,
                              S * hp + qp * 256:S * hp + (qp + 1) * 256],
                        pV[0:64, 0:256]),
                        after=(mm_pv, buf.get("attnT_war")))
                    cp2 = s.op("dve", lambda e, qp=qp: e.tensor_copy(
                        Rsb2[64:65, qp * 256:(qp + 1) * 256], pV[64:65, 0:256]),
                        after=(mm_pv, buf.get("rsb")))
                    pV_tok = cp2
                    att_toks.append(cp1)
                buf[f"negB{idx}"] = exp_toks[-1]
                rdma = s.dma("sp", RALLH[hx * S:(hx + 1) * S].rearrange(
                    "(a b) -> a b", a=1), Rsb2[64:65, 0:S],
                    after=(pV_tok,) + tuple(att_toks), stream="rb")
                buf["rsb"] = rdma
                attn_w[hx] = tuple(att_toks)
            buf["q2_cons"] = last_att_mm
            buf["k2_cons"] = last_att_mm
            buf["v_sb_cons"] = last_att_mm
        t_r1 = s.dma("sp", Rall[:], RALLH.rearrange("(p c) -> p c", p=128),
                     after=(buf.get("gate_done"),), stream="rb")

        def dv(fn, after=()):
            return s.op("dve", fn, after=after)
        g0 = dv(lambda e: e.reciprocal(ga["rcpR"][:], Rall[:]), (t_r1,))
        g1 = dv(lambda e: e.tensor_tensor(ga["E"][:], Rall[:], ga["rcpR"][:],
                                          op=ALU.mult), (g0,))
        m1 = dv(lambda e: e.reduce_max(gmax[:], ga["E"][:], axis=AX,
                                       apply_absolute_value=True), (g1,))
        m2 = s.op("pe", lambda e: e.matmul(pB[0:1, 0:128], gmax[:], idf[:],
                                           start=True, stop=True),
                  after=(m1, t_idf, pB_tok))
        m3 = s.op("act", lambda e: e.activation(strow[:], pB[0:1, 0:128],
                                                AF.Copy),
                  after=(m2, buf.get("strow")))
        pB_tok = m3
        buf["strow"] = m3
        h_toks = []
        for h in range(16):
            t_rh = dv(lambda e, h=h: e.tensor_reduce(
                st16[0:1, h:h + 1], strow[0:1, h * 8:(h + 1) * 8], axis=AX,
                op=ALU.max), (m3, buf.get("st16")))
            h_toks.append(t_rh)
        buf["st16"] = h_toks[-1]
        m4 = s.op("pe", lambda e: e.matmul(pB[0:16, 0:1], st16[0:1, 0:16],
                                           ones11c[:], start=True, stop=True),
                  after=tuple(h_toks) + (t_o11c, pB_tok))
        m5 = dv(lambda e: e.tensor_copy(st16c[:], pB[0:16, 0:1]),
                (m4, buf.get("st16c")))
        buf["st16c"] = m5
        m6 = s.op("pe", lambda e: e.matmul(pB[0:128, 1:2], sel16[:], st16c[:],
                                           start=True, stop=True),
                  after=(m5, t_sel, pB_tok))
        m7 = dv(lambda e: e.tensor_scalar(gcols["stim"][:], pB[0:128, 1:2],
                                          1e-6, None, op0=ALU.max),
                (m6, buf.get("stimc")))
        pB_tok = m7
        m8 = dv(lambda e: e.reciprocal(gcols["stim"][:], gcols["stim"][:]), (m7,))
        buf["stimc"] = m8
        g9 = s.op("act", lambda e: e.activation(ga["t1"][:], ga["E"][:], AF.Abs),
                  after=(g1,))
        g10 = s.op("act", lambda e: e.activation(ga["t2"][:], ga["t1"][:],
                                                 AF.Exp, scale=-10.0,
                                                 bias=gcols["five"][:]),
                   after=(g9, t_five))
        g11 = dv(lambda e: e.tensor_scalar(ga["t1"][:], ga["t2"][:], 1.0, None,
                                           op0=ALU.add), (g10,))
        g13 = dv(lambda e: e.reciprocal(ga["t2"][:], ga["t1"][:]), (g11,))
        g14 = dv(lambda e: e.tensor_scalar(ga["I"][:], ga["t2"][:], 0.9, 0.1,
                                           op0=ALU.mult, op1=ALU.add), (g13,))
        g15a = dv(lambda e: e.tensor_tensor(ga["I"][:], ga["I"][:], ga["E"][:],
                                            op=ALU.mult), (g14, g1))
        g15 = dv(lambda e: e.tensor_scalar(ga["I"][:], ga["I"][:],
                                           gcols["stim"][:], None,
                                           op0=ALU.mult), (g15a, m8))
        g16 = dv(lambda e: e.tensor_scalar(ga["v1"][:], ga["I"][:],
                                           gcols["dt"][:], None, op0=ALU.mult),
                 (g15, col_toks["dt"]))
        g17 = dv(lambda e: e.tensor_scalar(ga["w1"][:], ga["v1"][:],
                                           gcols["a"][:], None, op0=ALU.add),
                 (g16, col_toks["a"]))
        g18 = dv(lambda e: e.tensor_scalar(ga["w1"][:], ga["w1"][:],
                                           gcols["dtau"][:], None, op0=ALU.mult),
                 (g17, t_dtau))
        g19 = dv(lambda e: e.tensor_scalar(ga["w1"][:], ga["w1"][:],
                                           gcols["den"][:], None, op0=ALU.mult),
                 (g18, t_den))
        g20 = dv(lambda e: e.tensor_tensor(ga["t1"][:], ga["v1"][:], ga["v1"][:],
                                           op=ALU.mult), (g16,))
        g21 = dv(lambda e: e.tensor_tensor(ga["t1"][:], ga["t1"][:], ga["v1"][:],
                                           op=ALU.mult), (g20,))
        g22 = dv(lambda e: e.tensor_scalar(ga["t1"][:], ga["t1"][:], 1.0 / 3.0,
                                           None, op0=ALU.mult), (g21,))
        g23 = dv(lambda e: e.tensor_tensor(ga["t2"][:], ga["v1"][:], ga["t1"][:],
                                           op=ALU.subtract), (g22,))
        g24 = dv(lambda e: e.tensor_tensor(ga["t2"][:], ga["t2"][:], ga["w1"][:],
                                           op=ALU.subtract), (g23, g19))
        g25 = dv(lambda e: e.tensor_tensor(ga["t2"][:], ga["t2"][:], ga["I"][:],
                                           op=ALU.add), (g24,))
        g26 = dv(lambda e: e.scalar_tensor_tensor(ga["t2"][:], ga["t2"][:],
                                                  gcols["dt"][:], ga["v1"][:],
                                                  op0=ALU.mult, op1=ALU.add),
                 (g25,))
        g27 = s.op("act", lambda e: e.activation(ga["t1"][:], ga["t2"][:],
                                                 AF.Exp, scale=-1.0), (g26,))
        g28 = dv(lambda e: e.tensor_scalar(ga["t1"][:], ga["t1"][:], 1.0, None,
                                           op0=ALU.add), (g27,))
        g29 = dv(lambda e: e.reciprocal(ga["t2"][:], ga["t1"][:]), (g28,))
        g30 = dv(lambda e: e.tensor_scalar(ga["t2"][:], ga["t2"][:], 0.5, 0.5,
                                           op0=ALU.mult, op1=ALU.add), (g29,))
        g31 = dv(lambda e: e.tensor_tensor(ga["t1"][:], ga["t2"][:], ga["E"][:],
                                           op=ALU.mult), (g30,))
        g32 = dv(lambda e: e.tensor_scalar(ga["t1"][:], ga["t1"][:], 1e-8, None,
                                           op0=ALU.add), (g31,))
        g32b = dv(lambda e: e.reciprocal(ga["t1"][:], ga["t1"][:]), (g32,))
        g33 = dv(lambda e: e.tensor_tensor(ga["rs"][:], ga["t2"][:], ga["t1"][:],
                                           op=ALU.mult), (g32b,))
        g34 = dv(lambda e: e.tensor_tensor(ga["rs"][:], ga["rs"][:],
                                           ga["rcpR"][:], op=ALU.mult), (g33, g0))
        t_w2 = s.dma("sp", RALLH.rearrange("(p c) -> p c", p=128), ga["rs"][:],
                     after=(g34,), stream="rb")
        sc_toks = []
        row_war = [None, None]
        for hx in range(NH):
            slot = 0
            t_rr = s.dma("sp", rs_row_h[slot][:],
                         RALLH[hx * S:(hx + 1) * S].rearrange("(a b) -> a b",
                                                              a=1),
                         after=(row_war[slot],), stream="rb")
            for c4 in range(4):
                t_b = s.op("pe", lambda e, hx=hx, c4=c4, slot=slot: e.matmul(
                    pB[0:64, 0:512], onesr[0:1, 0:64],
                    rs_row_h[slot][0:1, c4 * 512:(c4 + 1) * 512],
                    start=True, stop=True),
                    after=(t_rr, pB_tok, t_onesr))
                t_s = s.op("dve", lambda e, hx=hx, c4=c4: e.tensor_tensor(
                    attnT[64 * (hx % 2):64 * (hx % 2) + 64,
                          S * (hx // 2) + c4 * 512:S * (hx // 2) + (c4 + 1) * 512],
                    attnT[64 * (hx % 2):64 * (hx % 2) + 64,
                          S * (hx // 2) + c4 * 512:S * (hx // 2) + (c4 + 1) * 512],
                    pB[0:64, 0:512], op=ALU.mult),
                    after=(t_b,) + attn_w[hx])
                pB_tok = t_s
                row_war[slot] = t_b
                sc_toks.append(t_s)
        buf["gate_done"] = sc_toks[-1]
        for w4 in range(4):
            otoks = []
            for oc in range(8):
                mm_last = None
                for c in range(8):
                    aft = ((tuple(sc_toks) + (buf.get(f"pSh{oc % 2}"), t_wo))
                           if c == 0 else ())
                    mm_last = s.op("pe", lambda e, oc=oc, c=c, w4=w4: e.matmul(
                        pS[:, oc % 2 * 512:(oc % 2 + 1) * 512],
                        woutT[:, c * D + oc * 128:c * D + (oc + 1) * 128],
                        attnT[:, c * S + w4 * 512:c * S + (w4 + 1) * 512],
                        start=(c == 0), stop=(c == 7)),
                        after=aft, inc=(c == 7))
                t_ev = s.op("act", lambda e, oc=oc: e.activation(
                    foutT[:, oc * 512:(oc + 1) * 512],
                    pS[:, oc % 2 * 512:(oc % 2 + 1) * 512], AF.Copy),
                    after=(mm_last, buf.get(f"foutT{oc}")))
                buf[f"pSh{oc % 2}"] = t_ev
                otoks.append(t_ev)
                buf[f"foutT{oc}"] = t_ev
            for ti4 in range(4):
                tile = sq * NT + w4 * 4 + ti4
                rname = f"res{ti4 % 2}"
                tlf = s.dma("sp", res[ti4 % 2][:],
                            Y1[tile * 128:(tile + 1) * 128, :],
                            after=(buf.get(rname),), stream="xb")
                add_toks = []
                for oc in range(8):
                    src = foutT[:, oc * 512 + ti4 * 128: oc * 512 + ti4 * 128 + 128]
                    sl = oc % 2
                    psl = pT if sl == 0 else pT2
                    t1 = s.op("pe", lambda e, src=src, psl=psl: e.transpose(
                        psl[:], src, idb[:]),
                              after=tuple(otoks) + (buf.get(f"vTw{sl}"), tlf))
                    t2 = s.op("dve", lambda e, oc=oc, b=res[ti4 % 2], psl=psl:
                              e.tensor_tensor(b[:, oc * 128:(oc + 1) * 128],
                                              psl[:],
                                              b[:, oc * 128:(oc + 1) * 128],
                                              op=ALU.add), after=(t1, tlf))
                    buf[f"vTw{sl}"] = t2
                    add_toks.append(t2)
                t_ln = ln.emit(res[ti4 % 2][:], res[ti4 % 2][:], tuple(add_toks))
                t_out = s.dma("sp", Y2[tile * 128:(tile + 1) * 128, :],
                              res[ti4 % 2][:], after=(t_ln,), stream="out")
                buf[rname] = t_out
            buf["attnT_war"] = mm_last
    s.barrier("sp", (t_out,))



def build_L1a():
    nc = bass.Bass("TRN2", target_bir_lowering=False)
    NTOK = TT * 128
    NH = 16
    HD = 64
    Y1 = nc.dram_tensor("Y1", [NTOK, D], F32, kind="ExternalInput").ap()
    WQKVT = nc.dram_tensor("WQKVT", [D, 3 * D], F32, kind="ExternalInput").ap()
    WOUTT = nc.dram_tensor("WOUTT", [D, D], F32, kind="ExternalInput").ap()
    ABD = nc.dram_tensor("ABD", [1, 3], F32, kind="ExternalInput").ap()
    NEGTRI = nc.dram_tensor("NEGTRI", [128, 128], F32, kind="ExternalInput").ap()
    NEGTRIL = nc.dram_tensor("NEGTRIL", [128, 128], F32, kind="ExternalInput").ap()
    IDENT = nc.dram_tensor("IDENT", [128, 128], F32, kind="ExternalInput").ap()
    ONESR = nc.dram_tensor("ONESR", [1, 128], F32, kind="ExternalInput").ap()
    SEL16 = nc.dram_tensor("SEL16", [16, 128], F32, kind="ExternalInput").ap()
    RALLH = nc.dram_tensor("RALLH", [NH * S], F32, kind="Internal").ap()
    Y2 = nc.dram_tensor("Y2", [NTOK, D], F32, kind="ExternalOutput").ap()

    s = Sched(nc)
    emit_L1a(nc, s, Y1, WQKVT, WOUTT, ABD, NEGTRI, NEGTRIL, IDENT, ONESR,
             SEL16, RALLH, Y2)
    s.emit()
    return nc


def prep_L1a(core_y1, Wqkv, Wout, a, b, dt):
    negtri = np.where(np.arange(128)[None, :] > np.arange(128)[:, None],
                      np.float32(-1e30), np.float32(0.0)).astype(np.float32)
    return dict(
        Y1=np.ascontiguousarray(core_y1.reshape(-1, D).astype(np.float32)),
        WQKVT=np.ascontiguousarray(Wqkv.T.astype(np.float32)),
        WOUTT=np.ascontiguousarray(Wout.T.astype(np.float32)),
        ABD=np.array([[a, b, dt]], dtype=np.float32),
        NEGTRI=negtri,
        NEGTRIL=np.ascontiguousarray(negtri.T),
        IDENT=np.eye(128, dtype=np.float32),
        ONESR=np.ones((1, 128), dtype=np.float32),
        SEL16=np.kron(np.eye(16, dtype=np.float32),
                      np.ones((1, 8), dtype=np.float32)),
    )



def emit_L0a(nc, s, X, WIHT, BIH, BHH, WHHT, WMS, BMS, IOTABC, IOTAV, RJP1,
             TRIPOS, ONES11, IDENT, Y0, quant_args=None):
    xbf = s.sb("xbf", [128, NT * D], BF16)
    xbuf = [s.sb(f"xbuf{i}", [128, D], BF16) for i in range(2)]
    xT = s.sb("xT", [128, 8 * 128], BF16)
    wiht = s.sb("wiht", [128, 8 * 128], BF16)
    whht = s.sb("whht", [20, 128], BF16)
    wms = s.sb("wms", [20, 128], BF16)
    bms = s.sb("bms", [128, 1], F32)
    bsum = s.sb("bsum", [128, 1], F32)
    xg_s = s.sb("xg_s", [128, S], F32)
    xg_int = s.sb("xg_int", [128, 2 * S], F32)
    pw = s.sb("pw", [32, 2 * S + 2], BF16)
    msr = s.sb("msr", [128, 2 * S], F32)
    a_vec = s.sb("a_vec", [64, S], F32)
    b_vec = s.sb("b_vec", [64, S], F32)
    b2_vec = s.sb("b2_vec", [64, S], F32)
    sg_vec = s.sb("sg_vec", [64, S], F32)
    mu_vec = s.sb("mu_vec", [64, S], F32)
    a2_vec = s.sb("a2_vec", [64, S], F32)
    sc_tmp = s.sb("sc_tmp", [64, S], F32)
    iotav = s.sb("iotav", [64, S], F32)
    iotabc = s.sb("iotabc", [128, S], F32)
    tripos = s.sb("tripos", [128, 128], F32)
    ones11 = s.sb("ones11", [64, 1], F32)
    idb = s.sb("idb", [128, 128], BF16)
    w_bf = s.sb("w_bf", [128, S], BF16)
    dum_bf = s.sb("dum_bf", [128, S], BF16)
    wt_bf = s.sb("wt_bf", [128, 128], BF16)
    ctx = s.sb("ctx", [128, D], F32)
    xres = [s.sb(f"xres{i}", [128, D], F32) for i in range(2)]
    rjcol = s.sb("rjcol", [128, 1], F32)
    cols = {k: s.sb(f"c_{k}", [128, 1], F32)
            for k in ["negmu", "sig", "s2", "negA", "negA2", "ssq", "rn"]}
    scl2 = s.sb("scl2", [128, 1], F32)
    bsum2 = s.sb("bsum2", [128, 1], F32)
    sgbuf = [s.sb(f"sgbuf_{q}", [32, 1], F32) for q in range(2)]
    v2t = [s.sb(f"v2t_{q}", [32, 1], F32) for q in range(2)]
    cc = [[s.sb(f"c{i}_{q}", [32, 1], F32) for i in range(2)] for q in range(2)]
    tc = [s.sb(f"tc_{q}", [32, 1], F32) for q in range(2)]
    ln = LNHelper(s)

    pT = s.ps("pT", [128, 128], BF16)
    ps1 = ExitStack()
    pA = ps1.enter_context(nc.psum_tensor("pA", [128, 512], F32))
    g_ps = [[ps1.enter_context(nc.psum_tensor(f"g{i}_{q}", [128, 1], F32))
             for i in range(2)] for q in range(2)]
    sg_ps = [ps1.enter_context(nc.psum_tensor(f"sg_ps{q}", [128, 1], F32))
             for q in range(2)]

    t_id = s.dma("pool", idb[:], IDENT)
    t_wih = s.dma("pool", wiht[:], WIHT.rearrange("(c p) f -> p c f", p=128))
    t_whh = s.dma("pool", whht[:], WHHT)
    t_wms = s.dma("pool", wms[:], WMS)
    t_bms = s.dma("sp", bms[:], BMS)
    t_bih = s.dma("sp", bsum[:], BIH)
    t_bhh = s.dma("sp", xg_s[:, 0:1], BHH)
    t_bsum = s.op("dve", lambda e: e.tensor_tensor(bsum[:], bsum[:], xg_s[:, 0:1],
                                                   op=ALU.add), after=(t_bih, t_bhh))
    t_scl_a = s.op("pool", lambda e: e.memset(scl2[0:96, :], 1.0))
    t_scl_b = s.op("pool", lambda e: e.memset(scl2[96:128, :], 2.0))
    t_b2a = s.op("dve", lambda e: e.tensor_copy(bsum2[:], bsum[:]), after=(t_bsum,))
    t_b2b = s.op("dve", lambda e: e.tensor_scalar(bsum2[96:128, :], bsum[96:128, :],
                                                  2.0, None, op0=ALU.mult),
                 after=(t_b2a,))
    t_iov = s.dma("sp", iotav[:], IOTAV)
    t_iob = s.dma("sp", iotabc[:], IOTABC)
    t_tri = s.dma("sp", tripos[:], TRIPOS)
    t_o11 = s.dma("sp", ones11[:], ONES11)
    t_ms0 = s.op("pool", lambda e: e.memset(pw[:, 0:2], 0.0))
    t_ms1 = s.op("pool", lambda e: e.memset(cc[0][0][:], 0.0))
    t_ms2 = s.op("pool", lambda e: e.memset(cc[0][1][:], 0.0))
    t_ms1b = s.op("pool", lambda e: e.memset(cc[1][0][:], 0.0))
    t_ms2b = s.op("pool", lambda e: e.memset(cc[1][1][:], 0.0))
    vec_ms = [s.op("pool", lambda e, tl=tl: e.memset(tl[:], 0.0))
              for tl in [a_vec, b_vec, b2_vec, sg_vec]]
    t_msv = s.op("pool", lambda e: e.memset(mu_vec[:], 0.0))

    buf = {}
    pT_tok = None
    pA_tok = None
    xg_toks = []
    for sq in range(NSEQ):
        for ti in range(NT):
            tile = sq * NT + ti
            bname = f"xbuf{ti % 2}"
            tl = s.dma("pool", xbuf[ti % 2][:], X[tile * 128:(tile + 1) * 128, :],
                       after=(buf.get(bname), t_id), stream="xa")
            trtoks = []
            last_pe = None
            for c in range(8):
                src = xbuf[ti % 2][:, c * 128:(c + 1) * 128]
                dst = xT[:, c * 128:(c + 1) * 128]
                pe_t, cp_t = emit_transpose(s, dst, src, idb, pT,
                                            after=(tl, buf.get("xT")),
                                            pst_free=pT_tok)
                pT_tok = cp_t
                trtoks.append(cp_t)
                last_pe = pe_t
            buf[bname] = last_pe
            mm_last = None
            for c in range(8):
                aft = (pA_tok, t_wih) + tuple(trtoks) if c == 0 else ()
                mm_last = s.op("pe", lambda e, c=c: e.matmul(
                    pA[:, 0:128], wiht[:, c * 128:(c + 1) * 128],
                    xT[:, c * 128:(c + 1) * 128],
                    start=(c == 0), stop=(c == 7)), after=aft, inc=(c == 7))
            buf["xT"] = mm_last
            t_ev = s.op("act", lambda e, ti=ti: e.activation(
                xg_s[:, ti * 128:(ti + 1) * 128], pA[:, 0:128], AF.Identity,
                bias=bsum2[:], scale=scl2[:]),
                after=(mm_last, t_b2b, t_scl_a, t_scl_b, buf.get("xg_s")))
            pA_tok = t_ev
            xg_toks.append(t_ev)
        t_di = s.op("dve", lambda e, sq=sq: e.tensor_copy(
            xg_int[:, sq::2], xg_s[:, 0:S]), after=tuple(xg_toks[-NT:]))
        buf["xg_s"] = t_di
        buf["xg_int_%d" % sq] = t_di
        if sq == 0:
            pass

    qstate = {"pA_tok": None}
    bg = None
    if quant_args is not None:
        W1T_, W2T_, W1QH_, W2QH_ = quant_args
        wldq = s.sb("wldq", [128, 1024], F32)
        qoutq = [s.sb(f"qoutq{i}", [128, 1024], BF16) for i in range(2)]
        m2bq = s.sb("m2bq", [128, 1024], BF16)
        sumsq = s.sb("sumsq", [128, 32], F32)
        totq = s.sb("totq", [128, 1], F32)
        thrq = s.sb("thrq", [128, 1], F32)
        nthrq = s.sb("nthrq", [128, 1], F32)
        t11q = s.sb("t11q", [1, 1], F32)
        onescolq = s.sb("onescolq", [128, 1], F32)
        onesrowq = s.sb("onesrowq", [1, 128], F32)
        t_oc = s.op("pool", lambda e: e.memset(onescolq[:], 1.0))
        t_or = s.op("pool", lambda e: e.memset(onesrowq[:], 1.0))

        def quant_gen(pA_war):
            t_prev = None
            for (WT, WQH, nchunk, npc) in [(W1T_, W1QH_, 8, 4),
                                           (W2T_, W2QH_, 32, 1)]:
                rtoks = []
                for c in range(nchunk):
                    for h in range(npc):
                        tl = s.dma("sp", wldq[:],
                                   WT[c * 128:(c + 1) * 128,
                                      h * 1024:(h + 1) * 1024],
                                   after=(t_prev,), stream="qw")
                        col = c * npc + h
                        tr = s.op("dve", lambda e, col=col: e.reduce_sum(
                            sumsq[:, col:col + 1], wldq[:], axis=AX,
                            apply_absolute_value=True), after=(tl,))
                        rtoks.append(tr)
                        t_prev = tr
                        yield
                t_f = s.op("dve", lambda e: e.reduce_sum(
                    totq[:], sumsq[:, 0:32], axis=AX), after=tuple(rtoks))
                t_t = s.op("pe", lambda e: e.matmul(pA[0:1, 0:1], totq[:],
                                                    onescolq[:], start=True,
                                                    stop=True),
                           after=(t_f, t_oc, pA_war))
                t_g = s.op("dve", lambda e: e.tensor_scalar(
                    t11q[:], pA[0:1, 0:1], 0.5 / (D * DFF), 0.5e-5,
                    op0=ALU.mult, op1=ALU.add), after=(t_t,))
                t_bc = s.op("pe", lambda e: e.matmul(pA[0:128, 1:2],
                                                     onesrowq[:], t11q[:],
                                                     start=True, stop=True),
                            after=(t_g, t_or))
                t_thr = s.op("dve", lambda e: e.tensor_copy(
                    thrq[:], pA[0:128, 1:2]), after=(t_bc,))
                t_nthr = s.op("dve", lambda e: e.tensor_scalar(
                    nthrq[:], thrq[:], -1.0, None, op0=ALU.mult),
                    after=(t_thr,))
                qstate["pA_tok"] = t_thr
                pA_war = t_thr
                spl = {}
                for c in range(nchunk):
                    for h in range(npc):
                        sl = (c * npc + h) % 2
                        tl = s.dma("sp", wldq[:],
                                   WT[c * 128:(c + 1) * 128,
                                      h * 1024:(h + 1) * 1024],
                                   after=(t_prev,), stream="qw")
                        t1 = s.op("dve", lambda e, sl=sl: e.tensor_scalar(
                            qoutq[sl][:], wldq[:], thrq[:], None,
                            op0=ALU.is_gt), after=(tl, t_nthr, spl.get(sl)))
                        t2 = s.op("dve", lambda e: e.tensor_scalar(
                            m2bq[:], wldq[:], nthrq[:], None, op0=ALU.is_lt))
                        t3 = s.op("dve", lambda e, sl=sl: e.tensor_tensor(
                            qoutq[sl][:], qoutq[sl][:], m2bq[:],
                            op=ALU.subtract))
                        ts_ = s.dma("sp", WQH[c * 128:(c + 1) * 128,
                                             h * 1024:(h + 1) * 1024],
                                    qoutq[sl][:], after=(t3,), stream="qs")
                        spl[sl] = ts_
                        t_prev = t3
                        yield
        bg = quant_gen(pA_tok)

    t_h = [None, None]
    init_toks = (t_ms0, t_ms1, t_ms2, t_ms1b, t_ms2b, t_whh,
                 buf["xg_int_0"], buf["xg_int_1"])
    for t in range(S):
        t_mm = [None, None]
        t_s3 = [None, None]
        t_c = [None, None]
        t_tc = [None, None]
        for q in range(2):
            col = 2 * t + q
            gq = g_ps[q][t % 2]
            aft_mm = (t_h[q],) if t_h[q] else init_toks
            t_mm[q] = s.op("pe", lambda e, col=col, gq=gq: e.matmul(
                gq[:], whht[:], pw[0:20, col:col + 1],
                start=True, stop=True), after=aft_mm)
        for q in range(2):
            col = 2 * t + q
            gq = g_ps[q][t % 2]
            aft = (t_mm[q],) + ((t_scl_a, t_scl_b) if t == 0 else ())
            t_s3[q] = s.op("act", lambda e, col=col, gq=gq, sq_=sg_ps[q]:
                           e.activation(sq_[:], gq[:], AF.Sigmoid,
                                        scale=scl2[:],
                                        bias=xg_int[:, col:col + 1]),
                           after=aft)
        for q in range(2):
            s.op("dve", lambda e, q=q, sq_=sg_ps[q]: e.tensor_scalar(
                sgbuf[q][:], sq_[96:128, :], 2.0, -1.0,
                op0=ALU.mult, op1=ALU.add), after=(t_s3[q],))
            s.op("dve", lambda e, q=q, sq_=sg_ps[q]: e.tensor_tensor(
                v2t[q][:], sq_[0:32, :], sgbuf[q][:], op=ALU.mult))
            t_c[q] = s.op("dve", lambda e, t=t, q=q, sq_=sg_ps[q]:
                          e.scalar_tensor_tensor(
                              cc[q][(t + 1) % 2][:], cc[q][t % 2][:],
                              sq_[32:64, :], v2t[q][:],
                              op0=ALU.mult, op1=ALU.add))
        for q in range(2):
            t_tc[q] = s.op("act", lambda e, t=t, q=q: e.activation(
                tc[q][:], cc[q][(t + 1) % 2][:], AF.Tanh), after=(t_c[q],))
        for q in range(2):
            col = 2 * t + q
            t_h[q] = s.op("dve", lambda e, col=col, q=q, sq_=sg_ps[q]:
                          e.tensor_tensor(pw[:, col + 2:col + 3], sq_[64:96, :],
                                          tc[q][:], op=ALU.mult),
                          after=(t_tc[q],))
        if bg is not None and t % 16 == 15:
            next(bg, None)
    if bg is not None:
        for _ in bg:
            pass
    if qstate["pA_tok"] is not None:
        pA_tok = qstate["pA_tok"]
    scan_done = t_h[0]
    scan_done2 = t_h[1]

    ms_toks = []
    for ch in range(8):
        mm = s.op("pe", lambda e, ch=ch: e.matmul(
            pA[:, 0:512], wms[:], pw[0:20, 2 + ch * 512: 2 + (ch + 1) * 512],
            start=True, stop=True),
            after=(scan_done, scan_done2, t_wms, pA_tok) if ch == 0
            else (ms_toks[-1],))
        evs = []
        for (base, fn) in [(0, AF.Relu), (32, AF.Relu), (64, AF.Relu),
                           (96, AF.Sigmoid)]:
            t_e = s.op("act", lambda e, ch=ch, base=base, fn=fn: e.activation(
                msr[base:base + 1, ch * 512:(ch + 1) * 512],
                pA[base:base + 1, 0:512], fn, bias=bms[base:base + 1, :]),
                after=(mm, t_bms, buf.get("msr")))
            evs.append(t_e)
        ms_toks.append(evs[-1])
        pA_tok = evs[-1]
        buf["msr"] = evs[-1]
    dtoks = []
    for (srcb, dst) in [(0, a_vec), (32, b_vec), (64, b2_vec), (96, sg_vec)]:
        for sq in range(2):
            t_d = s.op("dve", lambda e, srcb=srcb, dst=dst, sq=sq: e.tensor_copy(
                dst[32 * sq:32 * sq + 1, :], msr[srcb:srcb + 1, sq::2]),
                after=tuple(ms_toks) + tuple(vec_ms))
            dtoks.append(t_d)
    t_b1 = s.op("dve", lambda e: e.tensor_tensor(b2_vec[:], b2_vec[:], iotav[:],
                                                 op=ALU.mult),
                after=tuple(dtoks) + (t_iov,))
    t_b2 = s.op("dve", lambda e: e.tensor_tensor(b_vec[:], b_vec[:], b2_vec[:],
                                                 op=ALU.add), after=(t_b1,))
    t_b3 = s.op("dve", lambda e: e.tensor_scalar(b_vec[:], b_vec[:], 1.0 / S, None,
                                                 op0=ALU.mult), after=(t_b2,))
    A_cur, B_cur = a_vec, b_vec
    A_alt, B_alt = a2_vec, mu_vec
    t_prev = (t_b3, t_msv)
    d = 1
    while d < S:
        tA1 = s.op("dve", lambda e, A=A_cur, Aa=A_alt, d=d: e.tensor_copy(
            Aa[:, 0:d], A[:, 0:d]), after=t_prev)
        tA2 = s.op("dve", lambda e, A=A_cur, Aa=A_alt, d=d: e.tensor_tensor(
            Aa[:, d:S], A[:, d:S], A[:, 0:S - d], op=ALU.mult), after=t_prev)
        tB1 = s.op("dve", lambda e, B=B_cur, Ba=B_alt, d=d: e.tensor_copy(
            Ba[:, 0:d], B[:, 0:d]), after=t_prev)
        tB2 = s.op("dve", lambda e, A=A_cur, B=B_cur, d=d: e.tensor_tensor(
            sc_tmp[:, d:S], A[:, d:S], B[:, 0:S - d], op=ALU.mult),
            after=t_prev + (("dummy", 0),) if False else t_prev)
        tB3 = s.op("dve", lambda e, B=B_cur, Ba=B_alt, d=d: e.tensor_tensor(
            Ba[:, d:S], sc_tmp[:, d:S], B[:, d:S], op=ALU.add), after=(tB2,))
        t_prev = (tA1, tA2, tB1, tB3)
        A_cur, A_alt = A_alt, A_cur
        B_cur, B_alt = B_alt, B_cur
        d *= 2
    if B_cur is not mu_vec:
        t_mu = s.op("dve", lambda e, B=B_cur: e.tensor_copy(mu_vec[:], B[:]),
                    after=t_prev)
    else:
        t_mu = s.op("dve", lambda e: e.tensor_copy(sc_tmp[:, 0:8], mu_vec[:, 0:8]),
                    after=t_prev)

    ps1.close()
    ps2 = ExitStack()
    pCol = ps2.enter_context(nc.psum_tensor("pCol", [128, 8], F32))
    pCtx = [ps2.enter_context(nc.psum_tensor(f"pCtx{i}", [128, 512], F32))
            for i in range(2)]
    pT2b = ps2.enter_context(nc.psum_tensor("pT2b", [128, 128], BF16))
    def wsq_ap(b, c0, c1):
        return msr[:, b * S + c0: b * S + c1]
    wbf2 = [w_bf, s.sb("w_bfb", [128, S], BF16)]
    wt2 = [wt_bf, s.sb("wt_bfb", [128, 128], BF16)]

    t_out = None
    init_deps = (t_mu, pA_tok, scan_done)
    pCol_tok = None
    pCtx_tok = [None, None]
    ptok = [pT_tok, None]
    wt_war = [None, None]
    rj2 = [rjcol, s.sb("rjcolb", [128, 1], F32)]
    last_ctx_mm = None
    for sq in range(NSEQ):
        xbtoks = []
        for ti in range(NT):
            tile = sq * NT + ti
            t_x = s.dma("pool", xbf[:, ti * D:(ti + 1) * D],
                        X[tile * 128:(tile + 1) * 128, :],
                        after=(buf.get("xbf_ctx"),), stream="xa")
            xbtoks.append(t_x)
        for jt in range(NT):
            j0 = jt * 128
            shi = (jt + 1) * 128
            b = jt % 2
            rjc = rj2[b]
            t_rj = s.dma("sp", rjc[:], RJP1[j0:j0 + 128, :],
                         after=(buf.get(f"rjcol{b}"),), stream="rj")
            t_m1 = s.op("pe", lambda e, sq=sq, j0=j0: e.matmul(
                pCol[:, 0:1], mu_vec[32 * sq:32 * sq + 1, j0:j0 + 128],
                ones11[32 * sq:32 * sq + 1, :], start=True, stop=True),
                after=(t_mu, t_o11, pCol_tok) + init_deps)
            t_m2 = s.op("pe", lambda e, sq=sq, j0=j0: e.matmul(
                pCol[:, 1:2], sg_vec[32 * sq:32 * sq + 1, j0:j0 + 128],
                ones11[32 * sq:32 * sq + 1, :], start=True, stop=True), after=())
            t_nm = s.op("dve", lambda e: e.tensor_scalar(
                cols["negmu"][:], pCol[:, 0:1], -1.0, None, op0=ALU.mult),
                after=(t_m1, t_m2, buf.get("colswar")))
            t_sg = s.op("dve", lambda e: e.tensor_copy(cols["sig"][:], pCol[:, 1:2]),
                        after=(t_m2,))
            pCol_tok = t_sg
            t_s2 = s.op("dve", lambda e: e.tensor_tensor(
                cols["s2"][:], cols["sig"][:], cols["sig"][:], op=ALU.mult),
                after=(t_sg,))
            t_s3 = s.op("dve", lambda e: e.tensor_scalar(
                cols["s2"][:], cols["s2"][:], 2.0, None, op0=ALU.mult), after=(t_s2,))
            t_ra = s.op("dve", lambda e: e.reciprocal(cols["negA"][:], cols["s2"][:]),
                        after=(t_s3,))
            t_na = s.op("dve", lambda e: e.tensor_scalar(
                cols["negA"][:], cols["negA"][:], -1.0, None, op0=ALU.mult),
                after=(t_ra,))
            t_n2 = s.op("dve", lambda e: e.tensor_scalar(
                cols["negA2"][:], cols["negA"][:], 2.0, None, op0=ALU.mult),
                after=(t_na,))
            t_wsq = s.op("act", lambda e, shi=shi, b=b, rjc=rjc: e.activation(
                wsq_ap(b, 0, shi), iotabc[:, 0:shi], AF.Square,
                bias=cols["negmu"][:], scale=rjc[:]),
                after=(t_nm, t_rj, t_iob))
            t_mk = s.op("dve", lambda e, j0=j0, b=b: e.tensor_tensor(
                wsq_ap(b, j0, j0 + 128), wsq_ap(b, j0, j0 + 128), tripos[:],
                op=ALU.add), after=(t_wsq, t_tri))
            t_w = s.op("act", lambda e, shi=shi, b=b: e.activation(
                wbf2[b][:, 0:shi], wsq_ap(b, 0, shi), AF.Exp,
                scale=cols["negA"][:]),
                after=(t_mk, t_na, buf.get(f"w_bf{b}")))
            t_q = s.op("act", lambda e, shi=shi, b=b: e.activation(
                dum_bf[:, 0:shi], wsq_ap(b, 0, shi), AF.Exp,
                scale=cols["negA2"][:],
                accum_out=cols["ssq"][:]), after=(t_mk, t_n2, t_w))
            buf["colswar"] = t_q
            t_l = s.op("act", lambda e: e.activation(cols["rn"][:], cols["ssq"][:],
                                                     AF.Ln), after=(t_q,))
            t_x2 = s.op("act", lambda e: e.activation(cols["rn"][:], cols["rn"][:],
                                                      AF.Exp, scale=-0.5),
                        after=(t_l, buf.get("rn_war")))
            t_mn = s.op("dve", lambda e: e.tensor_scalar(
                cols["rn"][:], cols["rn"][:], 1e12, None, op0=ALU.min),
                after=(t_x2,))
            buf["rn_war"] = t_mn
            mm_last = [None, None]
            for ic in range(jt + 1):
                sl = ic % 2
                psl = pT if sl == 0 else pT2b
                aftT = (t_w, wt_war[sl])
                if ic == 0 and sq == 0 and jt == 0:
                    aftT = aftT + init_deps
                pe_t, cp_t = emit_transpose(s, wt2[sl][:],
                                            wbf2[b][:, ic * 128:(ic + 1) * 128],
                                            idb, psl, after=aftT,
                                            pst_free=ptok[sl])
                ptok[sl] = cp_t
                for hh in range(2):
                    mm_last[hh] = s.op("pe", lambda e, ic=ic, hh=hh, jt=jt,
                                       sl=sl: e.matmul(
                        pCtx[hh][:], wt2[sl][:],
                        xbf[:, ic * D + hh * 512: ic * D + (hh + 1) * 512],
                        start=(ic == 0), stop=(ic == jt)),
                        after=(cp_t, pCtx_tok[hh]) +
                              (tuple(xbtoks) if ic == 0 else ()),
                        inc=(ic == jt))
                wt_war[sl] = mm_last[1]
                last_ctx_mm = mm_last[1]
            buf[f"w_bf{b}"] = mm_last[1]
            ctx_evs = []
            for hh in range(2):
                t_cx = s.op("act", lambda e, hh=hh: e.activation(
                    ctx[:, hh * 512:(hh + 1) * 512], pCtx[hh][:], AF.Identity,
                    scale=cols["rn"][:]), after=(mm_last[hh], t_mn,
                                                 buf.get("ctx")))
                pCtx_tok[hh] = t_cx
                ctx_evs.append(t_cx)
            tile = sq * NT + jt
            rbuf = xres[jt % 2]
            t_xr = s.dma("sp", rbuf[:], X[tile * 128:(tile + 1) * 128, :],
                         after=(buf.get(f"xres{jt % 2}"),), stream="xb")
            t_rs = s.op("dve", lambda e, b=rbuf: e.tensor_tensor(
                b[:], ctx[:], b[:], op=ALU.add),
                after=(t_xr,) + tuple(ctx_evs))
            buf["ctx"] = t_rs
            t_ln = ln.emit(rbuf[:], rbuf[:], (t_rs,))
            t_out = s.dma("sp", Y0[tile * 128:(tile + 1) * 128, :], rbuf[:],
                          after=(t_ln,), stream="out")
            buf[f"xres{jt % 2}"] = t_out
            buf[f"rjcol{b}"] = t_wsq
        buf["xbf_ctx"] = last_ctx_mm
    pT_tok = ptok[0]
    s.barrier("sp", (t_out,))
    ps2.close()


def build_L0a():
    nc = bass.Bass("TRN2", target_bir_lowering=False)
    NTOK = TT * 128
    X = nc.dram_tensor("X", [NTOK, D], F32, kind="ExternalInput").ap()
    WIHT = nc.dram_tensor("WIHT", [D, 128], F32, kind="ExternalInput").ap()
    BIH = nc.dram_tensor("BIH", [128, 1], F32, kind="ExternalInput").ap()
    BHH = nc.dram_tensor("BHH", [128, 1], F32, kind="ExternalInput").ap()
    WHHT = nc.dram_tensor("WHHT", [20, 128], F32, kind="ExternalInput").ap()
    WMS = nc.dram_tensor("WMS", [20, 128], F32, kind="ExternalInput").ap()
    BMS = nc.dram_tensor("BMS", [128, 1], F32, kind="ExternalInput").ap()
    IOTABC = nc.dram_tensor("IOTABC", [128, S], F32, kind="ExternalInput").ap()
    IOTAV = nc.dram_tensor("IOTAV", [64, S], F32, kind="ExternalInput").ap()
    RJP1 = nc.dram_tensor("RJP1", [S, 1], F32, kind="ExternalInput").ap()
    TRIPOS = nc.dram_tensor("TRIPOS", [128, 128], F32, kind="ExternalInput").ap()
    ONES11 = nc.dram_tensor("ONES11", [64, 1], F32, kind="ExternalInput").ap()
    IDENT = nc.dram_tensor("IDENT", [128, 128], F32, kind="ExternalInput").ap()
    Y0 = nc.dram_tensor("Y0", [NTOK, D], F32, kind="ExternalOutput").ap()

    s = Sched(nc)
    emit_L0a(nc, s, X, WIHT, BIH, BHH, WHHT, WMS, BMS, IOTABC, IOTAV, RJP1, TRIPOS, ONES11, IDENT, Y0)
    s.emit()
    return nc


def prep_L0a(core_x, W_ih, W_hh, b_ih, b_hh, W_mu, b_mu, W_sig, b_sig):
    order = [0, 1, 3, 2]
    WIHT = np.zeros((D, 128), np.float32)
    BIH = np.zeros((128, 1), np.float32)
    BHH = np.zeros((128, 1), np.float32)
    WHHT = np.zeros((20, 128), np.float32)
    for slot, base in enumerate([0, 32, 64, 96]):
        li = order[slot]
        WIHT[:, base:base + 20] = W_ih[20 * li:20 * li + 20, :].T
        BIH[base:base + 20, 0] = b_ih[20 * li:20 * li + 20]
        BHH[base:base + 20, 0] = b_hh[20 * li:20 * li + 20]
        WHHT[:, base:base + 20] = W_hh[20 * li:20 * li + 20, :].T
    WMS = np.zeros((20, 128), np.float32)
    BMS = np.zeros((128, 1), np.float32)
    WMS[:, 0] = W_mu[0]; BMS[0, 0] = b_mu[0]
    WMS[:, 32] = W_mu[1]; BMS[32, 0] = b_mu[1]
    WMS[:, 64] = W_mu[2]; BMS[64, 0] = b_mu[2]
    WMS[:, 96] = W_sig[0]; BMS[96, 0] = b_sig[0]
    ii = np.arange(S, dtype=np.float32)
    return dict(
        X=np.ascontiguousarray(core_x.reshape(-1, D).astype(np.float32)),
        WIHT=WIHT, BIH=BIH, BHH=BHH, WHHT=WHHT, WMS=WMS, BMS=BMS,
        IOTABC=np.broadcast_to(ii, (128, S)).copy(),
        IOTAV=np.broadcast_to(ii + 1.0, (64, S)).copy(),
        RJP1=(1.0 / (ii + 1.0)).reshape(S, 1).copy(),
        TRIPOS=np.where(np.arange(128)[None, :] > np.arange(128)[:, None],
                        np.float32(1e30), np.float32(0.0)).astype(np.float32),
        ONES11=np.ones((64, 1), np.float32),
        IDENT=np.eye(128, dtype=np.float32),
    )



def build_all(debug=False):
    nc = bass.Bass("TRN2", target_bir_lowering=False)
    NTOK = TT * 128
    NE = 8
    d = {}
    X = nc.dram_tensor("X", [NTOK, D], F32, kind="ExternalInput").ap()
    WIHT = nc.dram_tensor("WIHT", [D, 128], F32, kind="ExternalInput").ap()
    BIH = nc.dram_tensor("BIH", [128, 1], F32, kind="ExternalInput").ap()
    BHH = nc.dram_tensor("BHH", [128, 1], F32, kind="ExternalInput").ap()
    WHHT = nc.dram_tensor("WHHT", [20, 128], F32, kind="ExternalInput").ap()
    WMS = nc.dram_tensor("WMS", [20, 128], F32, kind="ExternalInput").ap()
    BMS = nc.dram_tensor("BMS", [128, 1], F32, kind="ExternalInput").ap()
    IOTABC = nc.dram_tensor("IOTABC", [128, S], F32, kind="ExternalInput").ap()
    IOTAV = nc.dram_tensor("IOTAV", [64, S], F32, kind="ExternalInput").ap()
    RJP1 = nc.dram_tensor("RJP1", [S, 1], F32, kind="ExternalInput").ap()
    TRIPOS = nc.dram_tensor("TRIPOS", [128, 128], F32, kind="ExternalInput").ap()
    ONES11 = nc.dram_tensor("ONES11", [64, 1], F32, kind="ExternalInput").ap()
    IDENT = nc.dram_tensor("IDENT", [128, 128], F32, kind="ExternalInput").ap()
    GATEW = nc.dram_tensor("GATEW", [D, NE], F32, kind="ExternalInput").ap()
    GATEB = nc.dram_tensor("GATEB", [128, NE], F32, kind="ExternalInput").ap()
    EXPW = nc.dram_tensor("EXPW", [NE, D, D], F32, kind="ExternalInput").ap()
    EXPB = nc.dram_tensor("EXPB", [NE, D], F32, kind="ExternalInput").ap()
    IDX8 = nc.dram_tensor("IDX8", [128, NE], F32, kind="ExternalInput").ap()
    IDXB8 = nc.dram_tensor("IDXB8", [128, NE], F32, kind="ExternalInput").ap()
    WQKVT = nc.dram_tensor("WQKVT", [D, 3 * D], F32, kind="ExternalInput").ap()
    WOUTT = nc.dram_tensor("WOUTT", [D, D], F32, kind="ExternalInput").ap()
    ABD = nc.dram_tensor("ABD", [1, 3], F32, kind="ExternalInput").ap()
    NEGTRI = nc.dram_tensor("NEGTRI", [128, 128], F32, kind="ExternalInput").ap()
    NEGTRIL = nc.dram_tensor("NEGTRIL", [128, 128], F32, kind="ExternalInput").ap()
    ONESR = nc.dram_tensor("ONESR", [1, 128], F32, kind="ExternalInput").ap()
    SEL16 = nc.dram_tensor("SEL16", [16, 128], F32, kind="ExternalInput").ap()
    RALLH = nc.dram_tensor("RALLH", [16 * S], F32, kind="Internal").ap()
    W1T = nc.dram_tensor("W1T", [D, DFF], F32, kind="ExternalInput").ap()
    B1C = nc.dram_tensor("B1C", [DFF], F32, kind="ExternalInput").ap()
    W2T = nc.dram_tensor("W2T", [DFF, D], F32, kind="ExternalInput").ap()
    B2C = nc.dram_tensor("B2C", [D], F32, kind="ExternalInput").ap()
    ONESC = nc.dram_tensor("ONESC", [128, 1], F32, kind="ExternalInput").ap()
    OUT = nc.dram_tensor("OUT", [NTOK, D], F32, kind="ExternalOutput").ap()
    Y0i = nc.dram_tensor("Y0i", [NTOK, D], F32, kind="Internal").ap()
    Y1i = nc.dram_tensor("Y1i", [NTOK, D], F32, kind="Internal").ap()
    Y2i = nc.dram_tensor("Y2i", [NTOK, D], F32, kind="Internal").ap()
    W1QH = nc.dram_tensor("W1QH", [D, DFF], BF16, kind="Internal").ap()
    W2QH = nc.dram_tensor("W2QH", [DFF, D], BF16, kind="Internal").ap()

    s = Sched(nc)
    s.push_scope("a_")
    emit_L0a(nc, s, X, WIHT, BIH, BHH, WHHT, WMS, BMS, IOTABC, IOTAV, RJP1,
             TRIPOS, ONES11, IDENT, Y0i, quant_args=(W1T, W2T, W1QH, W2QH))
    s.pop_scope()
    s.push_scope("b_")
    emit_L0f(nc, s, Y0i, GATEW, GATEB, EXPW, EXPB, IDENT, IDX8, IDXB8, Y1i)
    s.pop_scope()
    s.push_scope("c_")
    emit_L1a(nc, s, Y1i, WQKVT, WOUTT, ABD, NEGTRI, NEGTRIL, IDENT, ONESR,
             SEL16, RALLH, Y2i)
    s.pop_scope()
    s.push_scope("d_")
    emit_L1f(nc, s, Y2i, W1T, B1C, W2T, B2C, IDENT, ONESC, ONESR, OUT,
             WQH=(W1QH, W2QH))
    s.pop_scope()
    s.emit()
    return nc


def prep_all(core_x, inputs):
    im = {}
    im.update(prep_L0a(core_x, inputs["L0_attn_W_ih"], inputs["L0_attn_W_hh"],
                       inputs["L0_attn_b_ih"], inputs["L0_attn_b_hh"],
                       inputs["L0_attn_W_mu"], inputs["L0_attn_b_mu"],
                       inputs["L0_attn_W_sig"], inputs["L0_attn_b_sig"]))
    d = prep_L0f(np.zeros((1, D), np.float32), inputs["L0_ffn_gate_W"],
                 inputs["L0_ffn_gate_b"], inputs["L0_ffn_expert_W"],
                 inputs["L0_ffn_expert_b"])
    d.pop("Y0"); d.pop("IDENT")
    im.update(d)
    d = prep_L1a(np.zeros((1, D), np.float32), inputs["L1_attn_Wqkv"],
                 inputs["L1_attn_Wout"], inputs["L1_attn_a"],
                 inputs["L1_attn_b"], inputs["L1_attn_dt"])
    d.pop("Y1"); d.pop("IDENT")
    im.update(d)
    d = prep_L1f(np.zeros((1, D), np.float32), inputs["L1_ffn_W1"],
                 inputs["L1_ffn_b1"], inputs["L1_ffn_W2"], inputs["L1_ffn_b2"])
    d.pop("Y2"); d.pop("IDENT")
    im.update(d)
    return im


def kernel_fused(**inputs):
    x = np.asarray(inputs["L0_attn_x"], dtype=np.float32)
    cores = list(range(NCORES))
    nc = build_all()
    ims = [prep_all(x[2 * c:2 * c + 2], inputs) for c in cores]
    r = run_bass_kernel_spmd(nc, ims, core_ids=cores)
    out = _gather(r, "OUT")
    return np.concatenate([o.reshape(NSEQ, S, D) for o in out], axis=0)


# version 14
# speedup vs baseline: 1.0073x; 1.0073x over previous
import numpy as np
import ml_dtypes
from contextlib import ExitStack
import concourse.bass as bass
import concourse.mybir as mybir
from concourse.bass_utils import run_bass_kernel_spmd

F32 = mybir.dt.float32
BF16 = mybir.dt.bfloat16
AX = mybir.AxisListType.X
ALU = mybir.AluOpType
AF = mybir.ActivationFunctionType

NCORES = 8
NSEQ = 2
S = 2048
D = 1024
NT = S // 128
TT = NSEQ * NT
DFF = 4096

ENG2BLOCK = {"pe": "tensor", "act": "scalar", "dve": "vector",
             "pool": "gpsimd", "sp": "sync"}


class Sched:

    def __init__(self, nc):
        self.nc = nc
        self.ops = []
        semnames = list(ENG2BLOCK) + ["poolc"]
        self.cnt = {e: 0 for e in semnames}
        self.es = ExitStack()
        self.sems = {e: self.es.enter_context(nc.semaphore(f"sem_{e}"))
                     for e in semnames}
        self.dstreams = {}
        self.scopes = [self.es]
        self.prefix = ""
        self.base_after = ()

    def push_scope(self, prefix):
        self.scopes.append(ExitStack())
        self.prefix = prefix
        self.base_after = tuple((k, v) for k, v in self.cnt.items() if v > 0)

    def pop_scope(self):
        self.scopes.pop().close()
        self.prefix = ""
        self.base_after = ()

    def sb(self, name, shape, dtype=F32):
        return self.scopes[-1].enter_context(
            self.nc.sbuf_tensor(self.prefix + name, shape, dtype))

    def ps(self, name, shape, dtype=F32):
        return self.scopes[-1].enter_context(
            self.nc.psum_tensor(self.prefix + name, shape, dtype))

    def op(self, eng, fn, after=(), inc=True):
        step = 16 if inc == "dma" else (1 if inc else 0)
        sem = eng
        if eng == "pool" and step == 1:
            sem = "poolc"
        aft = tuple(a for a in after if a is not None) + self.base_after
        self.ops.append((eng, sem, aft, fn, step))
        if step:
            self.cnt[sem] += step
            return (sem, self.cnt[sem])
        return None

    def dma(self, queue, out, in_, after=(), stream=None, **kw):
        stream = "dma_" + (stream if stream is not None else queue)
        if stream not in self.dstreams:
            self.sems[stream] = self.es.enter_context(self.nc.semaphore(stream))
            self.cnt[stream] = 0
            self.dstreams[stream] = True
        prev = (stream, self.cnt[stream]) if self.cnt[stream] else None
        aft = tuple(after) + ((prev,) if prev else ()) + self.base_after
        self.cnt[stream] += 16
        tok = (stream, self.cnt[stream])
        self.ops.append((queue, stream,
                         tuple(a for a in aft if a is not None),
                         lambda e: e.dma_start(out=out, in_=in_, **kw), 16))
        return tok

    def barrier(self, eng, after):
        self.op(eng, lambda e: None, after=after, inc=False)

    def emit(self):
        per_eng = {e: [] for e in ENG2BLOCK}
        for (eng, sem, aft, fn, step) in self.ops:
            per_eng[eng].append((sem, aft, fn, step))
        sems = self.sems

        def make_body(eng):
            def body(proxy):
                waited = {}
                for (sem, aft, fn, step) in per_eng[eng]:
                    for (weng, wval) in aft:
                        if waited.get(weng, 0) >= wval:
                            continue
                        proxy.wait_ge(sems[weng], wval)
                        waited[weng] = wval
                    ins = fn(proxy)
                    if step:
                        assert ins is not None
                        ins.then_inc(sems[sem], step)
            return body

        with self.nc.Block() as block:
            for eng, bname in ENG2BLOCK.items():
                if per_eng[eng]:
                    getattr(block, bname)(make_body(eng))
        self.es.close()



def emit_transpose(s, dst_ap, src_ap, idb, pst, after=(), pst_free=None,
                   copier="act"):
    p, f = src_ap.shape[0], src_ap.shape[1]
    t1 = s.op("pe", lambda e: e.transpose(pst[0:f, 0:p], src_ap, idb[0:p, 0:p]),
              after=tuple(after) + ((pst_free,) if pst_free else ()))
    t2 = s.op(copier, lambda e: (e.activation(dst_ap, pst[0:f, 0:p], AF.Copy)
                                 if copier == "act"
                                 else e.tensor_copy(dst_ap, pst[0:f, 0:p])),
              after=(t1,))
    return t1, t2


class LNHelper:

    def __init__(self, s, tag=""):
        self.s = s
        self.sets = []
        for i in range(2):
            dummy = s.sb(f"ln_dummy{tag}{i}", [128, D], BF16)
            t = {k: s.sb(f"ln_{k}{tag}{i}", [128, 1], F32)
                 for k in ["ssq", "sm", "mean", "rstd", "mb", "eps"]}
            t_eps = s.op("pool", lambda e, t=t: e.memset(t["eps"][:], 1e-5))
            self.sets.append({"dummy": dummy, "t": t, "eps": t_eps, "prev": None})
        self.i = 0

    def emit(self, res_ap, out_ap, after):
        s = self.s
        st = self.sets[self.i % 2]
        self.i += 1
        t = st["t"]
        dummy = st["dummy"]
        aft = tuple(after) + ((st["prev"],) if st["prev"] else ()) + (st["eps"],)
        t_sq = s.op("act", lambda e: e.activation(dummy[:], res_ap, AF.Square,
                                                  accum_out=t["ssq"][:]), after=aft)
        t_sm = s.op("dve", lambda e: e.reduce_sum(t["sm"][:], res_ap, axis=AX),
                    after=aft)
        t_mean = s.op("dve", lambda e: e.tensor_scalar(t["mean"][:], t["sm"][:],
                                                       1.0 / D, None, op0=ALU.mult),
                      after=(t_sm,))
        t_v1 = s.op("dve", lambda e: e.tensor_scalar(t["rstd"][:], t["ssq"][:],
                                                     1.0 / D, None, op0=ALU.mult),
                    after=(t_sq,))
        t_v2 = s.op("dve", lambda e: e.scalar_tensor_tensor(t["mb"][:], t["mean"][:],
                                                            -1.0, t["mean"][:],
                                                            op0=ALU.mult, op1=ALU.mult),
                    after=(t_mean,))
        t_v3 = s.op("dve", lambda e: e.tensor_tensor(t["rstd"][:], t["rstd"][:],
                                                     t["mb"][:], op=ALU.add),
                    after=(t_v1, t_v2))
        t_l = s.op("act", lambda e: e.activation(t["rstd"][:], t["rstd"][:], AF.Ln,
                                                 bias=t["eps"][:]),
                   after=(t_v3,))
        t_e = s.op("act", lambda e: e.activation(t["rstd"][:], t["rstd"][:], AF.Exp,
                                                 scale=-0.5), after=(t_l,))
        t_mb = s.op("dve", lambda e: e.scalar_tensor_tensor(t["mb"][:], t["mean"][:],
                                                            -1.0, t["rstd"][:],
                                                            op0=ALU.mult, op1=ALU.mult),
                    after=(t_e,))
        t_out = s.op("act", lambda e: e.activation(out_ap, res_ap, AF.Identity,
                                                   bias=t["mb"][:], scale=t["rstd"][:]),
                     after=(t_mb,))
        st["prev"] = t_out
        return t_out



def emit_L1f(nc, s, Y2, W1T, B1C, W2T, B2C, IDENT, ONESC, ONESR, OUT,
             WQH=None):
    w1q = s.sb("w1q", [128, 8 * DFF], BF16)
    w2q = s.sb("w2q", [128, 32 * D], BF16)
    wld = s.sb("wld", [128, 1024], F32)
    m2b = s.sb("m2b", [128, 1024], BF16)
    y2bf = [s.sb(f"y2bf{i}", [128, D], BF16) for i in range(2)]
    y2T = s.sb("y2T", [128, 8 * 512], BF16)
    hT = s.sb("hT", [128, 32 * 512], BF16)
    outT = s.sb("outT", [128, 8 * 512], BF16)
    res = [s.sb(f"res{i}", [128, D], F32) for i in range(2)]
    idb = s.sb("idb", [128, 128], BF16)
    idf = s.sb("idf", [128, 128], F32)
    onescol = s.sb("onescol", [128, 1], F32)
    onesrow = s.sb("onesrow", [1, 128], F32)
    b1r = s.sb("b1r", [32, 128], F32)
    b2r = s.sb("b2r", [8, 128], F32)
    b1c = s.sb("b1c", [128, 32], F32)
    b2c = s.sb("b2c", [128, 8], F32)
    sums = s.sb("sums", [128, 32], F32)
    thr = s.sb("thr", [128, 1], F32)
    nthr = s.sb("nthr", [128, 1], F32)
    tot = s.sb("tot", [128, 1], F32)
    t11 = s.sb("t11", [1, 1], F32)
    ln = LNHelper(s)

    pA = [s.ps(f"pA{i}", [128, 512], F32) for i in range(4)]
    pT = s.ps("pT", [128, 128], BF16)
    pTb = s.ps("pTb", [128, 128], BF16)
    pQ = s.ps("pQ", [128, 128], F32)
    p11 = s.ps("p11", [1, 512], F32)

    t_id = s.dma("pool", idb[:], IDENT)
    t_idf = s.dma("sp", idf[:], IDENT)
    t_ones = s.dma("sp", onescol[:], ONESC)
    t_onesr = s.dma("sp", onesrow[:], ONESR)
    t_b1r = s.dma("sp", b1r[:], B1C.rearrange("(c p) -> c p", p=128))
    t_b2r = s.dma("sp", b2r[:], B2C.rearrange("(c p) -> c p", p=128))

    tb1 = s.op("pe", lambda e: e.transpose(pQ[:, 0:32], b1r[:], idf[0:32, 0:32]),
               after=(t_b1r, t_idf))
    tb1c = s.op("dve", lambda e: e.tensor_copy(b1c[:], pQ[:, 0:32]), after=(tb1,))
    tb2 = s.op("pe", lambda e: e.transpose(pQ[:, 32:40], b2r[:], idf[0:8, 0:8]),
               after=(t_b2r, t_idf))
    tb2c = s.op("dve", lambda e: e.tensor_copy(b2c[:], pQ[:, 32:40]), after=(tb2,))

    def quantize(WT, nchunk, qdst, qdst_cols, nelem, entry):
        npc = WT.shape[1] // 1024
        half = 1024
        toks = []
        t_prev = entry
        for c in range(nchunk):
            for h in range(npc):
                tl = s.dma("sp", wld[:, 0:half],
                           WT[c * 128:(c + 1) * 128, h * half:(h + 1) * half],
                           after=(t_prev,), stream="w")
                col = (c * npc + h) % 32
                prevw = toks[col - 32] if len(toks) >= 32 else None
                tr = s.op("dve", lambda e, col=col: e.tensor_scalar(
                    sums[:, col:col + 1], wld[:, 0:half], 1.0, None, op0=ALU.mult)
                    if False else e.reduce_sum(
                    sums[:, col:col + 1], wld[:, 0:half], axis=AX,
                    apply_absolute_value=True), after=(tl,))
                toks.append(tr)
                t_prev = tr
        assert nchunk * npc <= 32, "sums col reuse would need accumulation"
        ncols = nchunk * npc
        t_f = s.op("dve", lambda e: e.reduce_sum(tot[:], sums[:, 0:ncols], axis=AX),
                   after=tuple(toks))
        t_t = s.op("pe", lambda e: e.matmul(p11[0:1, 0:1], tot[:], onescol[:],
                                            start=True, stop=True),
                   after=(t_f, t_ones))
        t_g = s.op("dve", lambda e: e.tensor_scalar(t11[:], p11[0:1, 0:1],
                                                    0.5 / nelem, 0.5e-5,
                                                    op0=ALU.mult, op1=ALU.add),
                   after=(t_t,))
        t_bc = s.op("pe", lambda e: e.matmul(pQ[:, 0:1], onesrow[:], t11[:],
                                             start=True, stop=True),
                    after=(t_g, t_onesr, tb1c, tb2c))
        t_thr = s.op("dve", lambda e: e.tensor_copy(thr[:], pQ[:, 0:1]),
                     after=(t_bc,))
        t_nthr = s.op("dve", lambda e: e.tensor_scalar(nthr[:], thr[:], -1.0, None,
                                                       op0=ALU.mult), after=(t_thr,))
        t_prev = t_nthr
        out_toks = []
        for c in range(nchunk):
            for h in range(npc):
                tl = s.dma("sp", wld[:, 0:half],
                           WT[c * 128:(c + 1) * 128, h * half:(h + 1) * half],
                           after=(t_prev,), stream="w")
                dst = qdst[:, c * qdst_cols + h * half: c * qdst_cols + (h + 1) * half]
                t1 = s.op("dve", lambda e, dst=dst: e.tensor_scalar(
                    dst, wld[:, 0:half], thr[:], None, op0=ALU.is_gt), after=(tl,))
                t2 = s.op("dve", lambda e: e.tensor_scalar(
                    m2b[:, 0:half], wld[:, 0:half], nthr[:], None, op0=ALU.is_lt),
                    after=(tl, t1))
                t3 = s.op("dve", lambda e, dst=dst: e.tensor_tensor(
                    dst, dst, m2b[:, 0:half], op=ALU.subtract), after=(t2,))
                out_toks.append(t3)
                t_prev = t3
        return out_toks[-1]

    if WQH is not None:
        W1QH, W2QH = WQH
        t_q1c = [s.dma("pool", w1q[:, dc * DFF:(dc + 1) * DFF],
                       W1QH[dc * 128:(dc + 1) * 128, :],
                       stream=("w" if dc % 2 == 0 else "wb"))
                 for dc in range(8)]
        t_q2c = [s.dma("pool",
                       w2q[:, g8 * 8 * D:(g8 + 1) * 8 * D].rearrange(
                           "p (c f) -> p c f", f=D),
                       W2QH[g8 * 1024:(g8 + 1) * 1024, :].rearrange(
                           "(c p) f -> p c f", p=128),
                       stream=("w" if g8 % 2 == 0 else "wb"))
                 for g8 in range(4)]
        t_q1 = t_q1c[-1]
        t_q2 = t_q2c[-1]
    else:
        t_q1c = t_q2c = None
        t_q1 = quantize(W1T, 8, w1q, DFF, D * DFF, None)
        t_q2 = quantize(W2T, 32, w2q, D, D * DFF, t_q1)

    buf_tok = {}
    pA_tok = [None] * 4
    pT_toks = [None, None]
    t_out_dma = None
    for g in range(TT // 4):
        trtoks = []
        for ti in range(4):
            tile = g * 4 + ti
            bname = f"y2bf{ti % 2}"
            tl = s.dma("pool", y2bf[ti % 2][:],
                       Y2[tile * 128:(tile + 1) * 128, :],
                       after=(buf_tok.get(bname), t_id), stream="xa")
            last_pe = None
            for c in range(8):
                src = y2bf[ti % 2][:, c * 128:(c + 1) * 128]
                dst = y2T[:, c * 512 + ti * 128: c * 512 + ti * 128 + 128]
                sl = c % 2
                pe_t, cp_t = emit_transpose(s, dst, src, idb,
                                            pT if sl == 0 else pTb,
                                            after=(tl,), pst_free=pT_toks[sl])
                pT_toks[sl] = cp_t
                last_pe = pe_t
                trtoks.append(cp_t)
            buf_tok[bname] = last_pe
        h_toks = []
        for fc in range(32):
            pb = pA[fc % 4]
            mm_last = None
            for dc in range(8):
                lhs = w1q[:, dc * DFF + fc * 128: dc * DFF + (fc + 1) * 128]
                rhs = y2T[:, dc * 512:(dc + 1) * 512]
                aft = ((pA_tok[fc % 4],) + tuple(trtoks)) if dc == 0 else ()
                if t_q1c is not None:
                    aft = aft + (t_q1c[dc],)
                elif dc == 0:
                    aft = aft + (t_q1,)
                mm_last = s.op("pe", lambda e, pb=pb, lhs=lhs, rhs=rhs, dc=dc:
                               e.matmul(pb[:], lhs, rhs, start=(dc == 0),
                                        stop=(dc == 7)),
                               after=aft, inc=(dc == 7))
            t_ev = s.op("act", lambda e, pb=pb, fc=fc: e.activation(
                hT[:, fc * 512:(fc + 1) * 512], pb[:], AF.Relu,
                bias=b1c[:, fc:fc + 1]), after=(mm_last, tb1c))
            pA_tok[fc % 4] = t_ev
            h_toks.append(t_ev)
        o_toks = []
        for oc in range(8):
            pb = pA[oc % 4]
            mm_last = None
            for fc in range(32):
                lhs = w2q[:, fc * D + oc * 128: fc * D + (oc + 1) * 128]
                rhs = hT[:, fc * 512:(fc + 1) * 512]
                aft = ((pA_tok[oc % 4],) + tuple(h_toks)) if fc == 0 else ()
                if t_q2c is not None:
                    aft = aft + (t_q2c[fc // 8],)
                elif fc == 0:
                    aft = aft + (t_q2,)
                mm_last = s.op("pe", lambda e, pb=pb, lhs=lhs, rhs=rhs, fc=fc:
                               e.matmul(pb[:], lhs, rhs, start=(fc == 0),
                                        stop=(fc == 31)),
                               after=aft, inc=(fc == 31))
            t_ev = s.op("act", lambda e, pb=pb, oc=oc: e.activation(
                outT[:, oc * 512:(oc + 1) * 512], pb[:], AF.Identity,
                bias=b2c[:, oc:oc + 1]), after=(mm_last, tb2c))
            pA_tok[oc % 4] = t_ev
            o_toks.append(t_ev)
        for ti in range(4):
            tile = g * 4 + ti
            rname = f"res{ti % 2}"
            tlf = s.dma("sp", res[ti % 2][:], Y2[tile * 128:(tile + 1) * 128, :],
                        after=(buf_tok.get(rname),), stream="xb")
            add_toks = []
            for oc in range(8):
                src = outT[:, oc * 512 + ti * 128: oc * 512 + ti * 128 + 128]
                sl = oc % 2
                psl = pT if sl == 0 else pTb
                t1 = s.op("pe", lambda e, src=src, psl=psl: e.transpose(
                    psl[:], src, idb[:]),
                          after=tuple(o_toks) + (pT_toks[sl], tlf))
                t2 = s.op("dve", lambda e, oc=oc, b=res[ti % 2], psl=psl:
                          e.tensor_tensor(
                    b[:, oc * 128:(oc + 1) * 128], psl[:],
                    b[:, oc * 128:(oc + 1) * 128], op=ALU.add),
                    after=(t1, tlf))
                pT_toks[sl] = t2
                add_toks.append(t2)
            t_ln = ln.emit(res[ti % 2][:], res[ti % 2][:], tuple(add_toks))
            t_out_dma = s.dma("sp", OUT[tile * 128:(tile + 1) * 128, :],
                              res[ti % 2][:], after=(t_ln,), stream="out")
            buf_tok[rname] = t_out_dma
    s.barrier("sp", (t_out_dma,))


def build_L1f():
    nc = bass.Bass("TRN2", target_bir_lowering=False)
    NTOK = TT * 128
    Y2 = nc.dram_tensor("Y2", [NTOK, D], F32, kind="ExternalInput").ap()
    W1T = nc.dram_tensor("W1T", [D, DFF], F32, kind="ExternalInput").ap()
    B1C = nc.dram_tensor("B1C", [DFF], F32, kind="ExternalInput").ap()
    W2T = nc.dram_tensor("W2T", [DFF, D], F32, kind="ExternalInput").ap()
    B2C = nc.dram_tensor("B2C", [D], F32, kind="ExternalInput").ap()
    IDENT = nc.dram_tensor("IDENT", [128, 128], F32, kind="ExternalInput").ap()
    ONESC = nc.dram_tensor("ONESC", [128, 1], F32, kind="ExternalInput").ap()
    ONESR = nc.dram_tensor("ONESR", [1, 128], F32, kind="ExternalInput").ap()
    OUT = nc.dram_tensor("OUT", [NTOK, D], F32, kind="ExternalOutput").ap()

    s = Sched(nc)
    emit_L1f(nc, s, Y2, W1T, B1C, W2T, B2C, IDENT, ONESC, ONESR, OUT)
    s.emit()
    return nc


def prep_L1f(core_y2, W1, b1, W2, b2):
    return dict(
        Y2=np.ascontiguousarray(core_y2.reshape(-1, D).astype(np.float32)),
        W1T=np.ascontiguousarray(W1.T.astype(np.float32)),
        B1C=np.ascontiguousarray(b1.astype(np.float32)),
        W2T=np.ascontiguousarray(W2.T.astype(np.float32)),
        B2C=np.ascontiguousarray(b2.astype(np.float32)),
        IDENT=np.eye(128, dtype=np.float32),
        ONESC=np.ones((128, 1), dtype=np.float32),
        ONESR=np.ones((1, 128), dtype=np.float32),
    )


def _gather(res, key):
    return [res.results[c][key] for c in range(NCORES)]


def kernel(**inputs):
    return kernel_fused(**inputs)


def kernel_multilaunch(**inputs):
    x = np.asarray(inputs["L0_attn_x"], dtype=np.float32)
    cores = list(range(NCORES))
    xs = [x[2 * c:2 * c + 2] for c in cores]

    nc = build_L0a()
    ims = [prep_L0a(xs[c], inputs["L0_attn_W_ih"], inputs["L0_attn_W_hh"],
                    inputs["L0_attn_b_ih"], inputs["L0_attn_b_hh"],
                    inputs["L0_attn_W_mu"], inputs["L0_attn_b_mu"],
                    inputs["L0_attn_W_sig"], inputs["L0_attn_b_sig"])
           for c in cores]
    r = run_bass_kernel_spmd(nc, ims, core_ids=cores)
    y0 = _gather(r, "Y0")

    nc = build_L0f()
    ims = [prep_L0f(y0[c], inputs["L0_ffn_gate_W"], inputs["L0_ffn_gate_b"],
                    inputs["L0_ffn_expert_W"], inputs["L0_ffn_expert_b"])
           for c in cores]
    r = run_bass_kernel_spmd(nc, ims, core_ids=cores)
    y1 = _gather(r, "OUT")

    nc = build_L1a()
    ims = [prep_L1a(y1[c], inputs["L1_attn_Wqkv"], inputs["L1_attn_Wout"],
                    inputs["L1_attn_a"], inputs["L1_attn_b"],
                    inputs["L1_attn_dt"]) for c in cores]
    r = run_bass_kernel_spmd(nc, ims, core_ids=cores)
    y2 = _gather(r, "Y2")

    nc = build_L1f()
    ims = [prep_L1f(y2[c], inputs["L1_ffn_W1"], inputs["L1_ffn_b1"],
                    inputs["L1_ffn_W2"], inputs["L1_ffn_b2"]) for c in cores]
    r = run_bass_kernel_spmd(nc, ims, core_ids=cores)
    out = _gather(r, "OUT")

    return np.concatenate([o.reshape(NSEQ, S, D) for o in out], axis=0)



def emit_L0f(nc, s, Y0, GATEW, GATEB, EXPW, EXPB, IDENT, IDX8, IDXB8, OUT, debug=False):
    NTOK = TT * 128
    NE = 8
    if debug:
        DBG_P = nc.dram_tensor("DBG_P", [NTOK, 8], F32, kind="ExternalOutput").ap()
        DBG_C = nc.dram_tensor("DBG_C", [NTOK, 8], F32, kind="ExternalOutput").ap()
        DBG_A = nc.dram_tensor("DBG_A", [NTOK, D], F32, kind="ExternalOutput").ap()
        DBG_I = nc.dram_tensor("DBG_I", [NTOK, D], F32, kind="ExternalOutput").ap()
    y0T = s.sb("y0T", [128, 64 * 128], BF16)
    y0f = s.sb("y0f", [128, 8 * D], F32)
    acc = s.sb("acc", [128, 8 * D], F32)
    expw = [s.sb(f"expw{i}", [128, 8 * D], BF16) for i in range(2)]
    y0bf = [s.sb(f"y0bf{i}", [128, D], BF16) for i in range(2)]
    gatew = s.sb("gatew", [128, 8 * NE], BF16)
    gateb = s.sb("gateb", [128, NE], F32)
    expb = s.sb("expb", [NE, D], BF16)
    idb = s.sb("idb", [128, 128], BF16)
    idx8 = s.sb("idx8", [128, NE], F32)
    idxb8 = s.sb("idxb8", [128, NE], F32)
    comb = s.sb("comb", [128, 8 * NE], F32)
    combbf = s.sb("combbf", [128, NE], BF16)
    combT = s.sb("combT", [NE, 8 * 128], BF16)
    gw = {k: s.sb(f"g_{k}", [128, NE], F32)
          for k in ["p", "oh1", "imk", "pw", "eq2", "oh2", "wcl"]}
    gc = {k: s.sb(f"gc_{k}", [128, 1], F32)
          for k in ["m1", "i1", "m2", "i2", "nmx", "rs", "tot", "rt"]}
    ln = LNHelper(s)

    pG = s.ps("pG", [128, NE], F32)
    pT = s.ps("pT", [128, 128], BF16)
    pTc = s.ps("pTc", [128, 128], BF16)
    pE = [s.ps(f"pE{i}", [128, 512], F32) for i in range(4)]

    t_id = s.dma("pool", idb[:], IDENT)
    t_gw = s.dma("pool", gatew[:].rearrange("p (c e) -> p c e", e=NE), GATEW.rearrange("(c p) e -> p c e", p=128))
    t_gb = s.dma("sp", gateb[:], GATEB)
    t_eb = s.dma("pool", expb[:], EXPB)
    t_ix = s.dma("sp", idx8[:], IDX8)
    t_ib = s.dma("sp", idxb8[:], IDXB8)

    buf_tok = {}
    pE_tok = [None] * 4
    pT_tok = None
    pT_toks2 = [None, None]
    pG_tok = None
    t_out = None
    NG = TT // 8
    t_wdma = {}
    t_wdma[0] = s.dma("pool", expw[0][:].rearrange("p (c f) -> p c f", f=D),
                      EXPW[0].rearrange("(c p) f -> p c f", p=128), stream="w")
    for g in range(NG):
        trtoks = []
        gtoks = []
        for ti in range(8):
            tile = g * 8 + ti
            bname = f"y0bf{ti % 2}"
            tl = s.dma("pool", y0bf[ti % 2][:],
                       Y0[tile * 128:(tile + 1) * 128, :],
                       after=(buf_tok.get(bname), t_id), stream="xa")
            tlf = s.dma("sp", y0f[:, ti * D:(ti + 1) * D],
                        Y0[tile * 128:(tile + 1) * 128, :],
                        after=(buf_tok.get("y0f_%d" % ti),), stream="xb")
            last_pe = None
            for c in range(8):
                src = y0bf[ti % 2][:, c * 128:(c + 1) * 128]
                dst = y0T[:, (ti * 8 + c) * 128:(ti * 8 + c + 1) * 128]
                pe_t, cp_t = emit_transpose(s, dst, src, idb,
                                            pT if c % 2 == 0 else pTc,
                                            after=(tl,),
                                            pst_free=pT_toks2[c % 2])
                pT_toks2[c % 2] = cp_t
                last_pe = pe_t
                trtoks.append(cp_t)
            buf_tok[bname] = last_pe
            mm_last = None
            for c in range(8):
                aft = (pG_tok, t_gw) + tuple(trtoks[-8:]) if c == 0 else ()
                mm_last = s.op("pe", lambda e, ti=ti, c=c: e.matmul(
                    pG[:], y0T[:, (ti * 8 + c) * 128:(ti * 8 + c + 1) * 128],
                    gatew[:, c * NE:(c + 1) * NE],
                    start=(c == 0), stop=(c == 7)), after=aft, inc=(c == 7))
            p = gw["p"]

            def dv(fn, after):
                return s.op("dve", fn, after=after)
            t1 = dv(lambda e: e.tensor_tensor(p[:], pG[:], gateb[:], op=ALU.add),
                    (mm_last, t_gb, buf_tok.get("dbgp")))
            t2 = dv(lambda e: e.tensor_scalar(p[:], p[:], 1e-4, None, op0=ALU.max),
                    (t1,))
            pG_tok = t1
            t3 = dv(lambda e: e.reduce_max(gc["nmx"][:], p[:], axis=AX, negate=True),
                    (t2,))
            t4 = s.op("act", lambda e: e.activation(p[:], p[:], AF.Exp,
                                                    bias=gc["nmx"][:],
                                                    accum_out=gc["rs"][:]),
                      after=(t3,))
            t5 = dv(lambda e: e.reciprocal(gc["rt"][:], gc["rs"][:]), (t4,))
            t6 = dv(lambda e: e.tensor_scalar(p[:], p[:], gc["rt"][:], None,
                                              op0=ALU.mult), (t5,))
            t7 = dv(lambda e: e.reduce_max(gc["m1"][:], p[:], axis=AX), (t6,))
            t8 = dv(lambda e: e.tensor_scalar(gw["oh1"][:], p[:], gc["m1"][:],
                                              None, op0=ALU.is_equal), (t7,))
            t9 = dv(lambda e: e.scalar_tensor_tensor(gw["imk"][:], gw["oh1"][:],
                                                     -1e3, idxb8[:],
                                                     op0=ALU.mult, op1=ALU.add),
                    (t8, t_ib))
            t10 = dv(lambda e: e.tensor_reduce(gc["i1"][:], gw["imk"][:], axis=AX,
                                               op=ALU.min), (t9,))
            t11 = dv(lambda e: e.tensor_scalar(gw["oh1"][:], idx8[:], gc["i1"][:],
                                               None, op0=ALU.is_equal), (t10, t_ix))
            t12 = dv(lambda e: e.scalar_tensor_tensor(gw["pw"][:], gw["oh1"][:],
                                                      -1e3, p[:],
                                                      op0=ALU.mult, op1=ALU.add),
                     (t11,))
            t13 = dv(lambda e: e.reduce_max(gc["m2"][:], gw["pw"][:], axis=AX),
                     (t12,))
            t14 = dv(lambda e: e.tensor_scalar(gw["eq2"][:], gw["pw"][:],
                                               gc["m2"][:], None,
                                               op0=ALU.is_equal), (t13,))
            t15 = dv(lambda e: e.scalar_tensor_tensor(gw["imk"][:], gw["eq2"][:],
                                                      -1e3, idxb8[:],
                                                      op0=ALU.mult, op1=ALU.add),
                     (t14,))
            t16 = dv(lambda e: e.tensor_reduce(gc["i2"][:], gw["imk"][:], axis=AX,
                                               op=ALU.min), (t15,))
            t17 = dv(lambda e: e.tensor_scalar(gw["oh2"][:], idx8[:], gc["i2"][:],
                                               None, op0=ALU.is_equal), (t16,))
            t18 = dv(lambda e: e.tensor_tensor(gw["oh1"][:], gw["oh1"][:],
                                               gw["oh2"][:], op=ALU.add), (t17,))
            t19 = dv(lambda e: e.tensor_scalar(gw["wcl"][:], p[:], 0.1, None,
                                               op0=ALU.max), (t6,))
            cslice = comb[:, ti * NE:(ti + 1) * NE]
            t20 = dv(lambda e, cs=cslice: e.tensor_tensor(cs, gw["oh1"][:],
                                                          gw["wcl"][:],
                                                          op=ALU.mult),
                     (t18, t19, buf_tok.get("comb_%d" % ti)))
            t21 = dv(lambda e, cs=cslice: e.reduce_sum(gc["tot"][:], cs, axis=AX),
                     (t20,))
            t22 = dv(lambda e: e.reciprocal(gc["rt"][:], gc["tot"][:]), (t21,))
            t23 = dv(lambda e: e.tensor_scalar(gc["rt"][:], gc["rt"][:], 0.5,
                                               None, op0=ALU.mult), (t22,))
            t24 = dv(lambda e, cs=cslice: e.tensor_scalar(cs, cs, gc["rt"][:],
                                                          None, op0=ALU.mult),
                     (t23,))
            if debug:
                tdbg = s.dma("sp", DBG_P[tile * 128:(tile + 1) * 128, :], p[:],
                             after=(t6, t7))
                buf_tok["dbgp"] = tdbg
                s.dma("sp", DBG_C[tile * 128:(tile + 1) * 128, :], cslice, after=(t24,))
            t25 = dv(lambda e, cs=cslice: e.tensor_copy(combbf[:], cs),
                     (t24, buf_tok.get("combbf")))
            tr = s.op("pe", lambda e: e.transpose(pT[0:NE, 0:128], combbf[:],
                                                  idb[:]),
                      after=(t25, pT_tok, pT_toks2[0]))
            buf_tok["combbf"] = tr
            t26 = s.op("act", lambda e, ti=ti: e.activation(
                combT[:, ti * 128:(ti + 1) * 128], pT[0:NE, 0:128], AF.Copy),
                after=(tr,))
            pT_tok = t26
            pT_toks2[0] = t26
            gtoks.append((t24, t26, tlf))
        init_toks = []
        for ti in range(8):
            (t24, t26, tlf) = gtoks[ti]
            for hhalf in range(2):
                pb = pE[hhalf]
                mm = s.op("pe", lambda e, ti=ti, hh=hhalf, pb=pb: e.matmul(
                    pb[:], combT[:, ti * 128:(ti + 1) * 128],
                    expb[:, hh * 512:(hh + 1) * 512],
                    start=True, stop=True),
                    after=(t26, t_eb, pE_tok[hhalf]), inc=True)
                t_in = s.op("dve", lambda e, ti=ti, hh=hhalf, pb=pb:
                            e.tensor_tensor(acc[:, ti * D + hh * 512:
                                                ti * D + (hh + 1) * 512],
                                            pb[:],
                                            y0f[:, ti * D + hh * 512:
                                                ti * D + (hh + 1) * 512],
                                            op=ALU.add),
                            after=(mm, tlf, buf_tok.get("acc_%d" % ti)))
                pE_tok[hhalf] = t_in
                init_toks.append(t_in)
        if debug:
            for ti in range(8):
                tile = g * 8 + ti
                tdi = s.dma("sp", DBG_I[tile * 128:(tile + 1) * 128, :],
                            acc[:, ti * D:(ti + 1) * D],
                            after=tuple(init_toks[ti * 2:ti * 2 + 2]))
                init_toks[ti * 2 + 1] = tdi
        for e_i in range(NE):
            t_w = t_wdma[g * NE + e_i]
            mm_final = None
            for ti in range(8):
                evs = []
                for hh in range(2):
                    pb = pE[2 + hh]
                    mm_last = None
                    for c in range(8):
                        aft = (pE_tok[2 + hh], t_w) + tuple(trtoks) if c == 0 else ()
                        mm_last = s.op("pe", lambda e, ti=ti, c=c, hh=hh, pb=pb, w=expw[e_i % 2]:
                                       e.matmul(pb[:],
                                                y0T[:, (ti * 8 + c) * 128:(ti * 8 + c + 1) * 128],
                                                w[:, c * D + hh * 512:c * D + (hh + 1) * 512],
                                                start=(c == 0), stop=(c == 7)),
                                       after=aft, inc=(c == 7))
                    t_ev = s.op("dve", lambda e, ti=ti, hh=hh, pb=pb, e_i=e_i:
                                e.scalar_tensor_tensor(
                                    acc[:, ti * D + hh * 512:ti * D + (hh + 1) * 512],
                                    pb[:], comb[:, ti * NE + e_i:ti * NE + e_i + 1],
                                    acc[:, ti * D + hh * 512:ti * D + (hh + 1) * 512],
                                    op0=ALU.mult, op1=ALU.add),
                                after=(mm_last, gtoks[ti][0]) + tuple(init_toks[ti*2:ti*2+2]))
                    pE_tok[2 + hh] = t_ev
                    evs.append(t_ev)
                    mm_final = mm_last
                if e_i == NE - 1:
                    buf_tok[f"done_{ti}"] = tuple(evs)
            buf_tok[f"expw{e_i % 2}"] = mm_final
            nxt = g * NE + e_i + 1
            if nxt < NG * NE:
                t_wdma[nxt] = s.dma("pool",
                                    expw[nxt % 2][:].rearrange("p (c f) -> p c f", f=D),
                                    EXPW[nxt % NE].rearrange("(c p) f -> p c f", p=128),
                                    after=(buf_tok.get(f"expw{nxt % 2}"),),
                                    stream="w")
        for ti in range(8):
            tile = g * 8 + ti
            evs = buf_tok[f"done_{ti}"]
            if debug:
                s.dma("sp", DBG_A[tile * 128:(tile + 1) * 128, :],
                      acc[:, ti * D:(ti + 1) * D], after=tuple(evs))
            t_ln = ln.emit(acc[:, ti * D:(ti + 1) * D], acc[:, ti * D:(ti + 1) * D],
                           tuple(evs))
            t_out = s.dma("sp", OUT[tile * 128:(tile + 1) * 128, :],
                          acc[:, ti * D:(ti + 1) * D], after=(t_ln,), stream="out")
            buf_tok["acc_%d" % ti] = t_out
            buf_tok["y0f_%d" % ti] = init_toks[ti * 2 + 1]
            buf_tok["comb_%d" % ti] = buf_tok[f"done_{ti}"][-1]
    s.barrier("sp", (t_out,))


def build_L0f(debug=False):
    nc = bass.Bass("TRN2", target_bir_lowering=False)
    NTOK = TT * 128
    NE = 8
    Y0 = nc.dram_tensor("Y0", [NTOK, D], F32, kind="ExternalInput").ap()
    GATEW = nc.dram_tensor("GATEW", [D, NE], F32, kind="ExternalInput").ap()
    GATEB = nc.dram_tensor("GATEB", [128, NE], F32, kind="ExternalInput").ap()
    EXPW = nc.dram_tensor("EXPW", [NE, D, D], F32, kind="ExternalInput").ap()
    EXPB = nc.dram_tensor("EXPB", [NE, D], F32, kind="ExternalInput").ap()
    IDENT = nc.dram_tensor("IDENT", [128, 128], F32, kind="ExternalInput").ap()
    IDX8 = nc.dram_tensor("IDX8", [128, NE], F32, kind="ExternalInput").ap()
    IDXB8 = nc.dram_tensor("IDXB8", [128, NE], F32, kind="ExternalInput").ap()
    OUT = nc.dram_tensor("OUT", [NTOK, D], F32, kind="ExternalOutput").ap()

    s = Sched(nc)
    emit_L0f(nc, s, Y0, GATEW, GATEB, EXPW, EXPB, IDENT, IDX8, IDXB8, OUT,
             debug=debug)
    s.emit()
    return nc


def prep_L0f(core_y0, gate_W, gate_b, expert_W, expert_b):
    return dict(
        Y0=np.ascontiguousarray(core_y0.reshape(-1, D).astype(np.float32)),
        GATEW=np.ascontiguousarray(gate_W.astype(np.float32)),
        GATEB=np.ascontiguousarray(np.broadcast_to(gate_b.astype(np.float32), (128, 8))),
        EXPW=np.ascontiguousarray(expert_W.astype(np.float32)),
        EXPB=np.ascontiguousarray(expert_b.astype(np.float32)),
        IDENT=np.eye(128, dtype=np.float32),
        IDX8=np.broadcast_to(np.arange(8, dtype=np.float32), (128, 8)).copy(),
        IDXB8=(np.broadcast_to(np.arange(8, dtype=np.float32), (128, 8)) + 1e3).copy(),
    )



def emit_L1a(nc, s, Y1, WQKVT, WOUTT, ABD, NEGTRI, NEGTRIL, IDENT, ONESR,
             SEL16, RALLH, Y2):
    NH = 16
    HD = 64
    y1bf = [s.sb(f"y1bf{i}", [128, D], BF16) for i in range(2)]
    y1T = s.sb("y1T", [128, 8 * S], BF16)
    wqkvT = s.sb("wqkvT", [128, 8 * 3 * D], BF16)
    woutT = s.sb("woutT", [128, 8 * D], BF16)
    qT2 = s.sb("qT2", [128, S], BF16)
    kT2 = s.sb("kT2", [128, S], BF16)
    p_bf = s.sb("p_bf", [128, S], BF16)
    v_sb = s.sb("v_sb", [128, NT * 130], BF16)
    ptb = s.sb("ptb", [128, 16 * 256], BF16)
    msml = s.sb("msml", [1, 4], F32)
    b11 = s.sb("b11", [1, 1], F32)
    negBt = [s.sb(f"negBt{i}", [128, 1], F32) for i in range(2)]
    negtril = s.sb("negtril", [128, 128], F32)
    negfull = s.sb("negfull", [128, 128], F32)
    onescolb = s.sb("onescolb", [128, 1], BF16)
    ones11c = s.sb("ones11c", [1, 1], F32)
    attnT = s.sb("attnT", [128, 8 * S], BF16)
    foutT = s.sb("foutT", [128, 8 * 512], BF16)
    qsq = foutT
    res = [s.sb(f"res{i}", [128, D], F32) for i in range(2)]
    Rsb2 = s.sb("Rsb2", [65, S], F32)
    Rall = s.sb("Rall", [128, 256], F32)
    rs_row0 = s.sb("rsrow0", [1, S], F32)
    rs_row_h = [rs_row0, rs_row0]
    nrow = rs_row0
    strow = s.sb("strow", [1, 128], F32)
    st16 = s.sb("st16", [1, 16], F32)
    st16c = s.sb("st16c", [16, 1], F32)
    gmax = s.sb("gmax", [128, 1], F32)
    sel16 = s.sb("sel16", [16, 128], F32)
    idb = s.sb("idb", [128, 128], BF16)
    idf = s.sb("idf", [128, 128], F32)
    onesr = s.sb("onesr", [1, 128], F32)
    negtri = s.sb("negtri", [128, 128], F32)
    abd = s.sb("abd", [1, 3], F32)
    gcols = {k: s.sb(f"ga_{k}", [128, 1], F32)
             for k in ["a", "b", "dt", "dtau", "den", "stim", "five"]}
    ga = {k: s.sb(f"g16_{k}", [128, 256], F32)
          for k in ["rcpR", "E", "t1", "t2", "I", "v1", "w1", "rs"]}
    t11 = s.sb("t11a", [1, 1], F32)
    ln = LNHelper(s)

    pS = s.ps("pS", [128, 1024], F32)
    pST = [s.ps(f"pST{i}", [128, 256], F32) for i in range(2)]
    pT = s.ps("pT", [128, 128], BF16)
    pT2 = s.ps("pT2", [128, 128], BF16)
    pV = s.ps("pV", [65, 256], F32)
    pB = s.ps("pB", [128, 512], F32)

    t_id = s.dma("pool", idb[:], IDENT)
    t_idf = s.dma("sp", idf[:], IDENT)
    t_onesr = s.dma("sp", onesr[:], ONESR)
    t_tri = s.dma("sp", negtri[:], NEGTRI)
    t_tril = s.dma("sp", negtril[:], NEGTRIL)
    t_sel = s.dma("sp", sel16[:], SEL16)
    t_mfull = s.op("pool", lambda e: e.memset(negfull[:], -1e30))
    t_ocb = s.op("pool", lambda e: e.memset(onescolb[:], 1.0))
    t_o11c = s.op("pool", lambda e: e.memset(ones11c[:], 1.0))
    t_abd = s.dma("sp", abd[:], ABD)
    t_five = s.op("pool", lambda e: e.memset(gcols["five"][:], 5.0))
    vones = []
    for sc in range(NT):
        vones.append(s.op("pool", lambda e, sc=sc: e.memset(
            v_sb[:, sc * 130 + 64:sc * 130 + 65], 1.0)))
        vones.append(s.op("pool", lambda e, sc=sc: e.memset(
            v_sb[:, sc * 130 + 129:sc * 130 + 130], 1.0)))
    t_vones = tuple(vones)
    t_wq = s.dma("pool", wqkvT[:].rearrange("p (c f) -> p c f", f=3 * D),
                 WQKVT.rearrange("(c p) f -> p c f", p=128))
    t_wo = s.dma("pool", woutT[:].rearrange("p (c f) -> p c f", f=D),
                 WOUTT.rearrange("(c p) f -> p c f", p=128))

    col_toks = {}
    for i, k in enumerate(["a", "b", "dt"]):
        tmm = s.op("pe", lambda e, i=i: e.matmul(pB[0:128, 0:1],
                                                 onesr[:], abd[0:1, i:i + 1],
                                                 start=True, stop=True),
                   after=(t_onesr, t_abd, col_toks.get("prev")))
        tcp = s.op("dve", lambda e, k=k: e.tensor_copy(gcols[k][:], pB[0:128, 0:1]),
                   after=(tmm,))
        col_toks["prev"] = tcp
        col_toks[k] = tcp
    t_dtau = s.op("dve", lambda e: e.tensor_scalar(gcols["dtau"][:], gcols["dt"][:],
                                                   1.0 / 12.5, None, op0=ALU.mult),
                  after=(col_toks["dt"],))
    t_d1 = s.op("dve", lambda e: e.tensor_tensor(gcols["den"][:], gcols["dtau"][:],
                                                 gcols["b"][:], op=ALU.mult),
                after=(t_dtau, col_toks["b"]))
    t_d2 = s.op("dve", lambda e: e.tensor_scalar(gcols["den"][:], gcols["den"][:],
                                                 1.0, None, op0=ALU.add),
                after=(t_d1,))
    t_den = s.op("dve", lambda e: e.reciprocal(gcols["den"][:], gcols["den"][:]),
                 after=(t_d2,))

    buf = {}
    pT_tok = None
    pV_tok = None
    pB_tok = col_toks["prev"]
    t_out = None
    for sq in range(NSEQ):
        trtoks = []
        for ti in range(NT):
            tile = sq * NT + ti
            bname = f"y1bf{ti % 2}"
            tl = s.dma("pool", y1bf[ti % 2][:],
                       Y1[tile * 128:(tile + 1) * 128, :],
                       after=(buf.get(bname), t_id), stream="xa")
            last_pe = None
            for c in range(8):
                src = y1bf[ti % 2][:, c * 128:(c + 1) * 128]
                dst = y1T[:, c * S + ti * 128: c * S + ti * 128 + 128]
                sl = c % 2
                pe_t, cp_t = emit_transpose(s, dst, src, idb,
                                            pT if sl == 0 else pT2,
                                            after=(tl, buf.get("y1T_cons")),
                                            pst_free=buf.get(f"vTw{sl}"))
                buf[f"vTw{sl}"] = cp_t
                last_pe = pe_t
                trtoks.append(cp_t)
            buf[bname] = last_pe
        attn_w = {}
        for hp in range(8):
            qkv_last_mm = None
            for (name, off) in [("q", 128 * hp), ("k", D + 128 * hp),
                                ("v", 2 * D + 128 * hp)]:
                dstT = {"q": qT2, "k": kT2, "v": p_bf}[name]
                warkey = {"q": "q2_cons", "k": "k2_cons",
                          "v": "p_bf_cons"}[name]
                qscale = 0.125 if name == "q" else 1.0
                for n4 in range(4):
                    half = n4 % 2
                    mm_last = None
                    for c in range(8):
                        aft = ()
                        if c == 0:
                            aft = (buf.get(f"pSh{half}"), t_wq,
                                   buf.get(warkey)) + tuple(trtoks)
                        mm_last = s.op("pe", lambda e, c=c, half=half, off=off,
                                       n4=n4: e.matmul(
                                       pS[0:128, half * 512:(half + 1) * 512],
                                       wqkvT[:, c * 3 * D + off:
                                             c * 3 * D + off + 128],
                                       y1T[:, c * S + n4 * 512:
                                           c * S + (n4 + 1) * 512],
                                       start=(c == 0), stop=(c == 7)),
                                       after=aft, inc=(c == 7))
                    qkv_last_mm = mm_last
                    cp = s.op("act", lambda e, d=dstT, n4=n4, half=half,
                              qs=qscale: e.activation(
                                  d[:, n4 * 512:(n4 + 1) * 512],
                                  pS[0:128, half * 512:(half + 1) * 512],
                                  AF.Copy, scale=qs),
                              after=(mm_last, buf.get(f"{name}2_w")))
                    buf[f"pSh{half}"] = cp
                    buf.setdefault(f"{name}2_done", [])
                    if n4 == 0:
                        buf[f"{name}2_done"] = []
                    buf[f"{name}2_done"].append(cp)
                buf[f"{name}2_w"] = buf[f"{name}2_done"][-1]
            buf["y1T_cons"] = qkv_last_mm
            q_done = tuple(buf["q2_done"])
            k_done = tuple(buf["k2_done"])
            v_done_cp = tuple(buf["v2_done"])
            vtoks = []
            for sc in range(NT):
                slot = sc % 2
                psl = pT if slot == 0 else pT2
                pe_t = s.op("pe", lambda e, sc=sc, psl=psl: e.transpose(
                    psl[:], p_bf[:, sc * 128:(sc + 1) * 128], idb[:]),
                    after=(v_done_cp[sc // 4], buf.get(f"vTw{slot}"),
                           buf.get("v_sb_cons")))
                c1 = s.op("act", lambda e, sc=sc, psl=psl: e.activation(
                    v_sb[:, sc * 130:sc * 130 + 64], psl[:, 0:64], AF.Copy),
                    after=(pe_t,) + (t_vones if sc == 0 else ()))
                c2 = s.op("act", lambda e, sc=sc, psl=psl: e.activation(
                    v_sb[:, sc * 130 + 65:sc * 130 + 129], psl[:, 64:128],
                    AF.Copy), after=(c1,))
                buf[f"vTw{slot}"] = c2
                vtoks.append(c2)
            v_ready = tuple(vtoks)
            buf["p_bf_cons"] = vtoks[-1]
            mx_toks = []
            for (mat, base) in [(qT2, 0), (kT2, 2)]:
                t_sq = s.op("dve", lambda e, mat=mat: e.tensor_tensor(
                    qsq[:, 0:S], mat[:], mat[:], op=ALU.mult),
                    after=(q_done if base == 0 else k_done) +
                          (buf.get("qsq_cons"),))
                for idx in range(2):
                    hh = 64 * idx
                    cps = []
                    for j in range(4):
                        t_mm = s.op("pe", lambda e, hh=hh, j=j: e.matmul(
                            pB[0:1, 0:512], onescolb[hh:hh + 64, 0:1],
                            qsq[hh:hh + 64, j * 512:(j + 1) * 512],
                            start=True, stop=True),
                            after=(t_sq, t_ocb, pB_tok))
                        t_cp = s.op("act", lambda e, j=j: e.activation(
                            nrow[0:1, j * 512:(j + 1) * 512], pB[0:1, 0:512],
                            AF.Copy), after=(t_mm, buf.get("nrow")))
                        pB_tok = t_cp
                        cps.append(t_cp)
                    t_mx = s.op("dve", lambda e, col=base + idx: e.reduce_max(
                        msml[0:1, col:col + 1], nrow[0:1, 0:S], axis=AX),
                        after=tuple(cps) + (buf.get("msml"),))
                    buf["nrow"] = t_mx
                    mx_toks.append(t_mx)
                    if base == 0 and idx == 1:
                        buf["qsq_cons"] = t_mx
            buf["msml"] = mx_toks[-1]
            buf["qsq_cons"] = mx_toks[-1]
            ng_toks = []
            for idx in range(2):
                t_b2 = s.op("dve", lambda e, idx=idx: e.tensor_tensor(
                    b11[:], msml[0:1, idx:idx + 1], msml[0:1, 2 + idx:3 + idx],
                    op=ALU.mult), after=(mx_toks[idx], mx_toks[2 + idx],
                                         buf.get("b11")))
                buf["msml"] = t_b2
                t_ln = s.op("act", lambda e: e.activation(b11[:], b11[:], AF.Ln),
                            after=(t_b2,))
                t_ex = s.op("act", lambda e: e.activation(b11[:], b11[:], AF.Exp,
                                                          scale=0.5),
                            after=(t_ln,))
                t_bc = s.op("pe", lambda e: e.matmul(pB[0:128, 0:1], onesr[:],
                                                     b11[:], start=True,
                                                     stop=True),
                            after=(t_ex, pB_tok))
                t_ng = s.op("dve", lambda e, idx=idx: e.tensor_scalar(
                    negBt[idx][:], pB[0:128, 0:1], -1.05, None, op0=ALU.mult),
                    after=(t_bc, buf.get(f"negB{idx}")))
                pB_tok = t_ng
                buf["b11"] = t_ng
                ng_toks.append(t_ng)
            last_att_mm = None
            for idx in range(2):
                hx = 2 * hp + idx
                hh = 64 * idx
                att_toks = []
                for qp in range(8):
                    nch = 2 * (qp + 1)
                    exp_toks = []
                    for c in range(nch):
                        slot = c % 2
                        aft = (buf.get(f"pSTw{slot}"),)
                        if c == 0 and qp == 0:
                            aft = aft + q_done + k_done
                        mmst = s.op("pe", lambda e, c=c, qp=qp, slot=slot,
                                    hh=hh: e.matmul(
                            pST[slot][:], kT2[hh:hh + 64, c * 128:(c + 1) * 128],
                            qT2[hh:hh + 64, qp * 256:(qp + 1) * 256],
                            start=True, stop=True), after=aft)
                        last_att_mm = mmst
                        t_f = mmst
                        for h2 in range(2):
                            qt = 2 * qp + h2
                            if c == qt:
                                t_f = s.op("dve", lambda e, slot=slot, h2=h2,
                                           t_f=t_f: e.tensor_tensor(
                                    pST[slot][:, h2 * 128:(h2 + 1) * 128],
                                    pST[slot][:, h2 * 128:(h2 + 1) * 128],
                                    negtril[:], op=ALU.add),
                                    after=(t_f, t_tril))
                            elif c > qt:
                                t_f = s.op("dve", lambda e, slot=slot, h2=h2,
                                           t_f=t_f: e.tensor_tensor(
                                    pST[slot][:, h2 * 128:(h2 + 1) * 128],
                                    pST[slot][:, h2 * 128:(h2 + 1) * 128],
                                    negfull[:], op=ALU.add),
                                    after=(t_f, t_mfull))
                        t_e = s.op("act", lambda e, c=c, slot=slot, idx=idx:
                                   e.activation(ptb[:, c * 256:(c + 1) * 256],
                                                pST[slot][:], AF.Exp,
                                                bias=negBt[idx][:]),
                                   after=(t_f, ng_toks[idx], buf.get("ptb")))
                        buf[f"pSTw{slot}"] = t_e
                        exp_toks.append(t_e)
                    mm_pv = None
                    for c in range(nch):
                        aft = (exp_toks[c],)
                        if c == 0:
                            aft = aft + (pV_tok,) + v_ready
                        mm_pv = s.op("pe", lambda e, c=c, nch=nch, idx=idx:
                                     e.matmul(
                            pV[:], v_sb[:, c * 130 + 65 * idx:
                                        c * 130 + 65 * idx + 65],
                            ptb[:, c * 256:(c + 1) * 256],
                            start=(c == 0), stop=(c == nch - 1)),
                            after=aft, inc=(c == nch - 1))
                        last_att_mm = mm_pv
                    buf["ptb"] = mm_pv
                    cp1 = s.op("dve", lambda e, idx=idx, hp=hp, qp=qp:
                               e.tensor_copy(
                        attnT[64 * idx:64 * idx + 64,
                              S * hp + qp * 256:S * hp + (qp + 1) * 256],
                        pV[0:64, 0:256]),
                        after=(mm_pv, buf.get("attnT_war")))
                    cp2 = s.op("dve", lambda e, qp=qp: e.tensor_copy(
                        Rsb2[64:65, qp * 256:(qp + 1) * 256], pV[64:65, 0:256]),
                        after=(mm_pv, buf.get("rsb")))
                    pV_tok = cp2
                    att_toks.append(cp1)
                buf[f"negB{idx}"] = exp_toks[-1]
                rdma = s.dma("sp", RALLH[hx * S:(hx + 1) * S].rearrange(
                    "(a b) -> a b", a=1), Rsb2[64:65, 0:S],
                    after=(pV_tok,) + tuple(att_toks), stream="rb")
                buf["rsb"] = rdma
                attn_w[hx] = tuple(att_toks)
            buf["q2_cons"] = last_att_mm
            buf["k2_cons"] = last_att_mm
            buf["v_sb_cons"] = last_att_mm
        t_r1 = s.dma("sp", Rall[:], RALLH.rearrange("(p c) -> p c", p=128),
                     after=(buf.get("gate_done"),), stream="rb")

        def dv(fn, after=()):
            return s.op("dve", fn, after=after)
        g0 = dv(lambda e: e.reciprocal(ga["rcpR"][:], Rall[:]), (t_r1,))
        g1 = dv(lambda e: e.tensor_tensor(ga["E"][:], Rall[:], ga["rcpR"][:],
                                          op=ALU.mult), (g0,))
        m1 = dv(lambda e: e.reduce_max(gmax[:], ga["E"][:], axis=AX,
                                       apply_absolute_value=True), (g1,))
        m2 = s.op("pe", lambda e: e.matmul(pB[0:1, 0:128], gmax[:], idf[:],
                                           start=True, stop=True),
                  after=(m1, t_idf, pB_tok))
        m3 = s.op("act", lambda e: e.activation(strow[:], pB[0:1, 0:128],
                                                AF.Copy),
                  after=(m2, buf.get("strow")))
        pB_tok = m3
        buf["strow"] = m3
        h_toks = []
        for h in range(16):
            t_rh = dv(lambda e, h=h: e.tensor_reduce(
                st16[0:1, h:h + 1], strow[0:1, h * 8:(h + 1) * 8], axis=AX,
                op=ALU.max), (m3, buf.get("st16")))
            h_toks.append(t_rh)
        buf["st16"] = h_toks[-1]
        m4 = s.op("pe", lambda e: e.matmul(pB[0:16, 0:1], st16[0:1, 0:16],
                                           ones11c[:], start=True, stop=True),
                  after=tuple(h_toks) + (t_o11c, pB_tok))
        m5 = dv(lambda e: e.tensor_copy(st16c[:], pB[0:16, 0:1]),
                (m4, buf.get("st16c")))
        buf["st16c"] = m5
        m6 = s.op("pe", lambda e: e.matmul(pB[0:128, 1:2], sel16[:], st16c[:],
                                           start=True, stop=True),
                  after=(m5, t_sel, pB_tok))
        m7 = dv(lambda e: e.tensor_scalar(gcols["stim"][:], pB[0:128, 1:2],
                                          1e-6, None, op0=ALU.max),
                (m6, buf.get("stimc")))
        pB_tok = m7
        m8 = dv(lambda e: e.reciprocal(gcols["stim"][:], gcols["stim"][:]), (m7,))
        buf["stimc"] = m8
        g9 = s.op("act", lambda e: e.activation(ga["t1"][:], ga["E"][:], AF.Abs),
                  after=(g1,))
        g10 = s.op("act", lambda e: e.activation(ga["t2"][:], ga["t1"][:],
                                                 AF.Exp, scale=-10.0,
                                                 bias=gcols["five"][:]),
                   after=(g9, t_five))
        g11 = dv(lambda e: e.tensor_scalar(ga["t1"][:], ga["t2"][:], 1.0, None,
                                           op0=ALU.add), (g10,))
        g13 = dv(lambda e: e.reciprocal(ga["t2"][:], ga["t1"][:]), (g11,))
        g14 = dv(lambda e: e.tensor_scalar(ga["I"][:], ga["t2"][:], 0.9, 0.1,
                                           op0=ALU.mult, op1=ALU.add), (g13,))
        g15a = dv(lambda e: e.tensor_tensor(ga["I"][:], ga["I"][:], ga["E"][:],
                                            op=ALU.mult), (g14, g1))
        g15 = dv(lambda e: e.tensor_scalar(ga["I"][:], ga["I"][:],
                                           gcols["stim"][:], None,
                                           op0=ALU.mult), (g15a, m8))
        g16 = dv(lambda e: e.tensor_scalar(ga["v1"][:], ga["I"][:],
                                           gcols["dt"][:], None, op0=ALU.mult),
                 (g15, col_toks["dt"]))
        g17 = dv(lambda e: e.tensor_scalar(ga["w1"][:], ga["v1"][:],
                                           gcols["a"][:], None, op0=ALU.add),
                 (g16, col_toks["a"]))
        g18 = dv(lambda e: e.tensor_scalar(ga["w1"][:], ga["w1"][:],
                                           gcols["dtau"][:], None, op0=ALU.mult),
                 (g17, t_dtau))
        g19 = dv(lambda e: e.tensor_scalar(ga["w1"][:], ga["w1"][:],
                                           gcols["den"][:], None, op0=ALU.mult),
                 (g18, t_den))
        g20 = dv(lambda e: e.tensor_tensor(ga["t1"][:], ga["v1"][:], ga["v1"][:],
                                           op=ALU.mult), (g16,))
        g21 = dv(lambda e: e.tensor_tensor(ga["t1"][:], ga["t1"][:], ga["v1"][:],
                                           op=ALU.mult), (g20,))
        g22 = dv(lambda e: e.tensor_scalar(ga["t1"][:], ga["t1"][:], 1.0 / 3.0,
                                           None, op0=ALU.mult), (g21,))
        g23 = dv(lambda e: e.tensor_tensor(ga["t2"][:], ga["v1"][:], ga["t1"][:],
                                           op=ALU.subtract), (g22,))
        g24 = dv(lambda e: e.tensor_tensor(ga["t2"][:], ga["t2"][:], ga["w1"][:],
                                           op=ALU.subtract), (g23, g19))
        g25 = dv(lambda e: e.tensor_tensor(ga["t2"][:], ga["t2"][:], ga["I"][:],
                                           op=ALU.add), (g24,))
        g26 = dv(lambda e: e.scalar_tensor_tensor(ga["t2"][:], ga["t2"][:],
                                                  gcols["dt"][:], ga["v1"][:],
                                                  op0=ALU.mult, op1=ALU.add),
                 (g25,))
        g27 = s.op("act", lambda e: e.activation(ga["t1"][:], ga["t2"][:],
                                                 AF.Exp, scale=-1.0), (g26,))
        g28 = dv(lambda e: e.tensor_scalar(ga["t1"][:], ga["t1"][:], 1.0, None,
                                           op0=ALU.add), (g27,))
        g29 = dv(lambda e: e.reciprocal(ga["t2"][:], ga["t1"][:]), (g28,))
        g30 = dv(lambda e: e.tensor_scalar(ga["t2"][:], ga["t2"][:], 0.5, 0.5,
                                           op0=ALU.mult, op1=ALU.add), (g29,))
        g31 = dv(lambda e: e.tensor_tensor(ga["t1"][:], ga["t2"][:], ga["E"][:],
                                           op=ALU.mult), (g30,))
        g32 = dv(lambda e: e.tensor_scalar(ga["t1"][:], ga["t1"][:], 1e-8, None,
                                           op0=ALU.add), (g31,))
        g32b = dv(lambda e: e.reciprocal(ga["t1"][:], ga["t1"][:]), (g32,))
        g33 = dv(lambda e: e.tensor_tensor(ga["rs"][:], ga["t2"][:], ga["t1"][:],
                                           op=ALU.mult), (g32b,))
        g34 = dv(lambda e: e.tensor_tensor(ga["rs"][:], ga["rs"][:],
                                           ga["rcpR"][:], op=ALU.mult), (g33, g0))
        t_w2 = s.dma("sp", RALLH.rearrange("(p c) -> p c", p=128), ga["rs"][:],
                     after=(g34,), stream="rb")
        sc_toks = []
        row_war = [None, None]
        for hx in range(NH):
            slot = 0
            t_rr = s.dma("sp", rs_row_h[slot][:],
                         RALLH[hx * S:(hx + 1) * S].rearrange("(a b) -> a b",
                                                              a=1),
                         after=(row_war[slot],), stream="rb")
            for c4 in range(4):
                t_b = s.op("pe", lambda e, hx=hx, c4=c4, slot=slot: e.matmul(
                    pB[0:64, 0:512], onesr[0:1, 0:64],
                    rs_row_h[slot][0:1, c4 * 512:(c4 + 1) * 512],
                    start=True, stop=True),
                    after=(t_rr, pB_tok, t_onesr))
                t_s = s.op("dve", lambda e, hx=hx, c4=c4: e.tensor_tensor(
                    attnT[64 * (hx % 2):64 * (hx % 2) + 64,
                          S * (hx // 2) + c4 * 512:S * (hx // 2) + (c4 + 1) * 512],
                    attnT[64 * (hx % 2):64 * (hx % 2) + 64,
                          S * (hx // 2) + c4 * 512:S * (hx // 2) + (c4 + 1) * 512],
                    pB[0:64, 0:512], op=ALU.mult),
                    after=(t_b,) + attn_w[hx])
                pB_tok = t_s
                row_war[slot] = t_b
                sc_toks.append(t_s)
        buf["gate_done"] = sc_toks[-1]
        for w4 in range(4):
            otoks = []
            for oc in range(8):
                mm_last = None
                for c in range(8):
                    aft = ((tuple(sc_toks) + (buf.get(f"pSh{oc % 2}"), t_wo))
                           if c == 0 else ())
                    mm_last = s.op("pe", lambda e, oc=oc, c=c, w4=w4: e.matmul(
                        pS[:, oc % 2 * 512:(oc % 2 + 1) * 512],
                        woutT[:, c * D + oc * 128:c * D + (oc + 1) * 128],
                        attnT[:, c * S + w4 * 512:c * S + (w4 + 1) * 512],
                        start=(c == 0), stop=(c == 7)),
                        after=aft, inc=(c == 7))
                t_ev = s.op("act", lambda e, oc=oc: e.activation(
                    foutT[:, oc * 512:(oc + 1) * 512],
                    pS[:, oc % 2 * 512:(oc % 2 + 1) * 512], AF.Copy),
                    after=(mm_last, buf.get(f"foutT{oc}")))
                buf[f"pSh{oc % 2}"] = t_ev
                otoks.append(t_ev)
                buf[f"foutT{oc}"] = t_ev
            for ti4 in range(4):
                tile = sq * NT + w4 * 4 + ti4
                rname = f"res{ti4 % 2}"
                tlf = s.dma("sp", res[ti4 % 2][:],
                            Y1[tile * 128:(tile + 1) * 128, :],
                            after=(buf.get(rname),), stream="xb")
                add_toks = []
                for oc in range(8):
                    src = foutT[:, oc * 512 + ti4 * 128: oc * 512 + ti4 * 128 + 128]
                    sl = oc % 2
                    psl = pT if sl == 0 else pT2
                    t1 = s.op("pe", lambda e, src=src, psl=psl: e.transpose(
                        psl[:], src, idb[:]),
                              after=tuple(otoks) + (buf.get(f"vTw{sl}"), tlf))
                    t2 = s.op("dve", lambda e, oc=oc, b=res[ti4 % 2], psl=psl:
                              e.tensor_tensor(b[:, oc * 128:(oc + 1) * 128],
                                              psl[:],
                                              b[:, oc * 128:(oc + 1) * 128],
                                              op=ALU.add), after=(t1, tlf))
                    buf[f"vTw{sl}"] = t2
                    add_toks.append(t2)
                t_ln = ln.emit(res[ti4 % 2][:], res[ti4 % 2][:], tuple(add_toks))
                t_out = s.dma("sp", Y2[tile * 128:(tile + 1) * 128, :],
                              res[ti4 % 2][:], after=(t_ln,), stream="out")
                buf[rname] = t_out
            buf["attnT_war"] = mm_last
    s.barrier("sp", (t_out,))



def build_L1a():
    nc = bass.Bass("TRN2", target_bir_lowering=False)
    NTOK = TT * 128
    NH = 16
    HD = 64
    Y1 = nc.dram_tensor("Y1", [NTOK, D], F32, kind="ExternalInput").ap()
    WQKVT = nc.dram_tensor("WQKVT", [D, 3 * D], F32, kind="ExternalInput").ap()
    WOUTT = nc.dram_tensor("WOUTT", [D, D], F32, kind="ExternalInput").ap()
    ABD = nc.dram_tensor("ABD", [1, 3], F32, kind="ExternalInput").ap()
    NEGTRI = nc.dram_tensor("NEGTRI", [128, 128], F32, kind="ExternalInput").ap()
    NEGTRIL = nc.dram_tensor("NEGTRIL", [128, 128], F32, kind="ExternalInput").ap()
    IDENT = nc.dram_tensor("IDENT", [128, 128], F32, kind="ExternalInput").ap()
    ONESR = nc.dram_tensor("ONESR", [1, 128], F32, kind="ExternalInput").ap()
    SEL16 = nc.dram_tensor("SEL16", [16, 128], F32, kind="ExternalInput").ap()
    RALLH = nc.dram_tensor("RALLH", [NH * S], F32, kind="Internal").ap()
    Y2 = nc.dram_tensor("Y2", [NTOK, D], F32, kind="ExternalOutput").ap()

    s = Sched(nc)
    emit_L1a(nc, s, Y1, WQKVT, WOUTT, ABD, NEGTRI, NEGTRIL, IDENT, ONESR,
             SEL16, RALLH, Y2)
    s.emit()
    return nc


def prep_L1a(core_y1, Wqkv, Wout, a, b, dt):
    negtri = np.where(np.arange(128)[None, :] > np.arange(128)[:, None],
                      np.float32(-1e30), np.float32(0.0)).astype(np.float32)
    return dict(
        Y1=np.ascontiguousarray(core_y1.reshape(-1, D).astype(np.float32)),
        WQKVT=np.ascontiguousarray(Wqkv.T.astype(np.float32)),
        WOUTT=np.ascontiguousarray(Wout.T.astype(np.float32)),
        ABD=np.array([[a, b, dt]], dtype=np.float32),
        NEGTRI=negtri,
        NEGTRIL=np.ascontiguousarray(negtri.T),
        IDENT=np.eye(128, dtype=np.float32),
        ONESR=np.ones((1, 128), dtype=np.float32),
        SEL16=np.kron(np.eye(16, dtype=np.float32),
                      np.ones((1, 8), dtype=np.float32)),
    )



def emit_L0a(nc, s, X, WIHT, BIH, BHH, WHHT, WMS, BMS, IOTABC, IOTAV, RJP1,
             TRIPOS, ONES11, IDENT, Y0, quant_args=None):
    xbf = s.sb("xbf", [128, NT * D], BF16)
    xbuf = [s.sb(f"xbuf{i}", [128, D], BF16) for i in range(2)]
    xT = s.sb("xT", [128, 8 * 128], BF16)
    wiht = s.sb("wiht", [128, 8 * 128], BF16)
    whht = s.sb("whht", [20, 128], BF16)
    wms = s.sb("wms", [20, 128], BF16)
    bms = s.sb("bms", [128, 1], F32)
    bsum = s.sb("bsum", [128, 1], F32)
    xg_s = s.sb("xg_s", [128, S], F32)
    xg_int = s.sb("xg_int", [128, 2 * S], F32)
    pw = s.sb("pw", [32, 2 * S + 2], BF16)
    msr = s.sb("msr", [128, 2 * S], F32)
    a_vec = s.sb("a_vec", [64, S], F32)
    b_vec = s.sb("b_vec", [64, S], F32)
    b2_vec = s.sb("b2_vec", [64, S], F32)
    sg_vec = s.sb("sg_vec", [64, S], F32)
    mu_vec = s.sb("mu_vec", [64, S], F32)
    a2_vec = s.sb("a2_vec", [64, S], F32)
    sc_tmp = s.sb("sc_tmp", [64, S], F32)
    iotav = s.sb("iotav", [64, S], F32)
    iotabc = s.sb("iotabc", [128, S], F32)
    tripos = s.sb("tripos", [128, 128], F32)
    ones11 = s.sb("ones11", [64, 1], F32)
    idb = s.sb("idb", [128, 128], BF16)
    w_bf = s.sb("w_bf", [128, S], BF16)
    dum_bf = s.sb("dum_bf", [128, S], BF16)
    wt_bf = s.sb("wt_bf", [128, 128], BF16)
    ctx = s.sb("ctx", [128, D], F32)
    xres = [s.sb(f"xres{i}", [128, D], F32) for i in range(2)]
    rjcol = s.sb("rjcol", [128, 1], F32)
    cols = {k: s.sb(f"c_{k}", [128, 1], F32)
            for k in ["negmu", "sig", "s2", "negA", "negA2", "ssq", "rn"]}
    scl2 = s.sb("scl2", [128, 1], F32)
    bsum2 = s.sb("bsum2", [128, 1], F32)
    sgbuf = [s.sb(f"sgbuf_{q}", [32, 1], F32) for q in range(2)]
    v2t = [s.sb(f"v2t_{q}", [32, 1], F32) for q in range(2)]
    cc = [[s.sb(f"c{i}_{q}", [32, 1], F32) for i in range(2)] for q in range(2)]
    tc = [s.sb(f"tc_{q}", [32, 1], F32) for q in range(2)]
    ln = LNHelper(s)

    pT = s.ps("pT", [128, 128], BF16)
    ps1 = ExitStack()
    pA = ps1.enter_context(nc.psum_tensor("pA", [128, 512], F32))
    g_ps = [[ps1.enter_context(nc.psum_tensor(f"g{i}_{q}", [128, 1], F32))
             for i in range(2)] for q in range(2)]
    sg_ps = [ps1.enter_context(nc.psum_tensor(f"sg_ps{q}", [128, 1], F32))
             for q in range(2)]

    t_id = s.dma("pool", idb[:], IDENT)
    t_wih = s.dma("pool", wiht[:], WIHT.rearrange("(c p) f -> p c f", p=128))
    t_whh = s.dma("pool", whht[:], WHHT)
    t_wms = s.dma("pool", wms[:], WMS)
    t_bms = s.dma("sp", bms[:], BMS)
    t_bih = s.dma("sp", bsum[:], BIH)
    t_bhh = s.dma("sp", xg_s[:, 0:1], BHH)
    t_bsum = s.op("dve", lambda e: e.tensor_tensor(bsum[:], bsum[:], xg_s[:, 0:1],
                                                   op=ALU.add), after=(t_bih, t_bhh))
    t_scl_a = s.op("pool", lambda e: e.memset(scl2[0:96, :], 1.0))
    t_scl_b = s.op("pool", lambda e: e.memset(scl2[96:128, :], 2.0))
    t_b2a = s.op("dve", lambda e: e.tensor_copy(bsum2[:], bsum[:]), after=(t_bsum,))
    t_b2b = s.op("dve", lambda e: e.tensor_scalar(bsum2[96:128, :], bsum[96:128, :],
                                                  2.0, None, op0=ALU.mult),
                 after=(t_b2a,))
    t_iov = s.dma("sp", iotav[:], IOTAV)
    t_iob = s.dma("sp", iotabc[:], IOTABC)
    t_tri = s.dma("sp", tripos[:], TRIPOS)
    t_o11 = s.dma("sp", ones11[:], ONES11)
    t_ms0 = s.op("pool", lambda e: e.memset(pw[:, 0:2], 0.0))
    t_ms1 = s.op("pool", lambda e: e.memset(cc[0][0][:], 0.0))
    t_ms2 = s.op("pool", lambda e: e.memset(cc[0][1][:], 0.0))
    t_ms1b = s.op("pool", lambda e: e.memset(cc[1][0][:], 0.0))
    t_ms2b = s.op("pool", lambda e: e.memset(cc[1][1][:], 0.0))
    vec_ms = [s.op("pool", lambda e, tl=tl: e.memset(tl[:], 0.0))
              for tl in [a_vec, b_vec, b2_vec, sg_vec]]
    t_msv = s.op("pool", lambda e: e.memset(mu_vec[:], 0.0))

    buf = {}
    pT_tok = None
    pA_tok = None
    xg_toks = []
    for sq in range(NSEQ):
        for ti in range(NT):
            tile = sq * NT + ti
            bname = f"xbuf{ti % 2}"
            tl = s.dma("pool", xbuf[ti % 2][:], X[tile * 128:(tile + 1) * 128, :],
                       after=(buf.get(bname), t_id), stream="xa")
            trtoks = []
            last_pe = None
            for c in range(8):
                src = xbuf[ti % 2][:, c * 128:(c + 1) * 128]
                dst = xT[:, c * 128:(c + 1) * 128]
                pe_t, cp_t = emit_transpose(s, dst, src, idb, pT,
                                            after=(tl, buf.get("xT")),
                                            pst_free=pT_tok)
                pT_tok = cp_t
                trtoks.append(cp_t)
                last_pe = pe_t
            buf[bname] = last_pe
            mm_last = None
            for c in range(8):
                aft = (pA_tok, t_wih) + tuple(trtoks) if c == 0 else ()
                mm_last = s.op("pe", lambda e, c=c: e.matmul(
                    pA[:, 0:128], wiht[:, c * 128:(c + 1) * 128],
                    xT[:, c * 128:(c + 1) * 128],
                    start=(c == 0), stop=(c == 7)), after=aft, inc=(c == 7))
            buf["xT"] = mm_last
            t_ev = s.op("act", lambda e, ti=ti: e.activation(
                xg_s[:, ti * 128:(ti + 1) * 128], pA[:, 0:128], AF.Identity,
                bias=bsum2[:], scale=scl2[:]),
                after=(mm_last, t_b2b, t_scl_a, t_scl_b, buf.get("xg_s")))
            pA_tok = t_ev
            xg_toks.append(t_ev)
        t_di = s.op("dve", lambda e, sq=sq: e.tensor_copy(
            xg_int[:, sq::2], xg_s[:, 0:S]), after=tuple(xg_toks[-NT:]))
        buf["xg_s"] = t_di
        buf["xg_int_%d" % sq] = t_di
        if sq == 0:
            pass

    qstate = {"pA_tok": None}
    bg = None
    if quant_args is not None:
        W1T_, W2T_, W1QH_, W2QH_ = quant_args
        wldq = s.sb("wldq", [128, 1024], F32)
        qoutq = [s.sb(f"qoutq{i}", [128, 1024], BF16) for i in range(2)]
        m2bq = s.sb("m2bq", [128, 1024], BF16)
        sumsq = s.sb("sumsq", [128, 32], F32)
        totq = s.sb("totq", [128, 1], F32)
        thrq = s.sb("thrq", [128, 1], F32)
        nthrq = s.sb("nthrq", [128, 1], F32)
        t11q = s.sb("t11q", [1, 1], F32)
        onescolq = s.sb("onescolq", [128, 1], F32)
        onesrowq = s.sb("onesrowq", [1, 128], F32)
        t_oc = s.op("pool", lambda e: e.memset(onescolq[:], 1.0))
        t_or = s.op("pool", lambda e: e.memset(onesrowq[:], 1.0))

        def quant_gen(pA_war):
            t_prev = None
            for (WT, WQH, nchunk, npc) in [(W1T_, W1QH_, 8, 4),
                                           (W2T_, W2QH_, 32, 1)]:
                rtoks = []
                for c in range(nchunk):
                    for h in range(npc):
                        tl = s.dma("sp", wldq[:],
                                   WT[c * 128:(c + 1) * 128,
                                      h * 1024:(h + 1) * 1024],
                                   after=(t_prev,), stream="qw")
                        col = c * npc + h
                        tr = s.op("dve", lambda e, col=col: e.reduce_sum(
                            sumsq[:, col:col + 1], wldq[:], axis=AX,
                            apply_absolute_value=True), after=(tl,))
                        rtoks.append(tr)
                        t_prev = tr
                        yield
                t_f = s.op("dve", lambda e: e.reduce_sum(
                    totq[:], sumsq[:, 0:32], axis=AX), after=tuple(rtoks))
                t_t = s.op("pe", lambda e: e.matmul(pA[0:1, 0:1], totq[:],
                                                    onescolq[:], start=True,
                                                    stop=True),
                           after=(t_f, t_oc, pA_war))
                t_g = s.op("dve", lambda e: e.tensor_scalar(
                    t11q[:], pA[0:1, 0:1], 0.5 / (D * DFF), 0.5e-5,
                    op0=ALU.mult, op1=ALU.add), after=(t_t,))
                t_bc = s.op("pe", lambda e: e.matmul(pA[0:128, 1:2],
                                                     onesrowq[:], t11q[:],
                                                     start=True, stop=True),
                            after=(t_g, t_or))
                t_thr = s.op("dve", lambda e: e.tensor_copy(
                    thrq[:], pA[0:128, 1:2]), after=(t_bc,))
                t_nthr = s.op("dve", lambda e: e.tensor_scalar(
                    nthrq[:], thrq[:], -1.0, None, op0=ALU.mult),
                    after=(t_thr,))
                qstate["pA_tok"] = t_thr
                pA_war = t_thr
                spl = {}
                for c in range(nchunk):
                    for h in range(npc):
                        sl = (c * npc + h) % 2
                        tl = s.dma("sp", wldq[:],
                                   WT[c * 128:(c + 1) * 128,
                                      h * 1024:(h + 1) * 1024],
                                   after=(t_prev,), stream="qw")
                        t1 = s.op("dve", lambda e, sl=sl: e.tensor_scalar(
                            qoutq[sl][:], wldq[:], thrq[:], None,
                            op0=ALU.is_gt), after=(tl, t_nthr, spl.get(sl)))
                        t2 = s.op("dve", lambda e: e.tensor_scalar(
                            m2bq[:], wldq[:], nthrq[:], None, op0=ALU.is_lt))
                        t3 = s.op("dve", lambda e, sl=sl: e.tensor_tensor(
                            qoutq[sl][:], qoutq[sl][:], m2bq[:],
                            op=ALU.subtract))
                        ts_ = s.dma("sp", WQH[c * 128:(c + 1) * 128,
                                             h * 1024:(h + 1) * 1024],
                                    qoutq[sl][:], after=(t3,), stream="qs")
                        spl[sl] = ts_
                        t_prev = t3
                        yield
        bg = quant_gen(pA_tok)

    t_h = [None, None]
    init_toks = (t_ms0, t_ms1, t_ms2, t_ms1b, t_ms2b, t_whh,
                 buf["xg_int_0"], buf["xg_int_1"])
    for t in range(S):
        t_mm = [None, None]
        t_s3 = [None, None]
        t_c = [None, None]
        t_tc = [None, None]
        for q in range(2):
            col = 2 * t + q
            gq = g_ps[q][t % 2]
            aft_mm = (t_h[q],) if t_h[q] else init_toks
            t_mm[q] = s.op("pe", lambda e, col=col, gq=gq: e.matmul(
                gq[:], whht[:], pw[0:20, col:col + 1],
                start=True, stop=True), after=aft_mm)
        for q in range(2):
            col = 2 * t + q
            gq = g_ps[q][t % 2]
            aft = (t_mm[q],) + ((t_scl_a, t_scl_b) if t == 0 else ())
            t_s3[q] = s.op("act", lambda e, col=col, gq=gq, sq_=sg_ps[q]:
                           e.activation(sq_[:], gq[:], AF.Sigmoid,
                                        scale=scl2[:],
                                        bias=xg_int[:, col:col + 1]),
                           after=aft)
        for q in range(2):
            s.op("dve", lambda e, q=q, sq_=sg_ps[q]: e.tensor_scalar(
                sgbuf[q][:], sq_[96:128, :], 2.0, -1.0,
                op0=ALU.mult, op1=ALU.add), after=(t_s3[q],))
            s.op("dve", lambda e, q=q, sq_=sg_ps[q]: e.tensor_tensor(
                v2t[q][:], sq_[0:32, :], sgbuf[q][:], op=ALU.mult))
            t_c[q] = s.op("dve", lambda e, t=t, q=q, sq_=sg_ps[q]:
                          e.scalar_tensor_tensor(
                              cc[q][(t + 1) % 2][:], cc[q][t % 2][:],
                              sq_[32:64, :], v2t[q][:],
                              op0=ALU.mult, op1=ALU.add))
        for q in range(2):
            t_tc[q] = s.op("act", lambda e, t=t, q=q: e.activation(
                tc[q][:], cc[q][(t + 1) % 2][:], AF.Tanh), after=(t_c[q],))
        for q in range(2):
            col = 2 * t + q
            t_h[q] = s.op("dve", lambda e, col=col, q=q, sq_=sg_ps[q]:
                          e.tensor_tensor(pw[:, col + 2:col + 3], sq_[64:96, :],
                                          tc[q][:], op=ALU.mult),
                          after=(t_tc[q],))
        if bg is not None and t % 16 == 15:
            next(bg, None)
    if bg is not None:
        for _ in bg:
            pass
    if qstate["pA_tok"] is not None:
        pA_tok = qstate["pA_tok"]
    scan_done = t_h[0]
    scan_done2 = t_h[1]

    ms_toks = []
    for ch in range(8):
        mm = s.op("pe", lambda e, ch=ch: e.matmul(
            pA[:, 0:512], wms[:], pw[0:20, 2 + ch * 512: 2 + (ch + 1) * 512],
            start=True, stop=True),
            after=(scan_done, scan_done2, t_wms, pA_tok) if ch == 0
            else (ms_toks[-1],))
        evs = []
        for (base, fn) in [(0, AF.Relu), (32, AF.Relu), (64, AF.Relu),
                           (96, AF.Sigmoid)]:
            t_e = s.op("act", lambda e, ch=ch, base=base, fn=fn: e.activation(
                msr[base:base + 1, ch * 512:(ch + 1) * 512],
                pA[base:base + 1, 0:512], fn, bias=bms[base:base + 1, :]),
                after=(mm, t_bms, buf.get("msr")))
            evs.append(t_e)
        ms_toks.append(evs[-1])
        pA_tok = evs[-1]
        buf["msr"] = evs[-1]
    dtoks = []
    for (srcb, dst) in [(0, a_vec), (32, b_vec), (64, b2_vec), (96, sg_vec)]:
        for sq in range(2):
            t_d = s.op("dve", lambda e, srcb=srcb, dst=dst, sq=sq: e.tensor_copy(
                dst[32 * sq:32 * sq + 1, :], msr[srcb:srcb + 1, sq::2]),
                after=tuple(ms_toks) + tuple(vec_ms))
            dtoks.append(t_d)
    t_b1 = s.op("dve", lambda e: e.tensor_tensor(b2_vec[:], b2_vec[:], iotav[:],
                                                 op=ALU.mult),
                after=tuple(dtoks) + (t_iov,))
    t_b2 = s.op("dve", lambda e: e.tensor_tensor(b_vec[:], b_vec[:], b2_vec[:],
                                                 op=ALU.add), after=(t_b1,))
    t_b3 = s.op("dve", lambda e: e.tensor_scalar(b_vec[:], b_vec[:], 1.0 / S, None,
                                                 op0=ALU.mult), after=(t_b2,))
    A_cur, B_cur = a_vec, b_vec
    A_alt, B_alt = a2_vec, mu_vec
    t_prev = (t_b3, t_msv)
    d = 1
    while d < S:
        tA1 = s.op("dve", lambda e, A=A_cur, Aa=A_alt, d=d: e.tensor_copy(
            Aa[:, 0:d], A[:, 0:d]), after=t_prev)
        tA2 = s.op("dve", lambda e, A=A_cur, Aa=A_alt, d=d: e.tensor_tensor(
            Aa[:, d:S], A[:, d:S], A[:, 0:S - d], op=ALU.mult), after=t_prev)
        tB1 = s.op("dve", lambda e, B=B_cur, Ba=B_alt, d=d: e.tensor_copy(
            Ba[:, 0:d], B[:, 0:d]), after=t_prev)
        tB2 = s.op("dve", lambda e, A=A_cur, B=B_cur, d=d: e.tensor_tensor(
            sc_tmp[:, d:S], A[:, d:S], B[:, 0:S - d], op=ALU.mult),
            after=t_prev + (("dummy", 0),) if False else t_prev)
        tB3 = s.op("dve", lambda e, B=B_cur, Ba=B_alt, d=d: e.tensor_tensor(
            Ba[:, d:S], sc_tmp[:, d:S], B[:, d:S], op=ALU.add), after=(tB2,))
        t_prev = (tA1, tA2, tB1, tB3)
        A_cur, A_alt = A_alt, A_cur
        B_cur, B_alt = B_alt, B_cur
        d *= 2
    if B_cur is not mu_vec:
        t_mu = s.op("dve", lambda e, B=B_cur: e.tensor_copy(mu_vec[:], B[:]),
                    after=t_prev)
    else:
        t_mu = s.op("dve", lambda e: e.tensor_copy(sc_tmp[:, 0:8], mu_vec[:, 0:8]),
                    after=t_prev)

    ps1.close()
    ps2 = ExitStack()
    pCol = ps2.enter_context(nc.psum_tensor("pCol", [128, 8], F32))
    pCtx = [ps2.enter_context(nc.psum_tensor(f"pCtx{i}", [128, 512], F32))
            for i in range(2)]
    pT2b = ps2.enter_context(nc.psum_tensor("pT2b", [128, 128], BF16))
    def wsq_ap(b, c0, c1):
        return msr[:, b * S + c0: b * S + c1]
    wbf2 = [w_bf, s.sb("w_bfb", [128, S], BF16)]
    wt2 = [wt_bf, s.sb("wt_bfb", [128, 128], BF16)]

    t_out = None
    init_deps = (t_mu, pA_tok, scan_done)
    pCol_tok = None
    pCtx_tok = [None, None]
    ptok = [pT_tok, None]
    wt_war = [None, None]
    rj2 = [rjcol, s.sb("rjcolb", [128, 1], F32)]
    last_ctx_mm = None
    for sq in range(NSEQ):
        xbtoks = []
        for ti in range(NT):
            tile = sq * NT + ti
            t_x = s.dma("pool", xbf[:, ti * D:(ti + 1) * D],
                        X[tile * 128:(tile + 1) * 128, :],
                        after=(buf.get("xbf_ctx"),), stream="xa")
            xbtoks.append(t_x)
        for jt in range(NT):
            j0 = jt * 128
            shi = (jt + 1) * 128
            b = jt % 2
            rjc = rj2[b]
            t_rj = s.dma("sp", rjc[:], RJP1[j0:j0 + 128, :],
                         after=(buf.get(f"rjcol{b}"),), stream="rj")
            t_m1 = s.op("pe", lambda e, sq=sq, j0=j0: e.matmul(
                pCol[:, 0:1], mu_vec[32 * sq:32 * sq + 1, j0:j0 + 128],
                ones11[32 * sq:32 * sq + 1, :], start=True, stop=True),
                after=(t_mu, t_o11, pCol_tok) + init_deps)
            t_m2 = s.op("pe", lambda e, sq=sq, j0=j0: e.matmul(
                pCol[:, 1:2], sg_vec[32 * sq:32 * sq + 1, j0:j0 + 128],
                ones11[32 * sq:32 * sq + 1, :], start=True, stop=True), after=())
            t_nm = s.op("dve", lambda e: e.tensor_scalar(
                cols["negmu"][:], pCol[:, 0:1], -1.0, None, op0=ALU.mult),
                after=(t_m1, t_m2, buf.get("colswar")))
            t_sg = s.op("dve", lambda e: e.tensor_copy(cols["sig"][:], pCol[:, 1:2]),
                        after=(t_m2,))
            pCol_tok = t_sg
            t_s2 = s.op("dve", lambda e: e.tensor_tensor(
                cols["s2"][:], cols["sig"][:], cols["sig"][:], op=ALU.mult),
                after=(t_sg,))
            t_s3 = s.op("dve", lambda e: e.tensor_scalar(
                cols["s2"][:], cols["s2"][:], 2.0, None, op0=ALU.mult), after=(t_s2,))
            t_ra = s.op("dve", lambda e: e.reciprocal(cols["negA"][:], cols["s2"][:]),
                        after=(t_s3,))
            t_na = s.op("dve", lambda e: e.tensor_scalar(
                cols["negA"][:], cols["negA"][:], -1.0, None, op0=ALU.mult),
                after=(t_ra,))
            t_n2 = s.op("dve", lambda e: e.tensor_scalar(
                cols["negA2"][:], cols["negA"][:], 2.0, None, op0=ALU.mult),
                after=(t_na,))
            t_wsq = s.op("act", lambda e, shi=shi, b=b, rjc=rjc: e.activation(
                wsq_ap(b, 0, shi), iotabc[:, 0:shi], AF.Square,
                bias=cols["negmu"][:], scale=rjc[:]),
                after=(t_nm, t_rj, t_iob))
            t_mk = s.op("dve", lambda e, j0=j0, b=b: e.tensor_tensor(
                wsq_ap(b, j0, j0 + 128), wsq_ap(b, j0, j0 + 128), tripos[:],
                op=ALU.add), after=(t_wsq, t_tri))
            t_w = s.op("act", lambda e, shi=shi, b=b: e.activation(
                wbf2[b][:, 0:shi], wsq_ap(b, 0, shi), AF.Exp,
                scale=cols["negA"][:]),
                after=(t_mk, t_na, buf.get(f"w_bf{b}")))
            t_q = s.op("act", lambda e, shi=shi, b=b: e.activation(
                dum_bf[:, 0:shi], wsq_ap(b, 0, shi), AF.Exp,
                scale=cols["negA2"][:],
                accum_out=cols["ssq"][:]), after=(t_mk, t_n2, t_w))
            buf["colswar"] = t_q
            t_l = s.op("act", lambda e: e.activation(cols["rn"][:], cols["ssq"][:],
                                                     AF.Ln), after=(t_q,))
            t_x2 = s.op("act", lambda e: e.activation(cols["rn"][:], cols["rn"][:],
                                                      AF.Exp, scale=-0.5),
                        after=(t_l, buf.get("rn_war")))
            t_mn = s.op("dve", lambda e: e.tensor_scalar(
                cols["rn"][:], cols["rn"][:], 1e12, None, op0=ALU.min),
                after=(t_x2,))
            buf["rn_war"] = t_mn
            mm_last = [None, None]
            for ic in range(jt + 1):
                sl = ic % 2
                psl = pT if sl == 0 else pT2b
                aftT = (t_w, wt_war[sl])
                if ic == 0 and sq == 0 and jt == 0:
                    aftT = aftT + init_deps
                pe_t, cp_t = emit_transpose(s, wt2[sl][:],
                                            wbf2[b][:, ic * 128:(ic + 1) * 128],
                                            idb, psl, after=aftT,
                                            pst_free=ptok[sl])
                ptok[sl] = cp_t
                for hh in range(2):
                    mm_last[hh] = s.op("pe", lambda e, ic=ic, hh=hh, jt=jt,
                                       sl=sl: e.matmul(
                        pCtx[hh][:], wt2[sl][:],
                        xbf[:, ic * D + hh * 512: ic * D + (hh + 1) * 512],
                        start=(ic == 0), stop=(ic == jt)),
                        after=(cp_t, pCtx_tok[hh]) +
                              (tuple(xbtoks) if ic == 0 else ()),
                        inc=(ic == jt))
                wt_war[sl] = mm_last[1]
                last_ctx_mm = mm_last[1]
            buf[f"w_bf{b}"] = mm_last[1]
            ctx_evs = []
            for hh in range(2):
                t_cx = s.op("act", lambda e, hh=hh: e.activation(
                    ctx[:, hh * 512:(hh + 1) * 512], pCtx[hh][:], AF.Identity,
                    scale=cols["rn"][:]), after=(mm_last[hh], t_mn,
                                                 buf.get("ctx")))
                pCtx_tok[hh] = t_cx
                ctx_evs.append(t_cx)
            tile = sq * NT + jt
            rbuf = xres[jt % 2]
            t_xr = s.dma("sp", rbuf[:], X[tile * 128:(tile + 1) * 128, :],
                         after=(buf.get(f"xres{jt % 2}"),), stream="xb")
            t_rs = s.op("dve", lambda e, b=rbuf: e.tensor_tensor(
                b[:], ctx[:], b[:], op=ALU.add),
                after=(t_xr,) + tuple(ctx_evs))
            buf["ctx"] = t_rs
            t_ln = ln.emit(rbuf[:], rbuf[:], (t_rs,))
            t_out = s.dma("sp", Y0[tile * 128:(tile + 1) * 128, :], rbuf[:],
                          after=(t_ln,), stream="out")
            buf[f"xres{jt % 2}"] = t_out
            buf[f"rjcol{b}"] = t_wsq
        buf["xbf_ctx"] = last_ctx_mm
    pT_tok = ptok[0]
    s.barrier("sp", (t_out,))
    ps2.close()


def build_L0a():
    nc = bass.Bass("TRN2", target_bir_lowering=False)
    NTOK = TT * 128
    X = nc.dram_tensor("X", [NTOK, D], F32, kind="ExternalInput").ap()
    WIHT = nc.dram_tensor("WIHT", [D, 128], F32, kind="ExternalInput").ap()
    BIH = nc.dram_tensor("BIH", [128, 1], F32, kind="ExternalInput").ap()
    BHH = nc.dram_tensor("BHH", [128, 1], F32, kind="ExternalInput").ap()
    WHHT = nc.dram_tensor("WHHT", [20, 128], F32, kind="ExternalInput").ap()
    WMS = nc.dram_tensor("WMS", [20, 128], F32, kind="ExternalInput").ap()
    BMS = nc.dram_tensor("BMS", [128, 1], F32, kind="ExternalInput").ap()
    IOTABC = nc.dram_tensor("IOTABC", [128, S], F32, kind="ExternalInput").ap()
    IOTAV = nc.dram_tensor("IOTAV", [64, S], F32, kind="ExternalInput").ap()
    RJP1 = nc.dram_tensor("RJP1", [S, 1], F32, kind="ExternalInput").ap()
    TRIPOS = nc.dram_tensor("TRIPOS", [128, 128], F32, kind="ExternalInput").ap()
    ONES11 = nc.dram_tensor("ONES11", [64, 1], F32, kind="ExternalInput").ap()
    IDENT = nc.dram_tensor("IDENT", [128, 128], F32, kind="ExternalInput").ap()
    Y0 = nc.dram_tensor("Y0", [NTOK, D], F32, kind="ExternalOutput").ap()

    s = Sched(nc)
    emit_L0a(nc, s, X, WIHT, BIH, BHH, WHHT, WMS, BMS, IOTABC, IOTAV, RJP1, TRIPOS, ONES11, IDENT, Y0)
    s.emit()
    return nc


def prep_L0a(core_x, W_ih, W_hh, b_ih, b_hh, W_mu, b_mu, W_sig, b_sig):
    order = [0, 1, 3, 2]
    WIHT = np.zeros((D, 128), np.float32)
    BIH = np.zeros((128, 1), np.float32)
    BHH = np.zeros((128, 1), np.float32)
    WHHT = np.zeros((20, 128), np.float32)
    for slot, base in enumerate([0, 32, 64, 96]):
        li = order[slot]
        WIHT[:, base:base + 20] = W_ih[20 * li:20 * li + 20, :].T
        BIH[base:base + 20, 0] = b_ih[20 * li:20 * li + 20]
        BHH[base:base + 20, 0] = b_hh[20 * li:20 * li + 20]
        WHHT[:, base:base + 20] = W_hh[20 * li:20 * li + 20, :].T
    WMS = np.zeros((20, 128), np.float32)
    BMS = np.zeros((128, 1), np.float32)
    WMS[:, 0] = W_mu[0]; BMS[0, 0] = b_mu[0]
    WMS[:, 32] = W_mu[1]; BMS[32, 0] = b_mu[1]
    WMS[:, 64] = W_mu[2]; BMS[64, 0] = b_mu[2]
    WMS[:, 96] = W_sig[0]; BMS[96, 0] = b_sig[0]
    ii = np.arange(S, dtype=np.float32)
    return dict(
        X=np.ascontiguousarray(core_x.reshape(-1, D).astype(np.float32)),
        WIHT=WIHT, BIH=BIH, BHH=BHH, WHHT=WHHT, WMS=WMS, BMS=BMS,
        IOTABC=np.broadcast_to(ii, (128, S)).copy(),
        IOTAV=np.broadcast_to(ii + 1.0, (64, S)).copy(),
        RJP1=(1.0 / (ii + 1.0)).reshape(S, 1).copy(),
        TRIPOS=np.where(np.arange(128)[None, :] > np.arange(128)[:, None],
                        np.float32(1e30), np.float32(0.0)).astype(np.float32),
        ONES11=np.ones((64, 1), np.float32),
        IDENT=np.eye(128, dtype=np.float32),
    )



def build_all(debug=False):
    nc = bass.Bass("TRN2", target_bir_lowering=False)
    NTOK = TT * 128
    NE = 8
    d = {}
    X = nc.dram_tensor("X", [NTOK, D], F32, kind="ExternalInput").ap()
    WIHT = nc.dram_tensor("WIHT", [D, 128], F32, kind="ExternalInput").ap()
    BIH = nc.dram_tensor("BIH", [128, 1], F32, kind="ExternalInput").ap()
    BHH = nc.dram_tensor("BHH", [128, 1], F32, kind="ExternalInput").ap()
    WHHT = nc.dram_tensor("WHHT", [20, 128], F32, kind="ExternalInput").ap()
    WMS = nc.dram_tensor("WMS", [20, 128], F32, kind="ExternalInput").ap()
    BMS = nc.dram_tensor("BMS", [128, 1], F32, kind="ExternalInput").ap()
    IOTABC = nc.dram_tensor("IOTABC", [128, S], F32, kind="ExternalInput").ap()
    IOTAV = nc.dram_tensor("IOTAV", [64, S], F32, kind="ExternalInput").ap()
    RJP1 = nc.dram_tensor("RJP1", [S, 1], F32, kind="ExternalInput").ap()
    TRIPOS = nc.dram_tensor("TRIPOS", [128, 128], F32, kind="ExternalInput").ap()
    ONES11 = nc.dram_tensor("ONES11", [64, 1], F32, kind="ExternalInput").ap()
    IDENT = nc.dram_tensor("IDENT", [128, 128], F32, kind="ExternalInput").ap()
    GATEW = nc.dram_tensor("GATEW", [D, NE], F32, kind="ExternalInput").ap()
    GATEB = nc.dram_tensor("GATEB", [128, NE], F32, kind="ExternalInput").ap()
    EXPW = nc.dram_tensor("EXPW", [NE, D, D], F32, kind="ExternalInput").ap()
    EXPB = nc.dram_tensor("EXPB", [NE, D], F32, kind="ExternalInput").ap()
    IDX8 = nc.dram_tensor("IDX8", [128, NE], F32, kind="ExternalInput").ap()
    IDXB8 = nc.dram_tensor("IDXB8", [128, NE], F32, kind="ExternalInput").ap()
    WQKVT = nc.dram_tensor("WQKVT", [D, 3 * D], F32, kind="ExternalInput").ap()
    WOUTT = nc.dram_tensor("WOUTT", [D, D], F32, kind="ExternalInput").ap()
    ABD = nc.dram_tensor("ABD", [1, 3], F32, kind="ExternalInput").ap()
    NEGTRI = nc.dram_tensor("NEGTRI", [128, 128], F32, kind="ExternalInput").ap()
    NEGTRIL = nc.dram_tensor("NEGTRIL", [128, 128], F32, kind="ExternalInput").ap()
    ONESR = nc.dram_tensor("ONESR", [1, 128], F32, kind="ExternalInput").ap()
    SEL16 = nc.dram_tensor("SEL16", [16, 128], F32, kind="ExternalInput").ap()
    RALLH = nc.dram_tensor("RALLH", [16 * S], F32, kind="Internal").ap()
    W1T = nc.dram_tensor("W1T", [D, DFF], F32, kind="ExternalInput").ap()
    B1C = nc.dram_tensor("B1C", [DFF], F32, kind="ExternalInput").ap()
    W2T = nc.dram_tensor("W2T", [DFF, D], F32, kind="ExternalInput").ap()
    B2C = nc.dram_tensor("B2C", [D], F32, kind="ExternalInput").ap()
    ONESC = nc.dram_tensor("ONESC", [128, 1], F32, kind="ExternalInput").ap()
    OUT = nc.dram_tensor("OUT", [NTOK, D], F32, kind="ExternalOutput").ap()
    Y0i = nc.dram_tensor("Y0i", [NTOK, D], F32, kind="Internal").ap()
    Y1i = nc.dram_tensor("Y1i", [NTOK, D], F32, kind="Internal").ap()
    Y2i = nc.dram_tensor("Y2i", [NTOK, D], F32, kind="Internal").ap()
    W1QH = nc.dram_tensor("W1QH", [D, DFF], BF16, kind="Internal").ap()
    W2QH = nc.dram_tensor("W2QH", [DFF, D], BF16, kind="Internal").ap()

    s = Sched(nc)
    s.push_scope("a_")
    emit_L0a(nc, s, X, WIHT, BIH, BHH, WHHT, WMS, BMS, IOTABC, IOTAV, RJP1,
             TRIPOS, ONES11, IDENT, Y0i, quant_args=(W1T, W2T, W1QH, W2QH))
    s.pop_scope()
    s.push_scope("b_")
    emit_L0f(nc, s, Y0i, GATEW, GATEB, EXPW, EXPB, IDENT, IDX8, IDXB8, Y1i)
    s.pop_scope()
    s.push_scope("c_")
    emit_L1a(nc, s, Y1i, WQKVT, WOUTT, ABD, NEGTRI, NEGTRIL, IDENT, ONESR,
             SEL16, RALLH, Y2i)
    s.pop_scope()
    s.push_scope("d_")
    emit_L1f(nc, s, Y2i, W1T, B1C, W2T, B2C, IDENT, ONESC, ONESR, OUT,
             WQH=(W1QH, W2QH))
    s.pop_scope()
    s.emit()
    return nc


def prep_all(core_x, inputs):
    im = {}
    im.update(prep_L0a(core_x, inputs["L0_attn_W_ih"], inputs["L0_attn_W_hh"],
                       inputs["L0_attn_b_ih"], inputs["L0_attn_b_hh"],
                       inputs["L0_attn_W_mu"], inputs["L0_attn_b_mu"],
                       inputs["L0_attn_W_sig"], inputs["L0_attn_b_sig"]))
    d = prep_L0f(np.zeros((1, D), np.float32), inputs["L0_ffn_gate_W"],
                 inputs["L0_ffn_gate_b"], inputs["L0_ffn_expert_W"],
                 inputs["L0_ffn_expert_b"])
    d.pop("Y0"); d.pop("IDENT")
    im.update(d)
    d = prep_L1a(np.zeros((1, D), np.float32), inputs["L1_attn_Wqkv"],
                 inputs["L1_attn_Wout"], inputs["L1_attn_a"],
                 inputs["L1_attn_b"], inputs["L1_attn_dt"])
    d.pop("Y1"); d.pop("IDENT")
    im.update(d)
    d = prep_L1f(np.zeros((1, D), np.float32), inputs["L1_ffn_W1"],
                 inputs["L1_ffn_b1"], inputs["L1_ffn_W2"], inputs["L1_ffn_b2"])
    d.pop("Y2"); d.pop("IDENT")
    im.update(d)
    return im


def kernel_fused(**inputs):
    x = np.asarray(inputs["L0_attn_x"], dtype=np.float32)
    cores = list(range(NCORES))
    nc = build_all()
    ims = [prep_all(x[2 * c:2 * c + 2], inputs) for c in cores]
    r = run_bass_kernel_spmd(nc, ims, core_ids=cores)
    out = _gather(r, "OUT")
    return np.concatenate([o.reshape(NSEQ, S, D) for o in out], axis=0)
